# version 10
# speedup vs baseline: 1.3497x; 1.0626x over previous
import os

import numpy as np

import concourse.bass as bass
import concourse.mybir as mybir
from concourse.bass_utils import run_bass_kernel_spmd

F32 = mybir.dt.float32
BF16 = mybir.dt.bfloat16
I32 = mybir.dt.int32
AF = mybir.ActivationFunctionType
ALU = mybir.AluOpType
AXX = mybir.AxisListType.X

S = 4096
D = 1024
H = 16
DH = 64
KC = 4
DFF = 4096
T1 = 256
NST = int(os.environ.get("KN_NST", S // T1))
C = 128
NCH = T1 // C
T2 = 256
NT2 = int(os.environ.get("KN_NT2", S // T2))
KN_P15 = int(os.environ.get("KN_P15", 1))
KN_DN = int(os.environ.get("KN_DN", 1))
KN_STAGE = int(os.environ.get("KN_STAGE", 9))
KN_ST2 = int(os.environ.get("KN_ST2", 9))
KN_DN3 = int(os.environ.get("KN_DN3", 9))
KN_DN4 = int(os.environ.get("KN_DN4", 9))
KN_DN5 = int(os.environ.get("KN_DN5", 9))
KN_DN6 = int(os.environ.get("KN_DN6", 9))
KN_SOLVE = int(os.environ.get("KN_SOLVE", 7))
KN_DBG = int(os.environ.get("KN_DBG", 0))
NCT2 = T2 // 128
HB = 8

ENG = ("sync", "tensor", "vector", "scalar", "gpsimd")
CSEM = {"tensor": "st", "vector": "sv", "scalar": "ss", "gpsimd": "sg", "sync": "sy"}
NDL = {"sync": 8, "gpsimd": 4}
DSEM = {"sync": ["di%d" % i for i in range(8)], "gpsimd": ["do%d" % i for i in range(4)]}

DEBUG = False


class Sch:

    SERIAL = ("vector", "scalar", "gpsimd")

    def __init__(self):
        self.stream = {e: [] for e in ENG}
        self.cnt = {e: 0 for e in ENG}
        self.didx = {"sync": 0, "gpsimd": 0}
        self.dval = {}
        self.lastw = {e: {} for e in ENG}
        self.lastp = None
        self.dummy_fn = None

    def _waits(self, e, deps):
        ws = []
        flat = []
        for d in deps:
            if d is None:
                continue
            if isinstance(d, list):
                flat.extend(d)
            else:
                flat.append(d)
        for dtok in flat:
            if dtok is None:
                continue
            sem, val = dtok
            if sem == CSEM.get(e) and e not in self.SERIAL:
                continue
            if sem == CSEM.get(e) and e in self.SERIAL:
                continue
            if self.lastw[e].get(sem, -1) >= val:
                continue
            ws.append((sem, val))
            self.lastw[e][sem] = val
        if ws:
            self.stream[e].append(("waits", ws))

    def g(self, e, *fns, deps=(), mk=None):
        assert fns
        self._waits(e, deps)
        if e == "tensor":
            metas = mk if mk is not None else [None] * len(fns)
            out = []
            for fn, m in zip(fns, metas):
                if m is not None and self.lastp is not None and m[0] != self.lastp[0]:
                    out.append(self.dummy_fn)
                self.lastp = m
                out.append(fn)
            fns = tuple(out)
        if e in self.SERIAL:
            for fn in fns:
                self.stream[e].append(("selfwait", self.cnt[e]))
                self.stream[e].append(("opinc", fn))
                self.cnt[e] += 1
        else:
            for fn in fns[:-1]:
                self.stream[e].append(("op", fn))
            self.stream[e].append(("opinc", fns[-1]))
            self.cnt[e] += 1
        return (CSEM[e], self.cnt[e])

    def dma(self, e, *fns, deps=()):
        self._waits(e, deps)
        toks = {}
        for fn in fns:
            lane = DSEM[e][self.didx[e] % NDL[e]]
            self.didx[e] += 1
            prev = self.dval.get(lane, 0)
            if self.lastw[e].get(lane, -1) < prev and prev > 0:
                self.stream[e].append(("waits", [(lane, prev)]))
                self.lastw[e][lane] = prev
            self.dval[lane] = prev + 16
            self.stream[e].append(("dma", fn, lane))
            toks[lane] = prev + 16
        out = [(k, v) for k, v in toks.items()]
        return out if len(out) > 1 else out[0]

    def replay(self, e, eng, sems):
        own = sems.get(CSEM.get(e))
        for item in self.stream[e]:
            k = item[0]
            if k == "waits":
                for (sem, val) in item[1]:
                    eng.wait_ge(sems[sem], val)
            elif k == "selfwait":
                if item[1] > 0:
                    eng.wait_ge(own, item[1])
            elif k == "op":
                item[1](eng)
            elif k == "opinc":
                item[1](eng).then_inc(own, 1)
            else:
                item[1](eng).then_inc(sems[item[2]], 16)


def sl(i, n):
    return slice(i * n, (i + 1) * n)


def _maxtok(*toks):
    best = {}
    for t in toks:
        if t is None:
            continue
        best[t[0]] = max(best.get(t[0], -1), t[1])
    return [(k, v) for k, v in best.items()]


_CACHE = {}
OFFS = {}


def build_nc():
    if "nc" in _CACHE:
        return _CACHE["nc"]
    nc = bass.Bass("TRN2", target_bir_lowering=False, debug=False)

    def din(name, shape, dt=F32):
        return nc.dram_tensor(name, list(shape), dt, kind="ExternalInput").ap()

    def dout_(name, shape, dt=F32):
        return nc.dram_tensor(name, list(shape), dt, kind="ExternalOutput").ap()

    def dscr(name, shape, dt):
        return nc.dram_tensor(name, list(shape), dt, kind="Internal").ap()

    xT = din("xT", [D, S])
    wq_d = din("wq", [D, D]); wk_d = din("wk", [D, D]); wv_d = din("wv", [D, D])
    wo_d = din("wo", [D, D]); wrq_d = din("wrq", [D, D]); wrk_d = din("wrk", [D, D])
    wfl_d = din("wfl", [D, 64])
    bfl_d = din("bfl", [64, 1])
    cvq_d = din("convq", [128, 8 * KC]); cvk_d = din("convk", [128, 8 * KC])
    cvv_d = din("convv", [128, 8 * KC])
    gsc_d = din("gsc", [16, 1])
    qnw_d = din("qnw", [128, 1]); knw_d = din("knw", [128, 1])
    onw_d = din("onw", [128, 8])
    w1T_d = din("w1T", [D, DFF])
    w2T_d = din("w2T", [DFF, D])
    ident_d = din("ident", [128, 128])
    id64_d = din("id64", [128, 64])
    maskM_d = din("maskM", [128, 128])
    maskMT_d = din("maskMT", [128, 128])
    maskAT_d = din("maskAT", [128, 128])
    bexp_d = din("bexp", [16, 8 * 128])
    bexpT_d = din("bexpT", [8 * 128, 16])
    onesc_d = din("onesc", [128, 1])
    onesr_d = din("onesr", [1, 128])
    sel16_d = din("sel16", [16, 16 * 128])

    y1s = dscr("y1s", [D, S], F32)
    w2qs = dscr("w2qs", [DFF, D], BF16)
    wrqs = dscr("wrqs", [D, D], BF16)
    wrks = dscr("wrks", [D, D], BF16)
    wos = dscr("wos", [D, D], BF16)
    outT = dout_("outT", [D, S])
    if KN_DBG:
        dbg_d = dout_("dbgd", [128, 4 * 512])
        dbg_s = dout_("dbgs", [128, NST * 512], BF16)
    dbg = {}
    if DEBUG:
        for nm, shp in [("d_qt", [D, T1]), ("d_kt", [D, T1]), ("d_vc", [D, T1]),
                        ("d_ctm", [128, 128]), ("d_W", [128, HB * 128]),
                        ("d_O", [D, T1]), ("d_y1", [D, T1])]:
            dbg[nm] = dout_(nm, shp, F32)

    sch = Sch()
    import contextlib
    stack = contextlib.ExitStack()
    sb = {}

    def alo(name, shape, dt):
        sb[name] = stack.enter_context(nc.sbuf_tensor(name, list(shape), dt))
        return sb[name]

    with stack:
        NB16 = 73 * 1024
        NF32 = 15320
        AB16 = alo("AB16", [128, NB16], BF16)
        AF32 = alo("AF32", [128, NF32], F32)
        _o = {"b": 0, "f": 0}

        def cb16(shape, at=None, name=None):
            n = int(np.prod(shape[1:])) if len(shape) > 1 else 1
            off = _o["b"] if at is None else at
            ap = AB16.ap()[0:shape[0], off:off + n]
            if at is None:
                _o["b"] += n
            if len(shape) > 2:
                ap = ap.rearrange("p (a n) -> p a n", a=shape[1])
            import traceback
            ln = traceback.extract_stack()[-2].line
            nm = ln.split("=")[0].strip().split(",")[0] if ln else None
            if nm:
                OFFS[nm] = ("b", off, tuple(shape))
            return ap, off

        def cf32(shape, at=None):
            n = int(np.prod(shape[1:])) if len(shape) > 1 else 1
            off = _o["f"] if at is None else at
            ap = AF32.ap()[0:shape[0], off:off + n]
            if at is None:
                _o["f"] += n
            if len(shape) > 2:
                ap = ap.rearrange("p (a n) -> p a n", a=shape[1])
            import traceback
            ln = traceback.extract_stack()[-2].line
            nm = ln.split("=")[0].strip().split(",")[0] if ln else None
            if nm:
                OFFS[nm] = ("f", off, tuple(shape))
            return ap, off

        wqb, o_wqb = cb16([128, 8, 1024])
        wkb, _ = cb16([128, 8, 1024])
        wvb, _ = cb16([128, 8, 1024])
        wsb0, _ = cb16([128, 8, 1024])
        w1qb, _ = cb16([128, 8, DFF], at=o_wqb)
        wsb1, o_wsb1 = cb16([128, 8, 1024])
        n2b, _ = cb16([128, 32, T2], at=o_wsb1)
        wsb = [wsb0, wsb1]
        wflb, _ = cb16([128, 8, 64])
        xbb, _ = cb16([128, 8, T1])
        pre, _ = cb16([128, 8, KC - 1 + T1])
        qc, o_qc = cb16([128, 8, T1])
        n1b, _ = cb16([128, 8, T2], at=o_qc)
        kc_, o_kc = cb16([128, 8, T1])
        w2qc0, _ = cb16([128, 1024], at=o_kc)
        w2qc1, _ = cb16([128, 1024], at=o_kc + 1024)
        w2qc = [w2qc0, w2qc1]
        vc, _ = cb16([128, 8, T1])
        rq, _ = cb16([128, 8, T1])
        rk, _ = cb16([128, 8, T1])
        ofm, _ = cb16([128, 8, T1])
        ypb, _ = cb16([128, 8, T1])
        sq1, _ = cb16([128, 8, T1])
        m_m, _ = cb16([128, HB, 128])
        m_mt, _ = cb16([128, HB, 128])
        m_at, _ = cb16([128, HB, 128])
        gb_, _ = cb16([128, HB, 128])
        gtb_, _ = cb16([128, HB, 128])
        gbuf = [gb_, gb_]
        gtbuf = [gtb_, gtb_]
        wb_, _ = cb16([128, HB, 128])
        wbuf = [wb_, wb_]
        expb, _ = cb16([128, 128])
        gexp, _ = cb16([128, HB, 128])
        kcs, _ = cb16([128, HB, 64])
        ztb, _ = cb16([128, HB, 128])
        awvb, _ = cb16([128, HB, 128])
        pktb, _ = cb16([128, HB // 2, 64])
        stb, _ = cb16([128, H // 2, 64])
        sc16, _ = cb16([16, T1])
        identb, _ = cb16([128, 128])
        maskMb, _ = cb16([128, 128])
        maskMTb, _ = cb16([128, 128])
        maskATb, _ = cb16([128, 128])
        bexpb, _ = cb16([16, 8 * 128])
        bexpTb, _ = cb16([128, 8, 16])
        onescb, _ = cb16([128, 1])
        onesrb, _ = cb16([1, 128])
        lam16, _ = cb16([16, 1])
        halo, _ = cb16([128, 3 * 8, KC - 1])
        assert _o["b"] <= NB16, _o["b"]

        stage, o_st = cf32([128, 1, 1024])
        xf0, _ = cf32([128, 8, T1])
        xf1, _ = cf32([128, 8, T1])
        xf = [xf0, xf1]
        hf, _ = cf32([128, 32, T2], at=o_st)
        ypre, o_yp = cf32([128, 8, T1])
        y1f = ypre
        stAB = [stage.rearrange("p a n -> p (a n)"),
                cf32([128, 1024], at=1024)[0]]
        flf, o_fl = cf32([64, T1])
        lnab, _ = cf32([64, T1])
        tbl1, _ = cf32([128, T1])
        tbl2, _ = cf32([128, T1])
        y2f, _ = cf32([128, 8, T2], at=0)
        ctm, o_ctm = cf32([128, NCH, 128])
        s16a, _ = cf32([16, T1])
        s16b, _ = cf32([16, T1])
        scexp, _ = cf32([128, 4, T2], at=o_ctm)
        cscr, o_cs = cf32([128, T1])
        crows, _ = cf32([16, 128])
        tmax, _ = cf32([128, T2], at=o_cs + T2)
        argf, _ = cf32([128, 128])
        pvf, o_pvf = cf32([128, HB // 2, 64])
        stf, o_stf = cf32([128, H // 2, 64])
        gcf, o_gcf = cf32([128, HB, 1])
        p2sc, _ = cf32([128, 8], at=o_gcf)
        crs_scr, _ = cf32([128, 16])
        r1a, _ = cf32([1, T1])
        r1b, _ = cf32([1, T1])
        qs_g, o_qsg = cf32([128, 512])
        qs_f, o_qsf = cf32([128, 512])
        argf8, _ = cf32([128, 8, 128], at=o_qsg)
        wsum, _ = cf32([128, 12])
        sval, _ = cf32([128, 8])
        identf, _ = cf32([128, 128])
        id64f, _ = cf32([128, 64])
        sel16f, _ = cf32([16, 16 * 128])
        onescf, _ = cf32([128, 1])
        onesrf, _ = cf32([1, 128])
        qnwb, _ = cf32([128, 1])
        knwb, _ = cf32([128, 1])
        onwb, _ = cf32([128, 8])
        lamb, _ = cf32([128, 8])
        convb, _ = cf32([128, 24, KC])
        bflb, _ = cf32([64, 1])
        o_tail = _o["f"]
        hy_pv, _ = cf32([128, 256], at=o_pvf)
        hy_st, _ = cf32([128, 2, 256], at=o_stf)
        hy_qg, _ = cf32([128, 256], at=o_qsg + 256)
        hy_qf, _ = cf32([128, 256], at=o_qsf + 256)
        hy_tl, _ = cf32([128, 3, 256], at=o_tail)
        scrows, _ = cf32([16, 128], at=o_tail + 768)
        hyfc = [hy_pv, hy_st[:, 0, :], hy_st[:, 1, :], hy_qg, hy_qf,
                hy_tl[:, 0, :], hy_tl[:, 1, :], hy_tl[:, 2, :]]
        assert o_tail + 768 + 128 <= NF32, o_tail
        assert _o["f"] <= NF32, _o["f"]
        qs_i = stack.enter_context(nc.sbuf_tensor("qs_i", [128, 512], I32))

        pf = stack.enter_context(nc.psum_tensor("pf", [128, 6, 512], F32))
        pb = stack.enter_context(nc.psum_tensor("pb", [128, 1, 1024], BF16))
        pfd = stack.enter_context(nc.psum_tensor("pfd", [128, 512], F32))
        sch.dummy_fn = lambda e: e.matmul(pfd[:, 0:16], identb[:], identb[:, 0:16],
                                          start=True, stop=True)

        sems = {}
        for s_ in list(CSEM.values()) + DSEM["sync"] + DSEM["gpsimd"]:
            sems[s_] = stack.enter_context(nc.semaphore(s_))

        pe = {}

        t0 = sch.dma(
            "sync",
            lambda e: e.dma_start(out=identf[:], in_=ident_d),
            lambda e: e.dma_start(out=id64f[:], in_=id64_d),
            lambda e: e.dma_start(out=stage[:, 0, 0:128], in_=maskM_d),
            lambda e: e.dma_start(out=stage[:, 0, 128:256], in_=maskMT_d),
            lambda e: e.dma_start(out=stage[:, 0, 256:384], in_=maskAT_d),
            lambda e: e.dma_start(out=stage[:, 0, 384:512].rearrange("p (a n) -> p a n", a=8),
                                  in_=bexpT_d.rearrange("(a p) n -> p a n", p=128)),
            lambda e: e.dma_start(out=onescf[:], in_=onesc_d),
            lambda e: e.dma_start(out=onesrf[:], in_=onesr_d),
            lambda e: e.dma_start(out=sel16f[:], in_=sel16_d),
            lambda e: e.dma_start(out=qnwb[:], in_=qnw_d),
            lambda e: e.dma_start(out=knwb[:], in_=knw_d),
            lambda e: e.dma_start(out=onwb[:], in_=onw_d),
            lambda e: e.dma_start(out=bflb[:], in_=bfl_d),
            lambda e: e.dma_start(out=stage[0:16, 0, 512:513], in_=gsc_d),
            lambda e: e.dma_start(out=convb[:, 0:8, :].rearrange("p a k -> p (a k)"),
                                  in_=cvq_d),
            lambda e: e.dma_start(out=convb[:, 8:16, :].rearrange("p a k -> p (a k)"),
                                  in_=cvk_d),
            lambda e: e.dma_start(out=convb[:, 16:24, :].rearrange("p a k -> p (a k)"),
                                  in_=cvv_d),
        )
        t_cc = sch.g(
            "vector",
            lambda e: e.tensor_copy(identb[:], identf[:]),
            lambda e: e.tensor_copy(maskMb[:], stage[:, 0, 0:128]),
            lambda e: e.tensor_copy(maskMTb[:], stage[:, 0, 128:256]),
            lambda e: e.tensor_copy(maskATb[:], stage[:, 0, 256:384]),
            lambda e: e.tensor_copy(bexpTb[:].rearrange("p a n -> p (a n)"),
                                    stage[:, 0, 384:512]),
            lambda e: e.tensor_copy(onescb[:], onescf[:]),
            lambda e: e.tensor_copy(onesrb[:], onesrf[:]),
            deps=[t0],
        )
        pass
        t_abs = sch.g("scalar",
                      lambda e: e.activation(stage[0:16, 0, 513:514], stage[0:16, 0, 512:513], AF.Abs),
                      deps=[t0])
        t_l16 = sch.g("vector", lambda e: e.tensor_copy(lam16[:], stage[0:16, 0, 513:514]),
                      deps=[t_abs])
        t_bexp_d = sch.dma("sync",
                           lambda e: e.dma_start(out=stage[0:16, 0, 0:1024], in_=bexp_d),
                           deps=[t_cc, t_l16])
        t_bexp = sch.g("vector",
                       lambda e: e.tensor_copy(bexpb[:], stage[0:16, 0, 0:1024]),
                       deps=[t_bexp_d])
        t_lmm = sch.g("tensor",
                      *[(lambda e, i=i: e.matmul(pf[:, 0, sl(i, 1)], bexpb[0:16, sl(i, 128)],
                                                 lam16[:], start=True, stop=True))
                        for i in range(8)],
                      deps=[t_l16, t_bexp], mk=[(0, 16)] * 8)
        t_lam = sch.g("vector", lambda e: e.tensor_copy(lamb[:], pf[:, 0, 0:8]), deps=[t_lmm])
        pe[("pf", 0)] = t_lam

        def w_res(dram, dst, prev):
            tok = None
            for r in range(8):
                t_d = sch.dma("sync",
                              lambda e, dram=dram, r=r: e.dma_start(
                                  out=stage[:, 0, :],
                                  in_=dram.rearrange("(a p) n -> p a n", p=128)[:, r, :]),
                              deps=(prev if r == 0 else [tok]))
                tok = sch.g("vector",
                            lambda e, r=r, dst=dst: e.tensor_copy(dst[:, r, :],
                                                                  stage[:, 0, :]),
                            deps=[t_d])
            return tok

        t_wq = w_res(wq_d, wqb, [t_lam, t_cc, t_bexp])
        t_wk = w_res(wk_d, wkb, [t_wq])
        t_wv = w_res(wv_d, wvb, [t_wk])
        t_fld = sch.dma("sync",
                        lambda e: e.dma_start(
                            out=stage[:, 0, 0:512].rearrange("p (a n) -> p a n", a=8),
                            in_=wfl_d.rearrange("(a p) n -> p a n", p=128)),
                        deps=[t_wv])
        t_wfl = sch.g("vector",
                      *[(lambda e, i=i: e.tensor_copy(
                          wflb[:, i, :],
                          stage[:, 0, 0:512].rearrange("p (a n) -> p a n", a=8)[:, i, :]))
                        for i in range(8)],
                      deps=[t_fld])

        def w_scr(dram, scr, fold, prev):
            tok = None
            for r in range(8):
                t_d = sch.dma("sync",
                              lambda e, dram=dram, r=r: e.dma_start(
                                  out=stage[:, 0, :],
                                  in_=dram.rearrange("(a p) n -> p a n", p=128)[:, r, :]),
                              deps=(prev if r == 0 else [tok]))
                if fold:
                    tok = sch.g("vector",
                                lambda e, r=r: e.tensor_scalar(
                                    out=wsb[0][:, r, :], in0=stage[:, 0, :],
                                    scalar1=onwb[:, r:r + 1], scalar2=None,
                                    op0=ALU.mult),
                                deps=[t_d])
                else:
                    tok = sch.g("vector",
                                lambda e, r=r: e.tensor_copy(wsb[0][:, r, :],
                                                             stage[:, 0, :]),
                                deps=[t_d])
            return sch.dma("gpsimd",
                           lambda e, scr=scr: e.dma_start(
                               out=scr.rearrange("(a p) n -> p a n", p=128),
                               in_=wsb[0]),
                           deps=[tok])

        t_srq = w_scr(wrq_d, wrqs, False, [t_wfl])
        t_srk = w_scr(wrk_d, wrks, False, [t_srq])
        t_swo = w_scr(wo_d, wos, True, [t_srk])

        t_init = sch.g(
            "vector",
            lambda e: e.memset(stf[:].rearrange("p a n -> p (a n)"), 0.0),
            lambda e: e.memset(stb[:].rearrange("p a n -> p (a n)"), 0.0),
            lambda e: e.memset(halo[:].rearrange("p a c -> p (a c)"), 0.0),
            lambda e: e.memset(tbl1[:], 0.0),
            lambda e: e.memset(tbl2[:], 0.0),
        )
        x_tok = [None, None]
        x_tok[0] = sch.dma("sync",
                           lambda e: e.dma_start(
                               out=xf[0],
                               in_=xT.rearrange("(a p) s -> p a s", p=128)[:, :, 0:T1]),
                           deps=[t_swo])
        ws_tok = {}

        def load_ws(which, buf, deps):
            scr = {"rq": wrqs, "rk": wrks, "wo": wos}[which]
            return sch.dma("sync",
                           lambda e, scr=scr, buf=buf: e.dma_start(
                               out=wsb[buf],
                               in_=scr.rearrange("(a p) n -> p a n", p=128)),
                           deps=deps)

        ws_tok[(0, "rq")] = load_ws("rq", 0, [t_swo])
        ws_tok[(0, "rk")] = load_ws("rk", 1, [ws_tok[(0, "rq")]])
        wsbuf_of = lambda st, k: (3 * st + k) % 2
        y1_toks = []

        for st in range(NST):
            cur = st % 2
            tx = x_tok[cur]
            if KN_DBG:
                t_dbgs = sch.dma("gpsimd",
                                 lambda e, st=st: e.dma_start(
                                     out=dbg_s[:, sl(st, 512)],
                                     in_=stb[:].rearrange("p a n -> p (a n)")),
                                 deps=[pe.get("stb_r"), pe.get("dbgs"), t_init])
                pe["dbgs"] = t_dbgs
                pe["stb_r2"] = t_dbgs
            if st + 1 < NST:
                x_tok[(st + 1) % 2] = sch.dma(
                    "sync",
                    lambda e, st=st: e.dma_start(
                        out=xf[(st + 1) % 2],
                        in_=xT.rearrange("(a p) s -> p a s", p=128)[:, :, sl(st + 1, T1)]),
                    deps=[pe.get(("xf", (st + 1) % 2))])

            t_xb = sch.g("vector",
                         *[(lambda e, i=i, cur=cur: e.tensor_copy(xbb[:, i, :],
                                                                  xf[cur][:, i, :]))
                           for i in range(8)],
                         deps=[tx, pe.get("xbb")])

            def proj(wsrc, dst, dst_off, wtok, freedep=None):
                toks = []
                for oc in range(8):
                    bank = oc % 4
                    mms = [(lambda e, oc=oc, kk=kk, bank=bank, wsrc=wsrc:
                            e.matmul(pf[:, bank, 0:T1], wsrc[:, kk, sl(oc, 128)],
                                     xbb[:, kk, :], start=(kk == 0), stop=(kk == 7)))
                           for kk in range(8)]
                    t_mm = sch.g("tensor", *mms,
                                 deps=[t_xb, wtok, pe.get(("pf", bank))])
                    t_ev = sch.g("scalar",
                                 lambda e, oc=oc, bank=bank, dst=dst, off=dst_off:
                                 e.copy(dst[:, oc, off:off + T1], pf[:, bank, 0:T1]),
                                 deps=[t_mm] + ([freedep] if freedep else []))
                    pe[("pf", bank)] = t_ev
                    toks.append(t_ev)
                return toks

            def halo_in(tix, deps):
                return sch.g("vector",
                             *[(lambda e, i=i, tix=tix: e.tensor_copy(
                                 pre[:, i, 0:KC - 1], halo[:, tix * 8 + i, :]))
                               for i in range(8)],
                             deps=deps + [t_init])

            def halo_out(tix, deps):
                return sch.g("vector",
                             *[(lambda e, i=i, tix=tix: e.tensor_copy(
                                 halo[:, tix * 8 + i, :], pre[:, i, T1:T1 + KC - 1]))
                               for i in range(8)],
                             deps=deps)

            def conv_emit(dst, widx, deps, freedep=None):
                toks = []
                for i in range(8):
                    ops = [lambda e, i=i, widx=widx: e.tensor_scalar(
                        out=cscr[:], in0=pre[:, i, 0:T1],
                        scalar1=convb[:, widx * 8 + i, 0:1], scalar2=None, op0=ALU.mult)]
                    for j in range(1, KC):
                        ops.append(lambda e, i=i, j=j, widx=widx: e.scalar_tensor_tensor(
                            out=cscr[:], in0=pre[:, i, j:j + T1],
                            scalar=convb[:, widx * 8 + i, j:j + 1], in1=cscr[:],
                            op0=ALU.mult, op1=ALU.add))
                    ops.append(lambda e, i=i, dst=dst: e.tensor_copy(dst[:, i, :], cscr[:]))
                    toks.append(sch.g("vector", *ops,
                                      deps=deps + ([freedep] if freedep else [])))
                return toks

            t_hq = halo_in(0, [pe.get("pre")])
            tq = proj(wqb, pre, KC - 1, t_wq, pe.get("pre"))
            t_hqs = halo_out(0, tq)
            tcq = conv_emit(qc, 0, tq + [t_hq], pe.get("qcF"))
            t_hk = halo_in(1, [tcq[-1], t_hqs])
            tk = proj(wkb, pre, KC - 1, t_wk, tcq[-1])
            t_hks = halo_out(1, tk)
            tck = conv_emit(kc_, 1, tk + [t_hk], pe.get("kcF"))
            t_hv = halo_in(2, [tck[-1], t_hks])
            tv = proj(wvb, pre, KC - 1, t_wv, tck[-1])
            t_hvs = halo_out(2, tv)
            tcv = conv_emit(vc, 2, tv + [t_hv], pe.get("vcF"))
            pe["pre"] = tcv[-1]

            trq = proj(wsb[wsbuf_of(st, 0)], rq, 0, ws_tok[(st, "rq")], pe.get("rqF"))
            trk = proj(wsb[wsbuf_of(st, 1)], rk, 0, ws_tok[(st, "rk")], pe.get("rkF"))
            ws_tok[(st, "wo")] = load_ws("wo", wsbuf_of(st, 2), trq)
            if st + 1 < NST:
                ws_tok[(st + 1, "rq")] = load_ws("rq", wsbuf_of(st + 1, 0), trk)

            t_flmm = sch.g("tensor",
                           *[(lambda e, kk=kk: e.matmul(pf[0:64, 4, 0:T1], wflb[:, kk, :],
                                                        xbb[:, kk, :],
                                                        start=(kk == 0), stop=(kk == 7)))
                             for kk in range(8)],
                           deps=[t_xb, t_wfl, pe.get(("pf", 4))])
            t_fl = sch.g("vector",
                         lambda e: e.tensor_scalar(out=flf[:], in0=pf[0:64, 4, 0:T1],
                                                   scalar1=bflb[:], scalar2=None, op0=ALU.add),
                         deps=[t_flmm, pe.get("flf")])
            pe[("pf", 4)] = t_fl

            def headsums(src, deps):
                t_sq = sch.g("scalar",
                             *[(lambda e, i=i, src=src: e.activation(sq1[:, i, :],
                                                                     src[:, i, :], AF.Square))
                               for i in range(8)],
                             deps=deps + [pe.get("sq1")])
                t_mm = sch.g("tensor",
                             *[(lambda e, i=i: e.matmul(pf[0:16, 4, 0:T1], bexpTb[:, i, :],
                                                        sq1[:, i, :],
                                                        start=(i == 0), stop=(i == 7)))
                               for i in range(8)],
                             deps=[t_sq, pe.get(("pf", 4))])
                pe["sq1"] = t_mm
                return t_mm

            def expand_apply(apply_fn_list, deps):
                toks = []
                for i in range(8):
                    t_e = sch.g("tensor",
                                lambda e, i=i: e.matmul(pf[:, 5, 0:T1],
                                                        bexpb[0:16, sl(i, 128)], sc16[:],
                                                        start=True, stop=True),
                                deps=deps + [pe.get(("pf", 5))], mk=[(0, 16)])
                    t_a = apply_fn_list[i](t_e)
                    pe[("pf", 5)] = t_a
                    toks.append(t_a)
                return toks

            t_qs = headsums(qc, tcq)
            t_qsc = sch.g("vector",
                          lambda e: e.tensor_scalar(out=s16a[:], in0=pf[0:16, 4, 0:T1],
                                                    scalar1=1.0 / DH, scalar2=1e-6,
                                                    op0=ALU.mult, op1=ALU.add),
                          deps=[t_qs])
            pe[("pf", 4)] = t_qsc
            t_q2 = sch.g("scalar",
                         lambda e: e.activation(s16b[:], s16a[:], AF.Ln),
                         lambda e: e.activation(s16a[:], s16b[:], AF.Exp, scale=-0.5),
                         deps=[t_qsc])
            t_qb = sch.g("vector", lambda e: e.tensor_copy(sc16[:], s16a[:]),
                         deps=[t_q2, pe.get("sc16")])

            def mkqfn(i):
                def f(t_e):
                    return sch.g("vector",
                                 (lambda e, i=i: e.scalar_tensor_tensor(
                                     out=qc[:, i, :], in0=qc[:, i, :], scalar=qnwb[:],
                                     in1=pf[:, 5, 0:T1], op0=ALU.mult, op1=ALU.mult)),
                                 (lambda e, i=i: e.scalar_tensor_tensor(
                                     out=qc[:, i, :], in0=rq[:, i, :],
                                     scalar=lamb[:, i:i + 1], in1=qc[:, i, :],
                                     op0=ALU.mult, op1=ALU.add)),
                                 deps=[t_e] + tcq + trq)
                return f

            tqt = expand_apply([mkqfn(i) for i in range(8)], [t_qb])
            pe["sc16"] = tqt[-1]

            t_ks = headsums(kc_, tck)
            t_ksc = sch.g("vector",
                          lambda e: e.tensor_scalar(out=s16a[:], in0=pf[0:16, 4, 0:T1],
                                                    scalar1=1.0 / DH, scalar2=1e-6,
                                                    op0=ALU.mult, op1=ALU.add),
                          deps=[t_ks])
            pe[("pf", 4)] = t_ksc
            t_k2 = sch.g("scalar",
                         lambda e: e.activation(s16b[:], s16a[:], AF.Ln),
                         lambda e: e.activation(s16a[:], s16b[:], AF.Exp, scale=-0.5),
                         deps=[t_ksc])
            t_kb = sch.g("vector", lambda e: e.tensor_copy(sc16[:], s16a[:]),
                         deps=[t_k2, pe.get("sc16")])

            def mkkfn(i):
                def f(t_e):
                    return sch.g("vector",
                                 (lambda e, i=i: e.scalar_tensor_tensor(
                                     out=kc_[:, i, :], in0=kc_[:, i, :], scalar=knwb[:],
                                     in1=pf[:, 5, 0:T1], op0=ALU.mult, op1=ALU.mult)),
                                 (lambda e, i=i: e.scalar_tensor_tensor(
                                     out=kc_[:, i, :], in0=rk[:, i, :],
                                     scalar=lamb[:, i:i + 1], in1=kc_[:, i, :],
                                     op0=ALU.mult, op1=ALU.add)),
                                 deps=[t_e] + tck + trk)
                return f

            tks_ = expand_apply([mkkfn(i) for i in range(8)], [t_kb])
            pe["sc16"] = tks_[-1]

            t_k2s = headsums(kc_, tks_)
            t_k2l = sch.g("scalar",
                          lambda e: e.activation(s16b[:], pf[0:16, 4, 0:T1], AF.Ln),
                          lambda e: e.activation(s16a[:], s16b[:], AF.Exp, scale=0.5),
                          deps=[t_k2s])
            pe[("pf", 4)] = t_k2l
            t_k2c = sch.g("vector",
                          lambda e: e.tensor_scalar(out=s16a[:], in0=s16a[:], scalar1=1e-12,
                                                    scalar2=None, op0=ALU.max),
                          lambda e: e.reciprocal(s16b[:], s16a[:]),
                          lambda e: e.tensor_copy(sc16[:], s16b[:]),
                          deps=[t_k2l, pe.get("sc16")])

            def mknfn(i):
                def f(t_e):
                    return sch.g("vector",
                                 lambda e, i=i: e.tensor_mul(kc_[:, i, :], kc_[:, i, :],
                                                             pf[:, 5, 0:T1]),
                                 deps=[t_e])
                return f

            tkt = expand_apply([mknfn(i) for i in range(8)], [t_k2c])
            pe["sc16"] = tkt[-1]

            if KN_STAGE < 2:
                continue
            t_t = sch.g("scalar",
                        lambda e: e.activation(lnab[:], flf[:], AF.Exp, scale=-1.0),
                        deps=[t_fl, pe.get("lnab")])
            pe["flf"] = t_t
            t_t2 = sch.g("vector",
                         lambda e: e.tensor_scalar(out=lnab[:], in0=lnab[:], scalar1=1.0,
                                                   scalar2=None, op0=ALU.add),
                         deps=[t_t])
            t_t3 = sch.g("scalar", lambda e: e.activation(lnab[:], lnab[:], AF.Ln),
                         deps=[t_t2])
            t_t4 = sch.g("vector",
                         lambda e: e.tensor_scalar(out=lnab[:], in0=lnab[:], scalar1=-1.0,
                                                   scalar2=None, op0=ALU.mult),
                         deps=[t_t3])
            if KN_ST2 < 1:
                continue
            ctm_toks = []
            for ch in range(NCH):
                cs = sl(ch, C)
                t_c = sch.g("vector",
                            (lambda e, cs=cs: e.tensor_tensor_scan(
                                out=tbl1[0:16, cs], data0=lnab[0:16, cs],
                                data1=lnab[0:16, cs], initial=0.0,
                                op0=ALU.add, op1=ALU.bypass)),
                            (lambda e, cs=cs: e.tensor_sub(tbl1[32:48, cs], tbl1[0:16, cs],
                                                           lnab[0:16, cs])),
                            (lambda e, cs=cs: e.tensor_add(tbl1[32:48, cs], tbl1[32:48, cs],
                                                           lnab[32:48, cs])),
                            deps=[t_t4, pe.get("tbl1")])
                if KN_ST2 < 2:
                    ctm_toks.append(t_c)
                    continue
                t_bx = sch.g("scalar",
                             (lambda e, cs=cs: e.activation(tbl1[64:80, cs], lnab[32:48, cs],
                                                            AF.Exp)),
                             (lambda e, cs=cs: e.activation(tbl1[96:112, cs],
                                                            tbl1[32:48, cs], AF.Exp)),
                             deps=[t_c])
                t_g1 = sch.g("vector",
                             lambda e, cs=cs, ch=ch: e.tensor_scalar(
                                 out=tbl2[0:16, cs], in0=tbl1[0:16, cs],
                                 scalar1=tbl1[0:16, ch * C + C - 1:ch * C + C],
                                 scalar2=None, op0=ALU.subtract),
                             deps=[t_c, pe.get("tbl2")])
                t_g2 = sch.g("scalar",
                             lambda e, cs=cs: e.activation(tbl2[0:16, cs], tbl2[0:16, cs],
                                                           AF.Exp, scale=-1.0),
                             deps=[t_g1])
                if KN_ST2 < 3:
                    ctm_toks.append(t_g2)
                    continue
                t_tr = sch.g("tensor",
                             lambda e, cs=cs: e.transpose(pf[:, 4, 0:128], tbl1[:, cs],
                                                          identf[:]),
                             deps=[t_bx, t_init, pe.get(("pf", 4))])
                t_ctm = sch.g("vector",
                              lambda e, ch=ch: e.tensor_copy(ctm[:, ch, :],
                                                             pf[:, 4, 0:128]),
                              deps=[t_tr, pe.get("ctm%d" % ch)])
                pe[("pf", 4)] = t_ctm
                t_tr2 = sch.g("tensor",
                              lambda e, cs=cs: e.transpose(pf[:, 4, 128:256], tbl2[:, cs],
                                                           identf[:]),
                              deps=[t_g2, t_ctm, pe.get(("pf", 4, "b"))])
                t_ctm2 = sch.g("vector",
                               lambda e, ch=ch: e.tensor_copy(ctm[:, ch, 112:128],
                                                              pf[:, 4, 128:144]),
                               deps=[t_tr2])
                pe[("pf", 4, "b")] = t_ctm2
                ctm_toks.append(t_ctm2)
            pe["tbl1"] = ctm_toks[-1]
            pe["tbl2"] = ctm_toks[-1]
            pe["lnab"] = ctm_toks[-1]

            if KN_STAGE < 3:
                continue
            ofm_toks = []
            for ch in range(NCH):
                cs = sl(ch, C)
                tctm = ctm_toks[ch]
                for bb in range(2):
                    heads = list(range(bb * HB, (bb + 1) * HB))
                    t_crc = sch.g(
                        "vector",
                        lambda e, ch=ch, bb=bb: e.tensor_copy(
                            crs_scr[:].rearrange("p (g r) -> p g r", g=2),
                            ctm[:, ch, :].rearrange("p (g r) -> p g r", g=4)
                            [:, 0:2, sl(bb, 8)]),
                        deps=[tctm, pe.get("crs_scr")])
                    t_cr = sch.g(
                        "tensor",
                        lambda e: e.transpose(pf[0:16, 4, 256:384], crs_scr[:],
                                              identf[:]),
                        deps=[t_crc, pe.get(("pf", 4, "cr"))])
                    pe["crs_scr"] = t_cr
                    t_crv = sch.g("vector",
                                  lambda e: e.tensor_copy(crows[:], pf[0:16, 4, 256:384]),
                                  deps=[t_cr, pe.get("crows")])
                    pe[("pf", 4, "cr")] = t_crv

                    if (KN_DBG and int(os.environ.get('KN_DBGSEL', 0)) == 3
                            and st == NST - 1 and bb == 1
                            and ch == int(os.environ.get('KN_DBGCH', NCH - 1))):
                        t_dbg3 = sch.g(
                            "vector",
                            (lambda e: e.tensor_copy(ypre[0:16, 0, 0:128], crows[:])),
                            (lambda e, ch=ch: e.tensor_copy(ypre[:, 1, 0:128],
                                                            ctm[:, ch, :])),
                            (lambda e, ch=ch: e.tensor_copy(ypre[:, 2, 0:128],
                                                            tbl1[:, sl(ch, C)])),
                            (lambda e, ch=ch: e.tensor_copy(ypre[0:64, 3, 0:128],
                                                            lnab[:, sl(ch, C)])),
                            deps=[t_crv, tctm, pe.get("ypre")])
                        t_dbg3d = sch.dma("gpsimd",
                                          lambda e: e.dma_start(
                                              out=dbg_d,
                                              in_=ypre[:, 0:8, :]
                                              .rearrange("p a n -> p (a n)")),
                                          deps=[t_dbg3])
                        pe["ypre"] = t_dbg3d
                    if KN_DN4 < 2:
                        ofm_toks.append(t_crv)
                        continue
                    sk_mms, sk_meta = [], []
                    for par in range(2):
                        for jj, h in enumerate(heads):
                            if h % 2 != par:
                                continue
                            ti = h // 2
                            rs = slice(par * 64, par * 64 + 64)
                            bank, qtr = jj // 4, jj % 4
                            sk_mms.append(lambda e, ti=ti, rs=rs, bank=bank, qtr=qtr, cs=cs:
                                          e.matmul(pf[:, bank, sl(qtr, 128)],
                                                   kc_[rs, ti, cs], kc_[rs, ti, cs],
                                                   start=True, stop=True,
                                                   tile_position=(rs.start, 0)))
                            sk_meta.append((par * 64, 64))
                            sk_mms.append(lambda e, ti=ti, rs=rs, bank=bank, qtr=qtr, cs=cs:
                                          e.matmul(pf[:, 2 + bank, sl(qtr, 128)],
                                                   kc_[rs, ti, cs], qc[rs, ti, cs],
                                                   start=True, stop=True,
                                                   tile_position=(rs.start, 0)))
                            sk_meta.append((par * 64, 64))
                    t_skk = sch.g("tensor", *sk_mms,
                                  deps=tqt + tkt + [pe.get(("pf", 0)), pe.get(("pf", 1)),
                                                    pe.get(("pf", 2)), pe.get(("pf", 3))],
                                  mk=sk_meta)
                    t_skq = t_skk

                    if KN_DN4 < 3:
                        ofm_toks.append(t_skk)
                        pe[("pf", 0)] = t_skk
                        pe[("pf", 2)] = t_skk
                        continue
                    tr_mms = []
                    for pi in range(4):
                        ti = bb * 4 + pi
                        tr_mms.append(lambda e, ti=ti, pi=pi, cs=cs: e.transpose(
                            pb[:, 0, sl(pi, 128)], kc_[:, ti, cs], identb[:]))
                        tr_mms.append(lambda e, ti=ti, pi=pi, cs=cs: e.transpose(
                            pb[:, 0, sl(4 + pi, 128)], vc[:, ti, cs], identb[:]))
                    t_tr = sch.g("tensor", *tr_mms,
                                 deps=tkt + tcv + [pe.get(("pb", 0)),
                                                   pe.get(("pb", 0, "s"))])

                    if KN_DN5 < 2:
                        ofm_toks.append(t_tr)
                        pe[("pb", 0)] = t_tr
                        pe[("pb", 0, "s")] = t_tr
                        continue
                    w0_ops, kcs_ops = [], []
                    for jj, h in enumerate(heads):
                        pi, par = jj // 2, jj % 2
                        koff = pi * 128 + par * 64
                        w0_ops.append(lambda e, jj=jj, koff=koff, ch=ch, h=h:
                                      e.tensor_scalar(
                                          out=wbuf[0][:, jj, 0:64],
                                          in0=pb[:, 0, 512 + koff:512 + koff + 64],
                                          scalar1=ctm[:, ch, 64 + h:65 + h],
                                          scalar2=None, op0=ALU.mult))
                        w0_ops.append(lambda e, jj=jj, koff=koff, ch=ch, h=h:
                                      e.tensor_scalar(
                                          out=wbuf[0][:, jj, 64:128],
                                          in0=pb[:, 0, koff:koff + 64],
                                          scalar1=ctm[:, ch, 96 + h:97 + h],
                                          scalar2=None, op0=ALU.mult))
                        kcs_ops.append(lambda e, jj=jj, koff=koff, ch=ch, h=h:
                                       e.tensor_scalar(
                                           out=kcs[:, jj, :],
                                           in0=pb[:, 0, koff:koff + 64],
                                           scalar1=ctm[:, ch, 112 + h:113 + h],
                                           scalar2=None, op0=ALU.mult))
                    t_w0 = sch.g("vector", *w0_ops, deps=[t_tr, pe.get("wbuf0")])
                    if KN_DN5 < 3:
                        t_kcs = t_w0
                    else:
                        t_kcs = sch.g("vector", *kcs_ops, deps=[t_tr, pe.get("kcs")])
                    pe[("pb", 0)] = t_w0
                    pe[("pb", 0, "s")] = t_kcs

                    if KN_DN4 < 4:
                        ofm_toks.append(t_kcs)
                        pe[("pf", 0)] = t_skk
                        pe[("pf", 2)] = t_skq
                        continue
                    def selq(jj):
                        return pf[:, 4 + jj // 4, sl(jj % 4, 128)]

                    t_selc = sch.g(
                        "tensor",
                        *[(lambda e, jj=jj: e.matmul(selq(jj),
                                                     sel16f[:, sl(2 * jj, 128)],
                                                     crows[:], start=True, stop=True))
                          for jj in range(HB)],
                        deps=[t_crv, pe.get(("pf", 4)), pe.get(("pf", 5)),
                              pe.get(("pf", 4, "s")), pe.get(("pf", 5, "s"))],
                        mk=[(0, 16)] * HB)
                    t_gx = sch.g(
                        "scalar",
                        (lambda e: e.activation(
                            gexp[:].rearrange("p a n -> p (a n)"),
                            pf[:, 4:6, :].rearrange("p a n -> p (a n)"), AF.Exp)),
                        (lambda e: e.activation(
                            gcf[:].rearrange("p a n -> p (a n)"),
                            pf[:, 4:6, :].rearrange("p a n -> p (a n)")
                            .rearrange("p (a n) -> p a n", a=8)[:, :, 127:128]
                            .rearrange("p a n -> p (a n)"), AF.Exp)),
                        deps=[t_selc, pe.get("gexp")])
                    if bb == 0:
                        t_negc = sch.g(
                            "vector",
                            lambda e, ch=ch: e.tensor_scalar(
                                out=argf8[:, 0, 0:16], in0=ctm[:, ch, 0:16],
                                scalar1=-1.0, scalar2=None, op0=ALU.mult),
                            deps=[tctm, pe.get("argf8")])
                        pe["negc"] = t_negc
                    t_am = sch.g(
                        "scalar",
                        *[(lambda e, jj=jj, ch=ch, h=bb * HB + jj: e.activation(
                            awvb[:, jj, :], selq(jj), AF.Exp,
                            bias=ctm[:, ch, 32 + h:33 + h], scale=-1.0))
                          for jj in range(HB)],
                        deps=[t_selc, t_gx, pe.get("awvb")])
                    t_msk1 = sch.g(
                        "gpsimd",
                        *[(lambda e, jj=jj: e.tensor_mul(awvb[:, jj, :],
                                                         awvb[:, jj, :], maskMb[:]))
                          for jj in range(HB)],
                        deps=[t_am, pe.get("mskf")])
                    t_mm1g = sch.g(
                        "vector",
                        *[(lambda e, jj=jj: e.tensor_mul(
                            m_m[:, jj, :], awvb[:, jj, :],
                            pf[:, jj // 4, sl(jj % 4, 128)]))
                          for jj in range(HB)],
                        deps=[t_msk1, t_skk, pe.get("m_m")])
                    t_aa = sch.g(
                        "scalar",
                        *[(lambda e, jj=jj, h=bb * HB + jj: e.activation(
                            gb_[:, jj, :], selq(jj), AF.Exp,
                            bias=argf8[:, 0, h:h + 1], scale=1.0))
                          for jj in range(HB)],
                        deps=[t_am, pe.get("negc"), pe.get("gb_f")])
                    t_msk3 = sch.g(
                        "gpsimd",
                        *[(lambda e, jj=jj: e.tensor_mul(gb_[:, jj, :],
                                                         gb_[:, jj, :], maskATb[:]))
                          for jj in range(HB)],
                        deps=[t_aa])
                    t_mm3g = sch.g(
                        "vector",
                        *[(lambda e, jj=jj: e.tensor_mul(
                            m_at[:, jj, :], gb_[:, jj, :],
                            pf[:, 2 + jj // 4, sl(jj % 4, 128)]))
                          for jj in range(HB)],
                        deps=[t_msk3, t_skk, pe.get("m_at")])
                    t_sell = sch.g(
                        "tensor",
                        *[(lambda e, jj=jj: e.matmul(selq(jj),
                                                     sel16f[:, sl(2 * jj + 1, 128)],
                                                     crows[:], start=True, stop=True))
                          for jj in range(HB)],
                        deps=[t_crv, t_aa, t_gx], mk=[(0, 16)] * HB)
                    t_amt = sch.g(
                        "scalar",
                        *[(lambda e, jj=jj, h=bb * HB + jj: e.activation(
                            gtb_[:, jj, :], selq(jj), AF.Exp,
                            bias=argf8[:, 0, h:h + 1], scale=1.0))
                          for jj in range(HB)],
                        deps=[t_sell, pe.get("negc"), pe.get("gtb_f")])
                    t_msk2 = sch.g(
                        "gpsimd",
                        *[(lambda e, jj=jj: e.tensor_mul(gtb_[:, jj, :],
                                                         gtb_[:, jj, :], maskMTb[:]))
                          for jj in range(HB)],
                        deps=[t_amt])
                    t_mm2g = sch.g(
                        "vector",
                        *[(lambda e, jj=jj: e.tensor_mul(
                            m_mt[:, jj, :], gtb_[:, jj, :],
                            pf[:, jj // 4, sl(jj % 4, 128)]))
                          for jj in range(HB)],
                        deps=[t_msk2, pe.get("m_mt")])
                    arg_last = t_mm2g
                    arg_last_s = t_amt
                    pe[("pf", 4)] = t_amt
                    pe[("pf", 5)] = t_amt
                    pe[("pf", 4, "s")] = t_amt
                    pe[("pf", 5, "s")] = t_amt
                    pe["argf8"] = t_amt
                    pe["awvb"] = t_mm1g
                    pe["mskf"] = t_mm2g
                    pe["gb_f"] = t_mm3g
                    pe["gtb_f"] = t_mm2g
                    pe["m_m"] = None
                    pe["m_mt"] = None
                    pe["m_at"] = None
                    pe["gexp"] = None
                    for b_ in range(4):
                        pe[("pf", b_)] = arg_last
                    pe["gexp"] = arg_last
                    pe["crows"] = arg_last

                    dbg_gate = None
                    if (KN_DBG and int(os.environ.get('KN_DBGSEL', 0)) in (1, 2)
                            and st == NST - 1 and bb == 1
                            and ch == int(os.environ.get('KN_DBGCH', NCH - 1))):
                        _dsel = int(os.environ.get('KN_DBGSEL', 0))
                        if _dsel == 1:
                            _src = [m_m, m_mt]
                        else:
                            _src = [m_at, wbuf[0]]
                        t_dbga = sch.g(
                            "vector",
                            (lambda e, _src=_src: e.tensor_copy(
                                ypre[:, 0:4, :].rearrange("p a n -> p (a n)"),
                                _src[0][:].rearrange("p a n -> p (a n)"))),
                            (lambda e, _src=_src: e.tensor_copy(
                                ypre[:, 4:8, :].rearrange("p a n -> p (a n)"),
                                _src[1][:].rearrange("p a n -> p (a n)"))),
                            deps=[arg_last, arg_last_s, t_w0, pe.get("ypre")])
                        t_dbgad = sch.dma("gpsimd",
                                          lambda e: e.dma_start(
                                              out=dbg_d,
                                              in_=ypre[:, 0:8, :]
                                              .rearrange("p a n -> p (a n)")),
                                          deps=[t_dbga])
                        pe["ypre"] = t_dbgad
                        dbg_gate = t_dbga
                    if KN_DN3 < 2:
                        ofm_toks.append(arg_last)
                        continue
                    cw = 0
                    t_wcur = t_w0
                    g_cur = (m_m, m_mt)
                    t_gdep = [arg_last]
                    t_app_dep = [arg_last, t_kcs, arg_last_s] + ([dbg_gate] if dbg_gate else [])
                    t_gn_dep = [arg_last, arg_last_s]
                    t_gtn_dep = [arg_last, arg_last_s]
                    for ks in range(KN_SOLVE):
                        gsrc, gtsrc = g_cur
                        t_app = sch.g("tensor",
                                      *[(lambda e, jj=jj, gtsrc=gtsrc, cw=cw: e.matmul(
                                          pf[:, jj // 4, sl(jj % 4, 128)],
                                          gtsrc[:, jj, :],
                                          wbuf[cw][:, jj, :], start=True, stop=True))
                                        for jj in range(HB)],
                                      deps=[t_wcur] + t_gdep + t_app_dep)
                        t_app_dep = []
                        opa = ALU.subtract if ks == 0 else ALU.add
                        t_wn = sch.g("vector",
                                     lambda e, cw=cw, opa=opa: e.tensor_tensor(
                                         out=wbuf[1 - cw].rearrange("p a n -> p (a n)"),
                                         in0=wbuf[cw].rearrange("p a n -> p (a n)"),
                                         in1=pf[:, 0:2, :].rearrange("p a n -> p (a n)"),
                                         op=opa),
                                     deps=[t_app, pe.get("wbuf%d" % (1 - cw))])
                        pe["wbuf%d" % cw] = t_wn
                        cw = 1 - cw
                        t_wcur = t_wn
                        t_app_dep = [t_wn]
                        if ks < 6:
                            t_gn = sch.g("tensor",
                                         *[(lambda e, jj=jj, gsrc=gsrc, gtsrc=gtsrc:
                                            e.matmul(pf[:, 2 + jj // 4, sl(jj % 4, 128)],
                                                     gtsrc[:, jj, :],
                                                     gsrc[:, jj, :], start=True,
                                                     stop=True))
                                           for jj in range(HB)],
                                         deps=t_gdep + t_gn_dep)
                            t_gtn = sch.g("tensor",
                                          *[(lambda e, jj=jj, gsrc=gsrc, gtsrc=gtsrc:
                                             e.matmul(pf[:, 4 + jj // 4, sl(jj % 4, 128)],
                                                      gsrc[:, jj, :],
                                                      gtsrc[:, jj, :], start=True,
                                                      stop=True))
                                            for jj in range(HB)],
                                          deps=t_gtn_dep)
                            t_gne = sch.g("vector",
                                          lambda e: e.tensor_copy(
                                              gbuf[0].rearrange("p a n -> p (a n)"),
                                              pf[:, 2:4, :].rearrange("p a n -> p (a n)")),
                                          deps=[t_gn, t_gtn])
                            t_gtne = sch.g("scalar",
                                           lambda e: e.copy(
                                               gtbuf[0].rearrange("p a n -> p (a n)"),
                                               pf[:, 4:6, :].rearrange("p a n -> p (a n)")),
                                           deps=[t_gtn, t_gn])
                            g_cur = (gbuf[0], gtbuf[0])
                            t_gdep = [t_gne, t_gtne]
                            t_gn_dep = [t_gne]
                            t_gtn_dep = [t_gtne]
                    t_wfin = t_wcur
                    if KN_SOLVE == 0:
                        t_gne = t_w0
                        t_gtne = t_w0
                    pe[("pf", 2)] = t_gne
                    pe[("pf", 3)] = t_gne
                    pe[("pf", 4)] = t_gtne
                    pe[("pf", 5)] = t_gtne
                    pe[("pf", 0)] = t_wfin
                    pe[("pf", 1)] = t_wfin

                    if KN_DN3 < 3:
                        ofm_toks.append(t_wfin)
                        continue
                    aw_mms = []
                    for jj, h in enumerate(heads):
                        par = h % 2
                        aw_mms.append(lambda e, jj=jj, par=par, cw=cw: e.matmul(
                            pf[slice(par * 64, par * 64 + 64), 2, sl(jj // 2, 128)],
                            wbuf[cw][:, jj, 0:64], m_at[:, jj, :],
                            start=True, stop=True, tile_position=(0, par * 64)))
                        aw_mms.append(lambda e, jj=jj, par=par, cw=cw: e.matmul(
                            pf[slice(par * 64, par * 64 + 64), 3, sl(jj // 2, 128)],
                            wbuf[cw][:, jj, 64:128], m_at[:, jj, :],
                            start=True, stop=True, tile_position=(0, par * 64)))
                    t_aw = sch.g("tensor", *aw_mms,
                                 deps=[t_wfin, arg_last, pe.get(("pf", 2)),
                                       pe.get(("pf", 3))])
                    awv_ops = []
                    zt_ops = []
                    for jj, h in enumerate(heads):
                        par = h % 2
                        prow = slice(par * 64, par * 64 + 64)
                        ti = h // 2
                        awv_ops.append(lambda e, jj=jj, prow=prow: e.copy(
                            awvb[prow, jj, :], pf[prow, 2, sl(jj // 2, 128)]))
                        zt_ops.append(lambda e, jj=jj, prow=prow, ti=ti, cs=cs:
                                      e.tensor_mul(ztb[prow, jj, :], qc[prow, ti, cs],
                                                   gexp[prow, jj, :]))
                        zt_ops.append(lambda e, jj=jj, prow=prow: e.tensor_sub(
                            ztb[prow, jj, :], ztb[prow, jj, :],
                            pf[prow, 3, sl(jj // 2, 128)]))
                    t_awv = sch.g("scalar", *awv_ops, deps=[t_aw, pe.get("awvb")])
                    t_zt = sch.g("vector", *zt_ops,
                                 deps=[t_aw, pe.get("ztb")] + tqt)
                    pe[("pf", 2)] = t_awv
                    pe[("pf", 3)] = t_zt

                    pv_mms, pkt_mms = [], []
                    for jj, h in enumerate(heads):
                        par = h % 2
                        pv_mms.append(lambda e, jj=jj, par=par, cw=cw: e.matmul(
                            pf[slice(par * 64, par * 64 + 64), 4, sl(jj // 2, 64)],
                            kcs[:, jj, :], wbuf[cw][:, jj, 0:64],
                            start=True, stop=True, tile_position=(0, par * 64)))
                        pkt_mms.append(lambda e, jj=jj, par=par, cw=cw: e.matmul(
                            pf[slice(par * 64, par * 64 + 64), 5, sl(jj // 2, 64)],
                            wbuf[cw][:, jj, 64:128], kcs[:, jj, :],
                            start=True, stop=True, tile_position=(0, par * 64)))
                    t_pv = sch.g("tensor", *pv_mms,
                                 deps=[t_wfin, t_kcs, pe.get(("pf", 4))] + t_gdep)
                    t_pkt = sch.g("tensor", *pkt_mms,
                                  deps=[t_wfin, t_kcs, pe.get(("pf", 5))] + t_gdep)
                    pv_ops, pkt_ops = [], []
                    for jj, h in enumerate(heads):
                        par = h % 2
                        prow = slice(par * 64, par * 64 + 64)
                        pv_ops.append(lambda e, jj=jj, prow=prow: e.tensor_copy(
                            pvf[prow, jj // 2, :], pf[prow, 4, sl(jj // 2, 64)]))
                        pkt_ops.append(lambda e, jj=jj, prow=prow: e.scalar_tensor_tensor(
                            out=pktb[prow, jj // 2, :], in0=id64f[prow, :],
                            scalar=gcf[prow, jj, 0:1], in1=pf[prow, 5, sl(jj // 2, 64)],
                            op0=ALU.mult, op1=ALU.subtract))
                    t_pve = sch.g("vector", *pv_ops, deps=[t_pv, pe.get("pvf")])
                    t_pkte = sch.g("vector", *pkt_ops, deps=[t_pkt, pe.get("pktb")])
                    if (KN_DBG and int(os.environ.get('KN_DBGSEL', 0)) == 0
                            and st == NST - 1 and bb == 1
                            and ch == int(os.environ.get('KN_DBGCH', NCH - 1))):
                        t_dbgc = sch.g(
                            "vector",
                            (lambda e: e.tensor_copy(
                                ypre[:, 0:2, :].rearrange("p a n -> p (a n)")[:, 0:256],
                                pvf[:].rearrange("p a n -> p (a n)"))),
                            (lambda e: e.tensor_copy(
                                ypre[:, 0:2, :].rearrange("p a n -> p (a n)")[:, 256:512],
                                pktb[:].rearrange("p a n -> p (a n)"))),
                            (lambda e: e.tensor_copy(
                                ypre[:, 2:6, :].rearrange("p a n -> p (a n)")[:, 0:1024],
                                wbuf[0][:].rearrange("p a n -> p (a n)"))),
                            (lambda e: e.tensor_copy(
                                ypre[:, 6:8, :].rearrange("p a n -> p (a n)")[:, 0:512],
                                kcs[:].rearrange("p a n -> p (a n)"))),
                            deps=[t_pve, t_pkte, t_wfin, t_kcs, pe.get("ypre")])
                        t_dbgcd = sch.dma("gpsimd",
                                          lambda e: e.dma_start(
                                              out=dbg_d,
                                              in_=ypre[:, 0:8, :]
                                              .rearrange("p a n -> p (a n)")),
                                          deps=[t_dbgc])
                        pe["ypre"] = t_dbgcd
                        pe["pvf"] = t_dbgc
                        pe["pktb"] = t_dbgc
                        pe["wbuf0"] = t_dbgc
                        pe["wbuf1"] = t_dbgc
                        pe["kcs"] = t_dbgc
                    pe[("pf", 4)] = t_pve
                    pe[("pf", 4, "cr")] = t_pve
                    pe[("pf", 4, "b")] = t_pve
                    pe[("pf", 5)] = t_pkte
                    pe[("pf", 4, "s")] = None
                    pe[("pf", 5, "s")] = None
                    pe["wbuf0"] = t_pkt
                    pe["wbuf1"] = t_pkt
                    pe["kcs"] = t_pkt
                    pe["m_m"] = t_aw
                    pe["m_mt"] = t_aw
                    pe["m_at"] = t_aw

                    if KN_DN3 < 4:
                        ofm_toks.append(t_pkte)
                        continue
                    y_mms, s_mms, ys_meta = [], [], []
                    for par in range(2):
                        for jj, h in enumerate(heads):
                            if h % 2 != par:
                                continue
                            prow = slice(par * 64, par * 64 + 64)
                            pi = h // 2
                            pj = jj // 2
                            y_mms.append(lambda e, prow=prow, pi=pi, jj=jj, pj=pj, par=par:
                                         e.matmul(pf[prow, 0, sl(pj, 128)],
                                                  stb[prow, pi, :], ztb[prow, jj, :],
                                                  start=True, stop=True,
                                                  tile_position=(par * 64, par * 64)))
                            s_mms.append(lambda e, prow=prow, pi=pi, jj=jj, pj=pj, par=par:
                                         e.matmul(pf[prow, 1, sl(pj, 64)],
                                                  pktb[prow, pj, :], stb[prow, pi, :],
                                                  start=True, stop=True,
                                                  tile_position=(par * 64, par * 64)))
                            ys_meta.append((par * 64, 64))
                    t_ymm = sch.g("tensor", *y_mms,
                                  deps=[t_zt, pe.get(("pf", 0)), pe.get("stb_r")],
                                  mk=ys_meta)
                    t_smm = sch.g("tensor", *s_mms,
                                  deps=[t_pkte, pe.get(("pf", 1)), t_ymm], mk=ys_meta)
                    o_ops, st_ops, stb_ops = [], [], []
                    for jj, h in enumerate(heads):
                        par = h % 2
                        prow = slice(par * 64, par * 64 + 64)
                        pi = h // 2
                        pj = jj // 2
                        o_ops.append(lambda e, prow=prow, pi=pi, jj=jj, pj=pj, cs=cs:
                                     e.tensor_add(ofm[prow, pi, cs],
                                                  pf[prow, 0, sl(pj, 128)],
                                                  awvb[prow, jj, :]))
                        st_ops.append(lambda e, prow=prow, pi=pi, pj=pj:
                                      e.tensor_add(stf[prow, pi, :],
                                                   pf[prow, 1, sl(pj, 64)],
                                                   pvf[prow, pj, :]))
                        stb_ops.append(lambda e, prow=prow, pi=pi:
                                       e.tensor_copy(stb[prow, pi, :], stf[prow, pi, :]))
                    t_oe = sch.g("vector", *o_ops, deps=[t_ymm, t_awv, pe.get("ofm")])
                    t_ste = sch.g("vector", *st_ops, deps=[t_smm, t_pve])
                    t_stb = sch.g("vector", *stb_ops, deps=[t_ste, pe.get("stb_r2")])
                    pe[("pf", 0)] = t_oe
                    pe[("pf", 1)] = t_ste
                    pe["stb_r"] = t_stb
                    pe["ztb"] = t_ymm
                    pe["pvf"] = t_ste
                    pe["pktb"] = t_smm
                    pe["awvb"] = t_oe
                    ofm_toks.append(t_oe)
                pe["ctm%d" % ch] = ofm_toks[-1]

            if KN_STAGE < 4:
                continue
            t_osq = sch.g("scalar",
                          *[(lambda e, i=i: e.activation(sq1[:, i, :], ofm[:, i, :],
                                                         AF.Square))
                            for i in range(8)],
                          deps=ofm_toks + [pe.get("sq1")])
            t_omm = sch.g("tensor",
                          *[(lambda e, i=i: e.matmul(pf[0:1, 4, 0:T1], onescb[:],
                                                     sq1[:, i, :],
                                                     start=(i == 0), stop=(i == 7)))
                            for i in range(8)],
                          deps=[t_osq, pe.get(("pf", 4))])
            pe["sq1"] = t_omm
            t_os = sch.g("vector",
                         lambda e: e.tensor_scalar(out=r1a[:], in0=pf[0:1, 4, 0:T1],
                                                   scalar1=1.0 / D, scalar2=1e-6,
                                                   op0=ALU.mult, op1=ALU.add),
                         deps=[t_omm, pe.get("r1a")])
            pe[("pf", 4)] = t_os
            t_ol = sch.g("scalar",
                         lambda e: e.activation(r1b[:], r1a[:], AF.Ln),
                         lambda e: e.activation(r1a[:], r1b[:], AF.Exp, scale=-0.5),
                         deps=[t_os, pe.get("r1b")])
            t_oexp = sch.g("tensor",
                           lambda e: e.matmul(pf[:, 5, 0:T1], onesrf[:], r1a[:],
                                              start=True, stop=True),
                           deps=[t_ol, pe.get(("pf", 5))], mk=[(0, 1)])
            t_on = sch.g("vector",
                         *[(lambda e, i=i: e.tensor_mul(ofm[:, i, :], ofm[:, i, :],
                                                        pf[:, 5, 0:T1]))
                           for i in range(8)],
                         deps=[t_oexp] + ofm_toks)
            pe[("pf", 5)] = t_on
            pe["r1a"] = t_oexp

            yp_toks = []
            for oc in range(8):
                bank = oc % 4
                mms = [(lambda e, oc=oc, kk=kk, bank=bank, st=st:
                        e.matmul(pf[:, bank, 0:T1],
                                 wsb[wsbuf_of(st, 2)][:, kk, sl(oc, 128)],
                                 ofm[:, kk, :], start=(kk == 0), stop=(kk == 7)))
                       for kk in range(8)]
                t_mm = sch.g("tensor", *mms,
                             deps=[t_on, ws_tok[(st, "wo")], pe.get(("pf", bank))])
                t_ev = sch.g("vector",
                             lambda e, oc=oc, bank=bank, cur=cur:
                             e.tensor_add(ypre[:, oc, :], pf[:, bank, 0:T1],
                                          xf[cur][:, oc, :]),
                             deps=[t_mm, tx, pe.get("ypre")])
                pe[("pf", bank)] = t_ev
                yp_toks.append(t_ev)
            pe[("xf", cur)] = yp_toks[-1]
            if st + 1 < NST:
                ws_tok[(st + 1, "rk")] = load_ws("rk", wsbuf_of(st + 1, 1), yp_toks)

            t_ypb = sch.g("scalar",
                          *[(lambda e, i=i: e.copy(ypb[:, i, :], ypre[:, i, :]))
                            for i in range(8)],
                          deps=yp_toks + [pe.get("ypb")])
            t_s1 = sch.g("tensor",
                         *[(lambda e, i=i: e.matmul(pf[0:1, 4, 0:T1], onescb[:],
                                                    ypb[:, i, :],
                                                    start=(i == 0), stop=(i == 7)))
                           for i in range(8)],
                         deps=[t_ypb, pe.get(("pf", 4))])
            t_mu = sch.g("vector",
                         lambda e: e.tensor_scalar(out=r1a[:], in0=pf[0:1, 4, 0:T1],
                                                   scalar1=1.0 / D, scalar2=None,
                                                   op0=ALU.mult),
                         deps=[t_s1, pe.get("r1a")])
            pe[("pf", 4)] = t_mu
            t_sq2 = sch.g("scalar",
                          *[(lambda e, i=i: e.activation(sq1[:, i, :], ypb[:, i, :],
                                                         AF.Square))
                            for i in range(8)],
                          deps=[t_ypb, pe.get("sq1")])
            t_s2 = sch.g("tensor",
                         *[(lambda e, i=i: e.matmul(pf[0:1, 4, 256:256 + T1], onescb[:],
                                                    sq1[:, i, :],
                                                    start=(i == 0), stop=(i == 7)))
                           for i in range(8)],
                         deps=[t_sq2, t_mu])
            pe["sq1"] = t_s2
            t_var = sch.g("vector",
                          (lambda e: e.tensor_scalar(out=r1b[:], in0=pf[0:1, 4, 256:256 + T1],
                                                     scalar1=1.0 / D, scalar2=None,
                                                     op0=ALU.mult)),
                          (lambda e: e.tensor_mul(pf[0:1, 4, 256:256 + T1], r1a[:], r1a[:])),
                          (lambda e: e.tensor_sub(r1b[:], r1b[:], pf[0:1, 4, 256:256 + T1])),
                          (lambda e: e.tensor_scalar(out=r1b[:], in0=r1b[:], scalar1=1e-5,
                                                     scalar2=None, op0=ALU.add)),
                          deps=[t_s2, pe.get("r1b")])
            pe[("pf", 4)] = t_var
            t_inv = sch.g("scalar",
                          lambda e: e.activation(r1b[:], r1b[:], AF.Ln),
                          lambda e: e.activation(r1b[:], r1b[:], AF.Exp, scale=-0.5),
                          deps=[t_var])
            t_mexp = sch.g("tensor",
                           lambda e: e.matmul(pf[:, 4, 0:T1], onesrf[:], r1a[:],
                                              start=True, stop=True),
                           deps=[t_inv, t_var], mk=[(0, 1)])
            t_iexp = sch.g("tensor",
                           lambda e: e.matmul(pf[:, 5, 0:T1], onesrf[:], r1b[:],
                                              start=True, stop=True),
                           deps=[t_inv, pe.get(("pf", 5))], mk=[(0, 1)])
            pe["r1a"] = t_mexp
            pe["r1b"] = t_iexp
            y1_ops = []
            for i in range(8):
                y1_ops.append(lambda e, i=i: e.tensor_sub(y1f[:, i, :], ypre[:, i, :],
                                                          pf[:, 4, 0:T1]))
                y1_ops.append(lambda e, i=i: e.tensor_mul(y1f[:, i, :], y1f[:, i, :],
                                                          pf[:, 5, 0:T1]))
            t_y1 = sch.g("vector", *y1_ops,
                         deps=[t_mexp, t_iexp] + yp_toks + [pe.get("y1f")])
            pe[("pf", 4)] = t_y1
            pe[("pf", 5)] = t_y1
            pe["ypre"] = t_y1
            pe["ypb"] = t_s2
            pe["ofm"] = t_mu
            pe["ofm"] = yp_toks[-1]
            t_y1d = sch.dma("gpsimd",
                            lambda e, st=st: e.dma_start(
                                out=y1s.rearrange("(a p) s -> p a s", p=128)[:, :, sl(st, T1)],
                                in_=y1f),
                            deps=[t_y1])
            pe["y1f"] = t_y1d
            pe["ypre"] = t_y1d
            y1_toks.append(t_y1d)
            pe["qcF"] = t_skq if False else None
            pe["qcF"] = t_y1
            pe["kcF"] = t_y1
            pe["vcF"] = t_y1
            pe["rqF"] = tqt[-1]
            pe["rkF"] = tks_[-1]
            if DEBUG and st == 0:
                sch.dma("gpsimd",
                        lambda e: e.dma_start(out=dbg["d_qt"].rearrange(
                            "(a p) n -> p (a n)", p=128),
                            in_=qc[:].rearrange("p a n -> p (a n)")),
                        lambda e: e.dma_start(out=dbg["d_kt"].rearrange(
                            "(a p) n -> p (a n)", p=128),
                            in_=kc_[:].rearrange("p a n -> p (a n)")),
                        lambda e: e.dma_start(out=dbg["d_vc"].rearrange(
                            "(a p) n -> p (a n)", p=128),
                            in_=vc[:].rearrange("p a n -> p (a n)")),
                        lambda e: e.dma_start(out=dbg["d_ctm"], in_=ctm[:, 0, :]),
                        lambda e: e.dma_start(out=dbg["d_O"].rearrange(
                            "(a p) n -> p (a n)", p=128),
                            in_=ofm[:].rearrange("p a n -> p (a n)")),
                        lambda e: e.dma_start(out=dbg["d_y1"].rearrange(
                            "(a p) n -> p (a n)", p=128),
                            in_=y1f[:].rearrange("p a n -> p (a n)")),
                        deps=[t_y1])

        t_q_last = None
        if not KN_P15:
            sch.dma("gpsimd",
                    lambda e: e.dma_start(
                        out=outT.rearrange("(a p) s -> p a s", p=128)[:, :, 0:T1],
                        in_=y1f),
                    deps=y1_toks[-1:] if y1_toks else [])
        t_wz = sch.g("vector", lambda e: e.memset(wsum[:], 0.0), deps=[])
        t_sum_last = t_wz
        red_tok = {}
        s_units = []
        for i in range(8):
            for hlf in range(4):
                s_units.append(("w1", i, hlf))
        for j in range(32):
            s_units.append(("w2", j, 0))
        for u, (kind, i, hlf) in enumerate(s_units if KN_P15 else []):
            buf = stAB[u % 2]
            if kind == "w1":
                src = w1T_d[sl(i, 128), sl(hlf, 1024)]
                tcol, acol = 10, i
            else:
                src = w2T_d[sl(i, 128), :]
                tcol, acol = 9, 8
            t_d = sch.dma(
                "sync",
                lambda e, buf=buf, src=src: e.dma_start(out=buf[:], in_=src),
                deps=[red_tok[u - 2]] if u >= 2 else
                ([t_swo] if u == 0 else [yp_toks[-1]]))
            t_r = sch.g("vector",
                        lambda e, buf=buf, tcol=tcol: e.tensor_reduce(
                            out=wsum[:, tcol:tcol + 1], in_=buf[:],
                            axis=AXX, op=ALU.add, apply_absolute_value=True),
                        deps=[t_d, t_wz])
            red_tok[u] = sch.g("vector",
                               lambda e, tcol=tcol, acol=acol: e.tensor_add(
                                   wsum[:, acol:acol + 1], wsum[:, acol:acol + 1],
                                   wsum[:, tcol:tcol + 1]),
                               deps=[t_r])
            t_sum_last = red_tok[u]
        t_tot = sch.g("tensor",
                      (lambda e: e.matmul(pf[0:1, 4, 0:12], onescf[:], wsum[:],
                                          start=True, stop=True)),
                      deps=[t_sum_last, pe.get(("pf", 4))])
        t_sc = sch.g("vector",
                     (lambda e: e.tensor_reduce(out=pf[0:1, 4, 16:17], in_=pf[0:1, 4, 0:8],
                                                axis=AXX, op=ALU.add)),
                     (lambda e: e.tensor_scalar(out=pf[0:1, 4, 17:18], in0=pf[0:1, 4, 16:17],
                                                scalar1=1.0 / (D * DFF), scalar2=1e-5,
                                                op0=ALU.mult, op1=ALU.max)),
                     (lambda e: e.tensor_scalar(out=pf[0:1, 4, 18:19], in0=pf[0:1, 4, 8:9],
                                                scalar1=1.0 / (D * DFF), scalar2=1e-5,
                                                op0=ALU.mult, op1=ALU.max)),
                     deps=[t_tot])
        t_scv = sch.g("vector",
                      lambda e: e.tensor_copy(sval[0:1, 6:8], pf[0:1, 4, 17:19]),
                      deps=[t_sc])
        t_bc = sch.g("tensor",
                     lambda e: e.matmul(pf[:, 5, 0:2], onesrf[:], sval[0:1, 6:8],
                                        start=True, stop=True),
                     deps=[t_scv, pe.get(("pf", 5))], mk=[(0, 1)])
        t_sv = sch.g("vector",
                     (lambda e: e.tensor_copy(sval[:, 4:6], pf[:, 5, 0:2])),
                     (lambda e: e.reciprocal(sval[:, 0:1], sval[:, 4:5])),
                     (lambda e: e.reciprocal(sval[:, 1:2], sval[:, 5:6])),
                     (lambda e: e.tensor_scalar(out=sval[:, 2:3], in0=sval[:, 4:5],
                                                scalar1=1.0 / 127.0, scalar2=None,
                                                op0=ALU.mult)),
                     (lambda e: e.tensor_scalar(out=sval[:, 3:4], in0=sval[:, 5:6],
                                                scalar1=1.0 / 127.0, scalar2=None,
                                                op0=ALU.mult)),
                     deps=[t_bc])
        pe[("pf", 4)] = t_sv
        pe[("pf", 5)] = t_sv

        def quant_chunk(dst_ap, src_ap, invcol, clip_lo, clip_hi, deps):
            return sch.g(
                "vector",
                (lambda e: e.tensor_scalar(out=qs_g[:, 0:src_ap.shape[-1]], in0=src_ap,
                                           scalar1=invcol, scalar2=0.5,
                                           op0=ALU.mult, op1=ALU.add)),
                (lambda e: e.tensor_copy(qs_i[:, 0:src_ap.shape[-1]],
                                         qs_g[:, 0:src_ap.shape[-1]])),
                (lambda e: e.tensor_copy(qs_f[:, 0:src_ap.shape[-1]],
                                         qs_i[:, 0:src_ap.shape[-1]])),
                (lambda e: e.tensor_tensor(out=qs_g[:, 0:src_ap.shape[-1]],
                                           in0=qs_f[:, 0:src_ap.shape[-1]],
                                           in1=qs_g[:, 0:src_ap.shape[-1]],
                                           op=ALU.is_gt)),
                (lambda e: e.tensor_sub(qs_f[:, 0:src_ap.shape[-1]],
                                        qs_f[:, 0:src_ap.shape[-1]],
                                        qs_g[:, 0:src_ap.shape[-1]])),
                (lambda e: e.tensor_scalar(out=dst_ap, in0=qs_f[:, 0:src_ap.shape[-1]],
                                           scalar1=float(clip_lo), scalar2=float(clip_hi),
                                           op0=ALU.max, op1=ALU.min)),
                deps=deps)

        t_q_last = t_sv
        qdone = {}
        NU1 = 32 if KN_P15 else 0
        for u in range(NU1):
            i, hlf = u // 4, u % 4
            dep0 = [qdone[u - 2]] if u >= 2 else \
                ([red_tok[62 + u]] if red_tok else [])
            t_d = sch.dma("sync",
                          lambda e, i=i, hlf=hlf, u=u: e.dma_start(
                              out=stAB[u % 2][:],
                              in_=w1T_d[sl(i, 128), sl(hlf, 1024)]),
                          deps=dep0)
            for s2_ in range(2):
                t_q_last = quant_chunk(
                    w1qb[:, i, hlf * 1024 + s2_ * 512:hlf * 1024 + s2_ * 512 + 512],
                    stAB[u % 2][:, sl(s2_, 512)],
                    sval[:, 0:1], -1, 1, [t_d, t_q_last])
            qdone[u] = t_q_last
        for j in range(32 if KN_P15 else 0):
            u = NU1 + j
            t_d = sch.dma("sync",
                          lambda e, j=j, u=u: e.dma_start(out=stAB[u % 2][:],
                                                          in_=w2T_d[sl(j, 128), :]),
                          deps=[qdone[u - 2]] if u >= 2 else [])
            for s2_ in range(2):
                t_q_last = quant_chunk(w2qc[j % 2][:, sl(s2_, 512)],
                                       stAB[u % 2][:, sl(s2_, 512)], sval[:, 1:2],
                                       -1, 1,
                                       [t_d, t_q_last]
                                       + ([pe.get(("w2st", j % 2))]
                                          if s2_ == 0 else []))
            qdone[u] = t_q_last
            t_wout = sch.dma("gpsimd",
                             lambda e, j=j: e.dma_start(out=w2qs[sl(j, 128), :],
                                                        in_=w2qc[j % 2][:, :]),
                             deps=[t_q_last])
            pe[("w2st", j % 2)] = t_wout

        y1r = y1s.rearrange("(a p) s -> p a s", p=128)

        def load_y(tt, deps):
            return sch.dma(
                "sync",
                (lambda e, tt=tt: e.dma_start(out=hy_pv,
                                              in_=y1r[:, 0, sl(tt, T2)])),
                (lambda e, tt=tt: e.dma_start(out=hy_st,
                                              in_=y1r[:, 1:3, sl(tt, T2)])),
                (lambda e, tt=tt: e.dma_start(out=hy_qg,
                                              in_=y1r[:, 3, sl(tt, T2)])),
                (lambda e, tt=tt: e.dma_start(out=hy_qf,
                                              in_=y1r[:, 4, sl(tt, T2)])),
                (lambda e, tt=tt: e.dma_start(out=hy_tl,
                                              in_=y1r[:, 5:8, sl(tt, T2)])),
                deps=deps)

        y_ld = None
        if NT2 > 0:
            y_ld = load_y(0, [t_q_last] + y1_toks[0:1])
        w2_tok = {}

        def w2l(i):
            return w2qc[i // 2][:, (i % 2) * 512:(i % 2) * 512 + 512]

        if NT2 > 0:
            w2_tok[0] = sch.dma("sync",
                                lambda e: e.dma_start(out=w2l(0),
                                                      in_=w2qs[0:128, 0:512]),
                                deps=[pe.get(("w2st", 0)), t_q_last])
            pe[("w2u", 1)] = pe.get(("w2st", 0))
            pe[("w2u", 2)] = pe.get(("w2st", 1))
            pe[("w2u", 3)] = pe.get(("w2st", 1))
        out_toks = []

        for tt in range(NT2):
            ty = y_ld

            t_ab = sch.g("scalar",
                         *[(lambda e, i=i: e.activation(
                             sq1[:, i, 0:T2], hyfc[i], AF.Abs))
                           for i in range(8)],
                         deps=[ty, pe.get("sq1")])
            mx_ops = [lambda e: e.tensor_max(tmax[:], sq1[:, 0, 0:T2], sq1[:, 1, 0:T2])]
            for i in range(2, 8):
                mx_ops.append(lambda e, i=i: e.tensor_max(tmax[:], tmax[:],
                                                          sq1[:, i, 0:T2]))
            t_mx = sch.g("vector", *mx_ops, deps=[t_ab, pe.get("tmax")])
            pe["sq1"] = t_mx
            t_mt = sch.g("tensor",
                         (lambda e: e.transpose(pf[:, 4, 0:128], tmax[:, 0:128],
                                                identf[:])),
                         (lambda e: e.transpose(pf[:, 4, 128:256], tmax[:, 128:256],
                                                identf[:])),
                         deps=[t_mx, pe.get(("pf", 4))])
            t_pm = sch.g("vector",
                         (lambda e: e.tensor_reduce(out=p2sc[:, 0:1],
                                                    in_=pf[:, 4, 0:128],
                                                    axis=AXX, op=ALU.max)),
                         (lambda e: e.tensor_reduce(out=p2sc[:, 1:2],
                                                    in_=pf[:, 4, 128:256],
                                                    axis=AXX, op=ALU.max)),
                         (lambda e: e.tensor_scalar(out=p2sc[:, 0:2], in0=p2sc[:, 0:2],
                                                    scalar1=1e-5, scalar2=None,
                                                    op0=ALU.max)),
                         (lambda e: e.reciprocal(p2sc[:, 2:4], p2sc[:, 0:2])),
                         (lambda e: e.tensor_scalar(out=p2sc[:, 2:4], in0=p2sc[:, 2:4],
                                                    scalar1=127.0, scalar2=None,
                                                    op0=ALU.mult)),
                         (lambda e: e.tensor_scalar(out=p2sc[:, 0:2], in0=p2sc[:, 0:2],
                                                    scalar1=sval[:, 2:3], scalar2=None,
                                                    op0=ALU.mult)),
                         deps=[t_mt, pe.get("p2sc")])
            pe[("pf", 4)] = t_pm
            t_rt = sch.g("tensor",
                         lambda e: e.transpose(pf[0:4, 4, 256:384], p2sc[:, 0:4],
                                               identf[:]),
                         deps=[t_pm, pe.get(("pf", 4, "b"))])
            t_rv = sch.g("vector",
                         lambda e: e.tensor_copy(scrows[0:4, :], pf[0:4, 4, 256:384]),
                         deps=[t_rt, pe.get("scrows")])
            pe[("pf", 4, "b")] = t_rv
            t_ex = sch.g("tensor",
                         (lambda e: e.matmul(pf[:, 5, 0:128], sel16f[:, sl(4, 128)],
                                             scrows[:], start=True, stop=True)),
                         (lambda e: e.matmul(pf[:, 5, 128:256], sel16f[:, sl(6, 128)],
                                             scrows[:], start=True, stop=True)),
                         (lambda e: e.matmul(pf[:, 5, 256:384], sel16f[:, sl(0, 128)],
                                             scrows[:], start=True, stop=True)),
                         (lambda e: e.matmul(pf[:, 5, 384:512], sel16f[:, sl(2, 128)],
                                             scrows[:], start=True, stop=True)),
                         deps=[t_rv, pe.get(("pf", 5))], mk=[(0, 16)] * 4)
            t_exe = sch.g("vector",
                          (lambda e: e.tensor_copy(scexp[:, 0, :], pf[:, 5, 0:256])),
                          (lambda e: e.tensor_copy(scexp[:, 1, :],
                                                   pf[:, 5, 256:512])),
                          deps=[t_ex, pe.get("scexp")])
            pe[("pf", 5)] = t_exe

            n1_toks = []
            for i in range(8):
                t_n = sch.g(
                    "vector",
                    (lambda e, i=i: e.scalar_tensor_tensor(
                        out=qs_g[:, 0:T2], in0=hyfc[i], scalar=0.5,
                        in1=scexp[:, 0, :], op0=ALU.bypass, op1=ALU.mult)),
                    (lambda e: e.tensor_scalar(out=qs_g[:, 0:T2], in0=qs_g[:, 0:T2],
                                               scalar1=0.5, scalar2=None, op0=ALU.add)),
                    (lambda e: e.tensor_copy(qs_i[:, 0:T2], qs_g[:, 0:T2])),
                    (lambda e: e.tensor_copy(qs_f[:, 0:T2], qs_i[:, 0:T2])),
                    (lambda e: e.tensor_tensor(out=qs_g[:, 0:T2], in0=qs_f[:, 0:T2],
                                               in1=qs_g[:, 0:T2], op=ALU.is_gt)),
                    (lambda e: e.tensor_sub(qs_f[:, 0:T2], qs_f[:, 0:T2],
                                            qs_g[:, 0:T2])),
                    (lambda e, i=i: e.tensor_scalar(out=n1b[:, i, :], in0=qs_f[:, 0:T2],
                                                    scalar1=-128.0, scalar2=127.0,
                                                    op0=ALU.max, op1=ALU.min)),
                    deps=[t_exe, ty, pe.get("n1b")])
                n1_toks.append(t_n)

            h_toks = []
            for oc in range(32):
                bank = oc % 4
                po = ((oc // 4) % 2) * 256
                mms = [(lambda e, oc=oc, kk=kk, bank=bank, po=po:
                        e.matmul(pf[:, bank, po:po + T2], w1qb[:, kk, sl(oc, 128)],
                                 n1b[:, kk, :], start=(kk == 0), stop=(kk == 7)))
                       for kk in range(8)]
                t_mm = sch.g("tensor", *mms,
                             deps=n1_toks + [t_q_last, pe.get(("pf", bank, po))])
                pe["n1b"] = t_mm
                t_dq = sch.g("vector",
                             lambda e, oc=oc, bank=bank, po=po: e.tensor_mul(
                                 hf[:, oc, :], pf[:, bank, po:po + T2],
                                 scexp[:, 1, :]),
                             deps=[t_mm, pe.get("hf")]
                             + ([pe.get("y2f")] if oc == 0 else [])
                             + ([y1_toks[-1]] if oc == 20 and y1_toks else []))
                t_g = sch.g("scalar",
                            lambda e, oc=oc: e.activation(hf[:, oc, :], hf[:, oc, :],
                                                          AF.Gelu),
                            deps=[t_dq])
                pe[("pf", bank, po)] = t_g
                h_toks.append(t_g)

            t_mx2 = None
            for grp in range(4):
                t_ab2 = sch.g("scalar",
                              *[(lambda e, i=i, grp=grp: e.activation(
                                  sq1[:, i, 0:T2], hf[:, grp * 8 + i, :], AF.Abs))
                                for i in range(8)],
                              deps=h_toks[grp * 8:grp * 8 + 8]
                              + ([t_mx2] if t_mx2 else [pe.get("sq1")]))
                ops = []
                if grp == 0:
                    ops.append(lambda e: e.tensor_max(tmax[:], sq1[:, 0, 0:T2],
                                                      sq1[:, 1, 0:T2]))
                    rng = range(2, 8)
                else:
                    rng = range(0, 8)
                for i in rng:
                    ops.append(lambda e, i=i: e.tensor_max(tmax[:], tmax[:],
                                                           sq1[:, i, 0:T2]))
                t_mx2 = sch.g("vector", *ops,
                              deps=[t_ab2] + ([t_exe] if grp == 0 else []))
            pe["sq1"] = t_mx2
            pe["tmax"] = t_mx2
            t_mt2 = sch.g("tensor",
                          (lambda e: e.transpose(pf[:, 4, 0:128], tmax[:, 0:128],
                                                 identf[:])),
                          (lambda e: e.transpose(pf[:, 4, 128:256], tmax[:, 128:256],
                                                 identf[:])),
                          deps=[t_mx2, pe.get(("pf", 4))])
            t_pm2 = sch.g("vector",
                          (lambda e: e.tensor_reduce(out=p2sc[:, 4:5],
                                                     in_=pf[:, 4, 0:128],
                                                     axis=AXX, op=ALU.max)),
                          (lambda e: e.tensor_reduce(out=p2sc[:, 5:6],
                                                     in_=pf[:, 4, 128:256],
                                                     axis=AXX, op=ALU.max)),
                          (lambda e: e.tensor_scalar(out=p2sc[:, 4:6], in0=p2sc[:, 4:6],
                                                     scalar1=1e-5, scalar2=None,
                                                     op0=ALU.max)),
                          (lambda e: e.reciprocal(p2sc[:, 6:8], p2sc[:, 4:6])),
                          (lambda e: e.tensor_scalar(out=p2sc[:, 6:8], in0=p2sc[:, 6:8],
                                                     scalar1=127.0, scalar2=None,
                                                     op0=ALU.mult)),
                          (lambda e: e.tensor_scalar(out=p2sc[:, 4:6], in0=p2sc[:, 4:6],
                                                     scalar1=sval[:, 3:4], scalar2=None,
                                                     op0=ALU.mult)),
                          deps=[t_mt2, t_rv])
            pe[("pf", 4)] = t_pm2
            t_rt2 = sch.g("tensor",
                          lambda e: e.transpose(pf[0:4, 4, 256:384], p2sc[:, 4:8],
                                                identf[:]),
                          deps=[t_pm2, pe.get(("pf", 4, "b"))])
            t_rv2 = sch.g("vector",
                          lambda e: e.tensor_copy(scrows[0:4, :], pf[0:4, 4, 256:384]),
                          deps=[t_rt2, t_exe, t_ex])
            pe[("pf", 4, "b")] = t_rv2
            t_ex2 = sch.g("tensor",
                          (lambda e: e.matmul(pf[:, 5, 0:128], sel16f[:, sl(4, 128)],
                                              scrows[:], start=True, stop=True)),
                          (lambda e: e.matmul(pf[:, 5, 128:256], sel16f[:, sl(6, 128)],
                                              scrows[:], start=True, stop=True)),
                          (lambda e: e.matmul(pf[:, 5, 256:384], sel16f[:, sl(0, 128)],
                                              scrows[:], start=True, stop=True)),
                          (lambda e: e.matmul(pf[:, 5, 384:512], sel16f[:, sl(2, 128)],
                                              scrows[:], start=True, stop=True)),
                          deps=[t_rv2, pe.get(("pf", 5))], mk=[(0, 16)] * 4)
            t_exe2 = sch.g("vector",
                           (lambda e: e.tensor_copy(scexp[:, 2, :], pf[:, 5, 0:256])),
                           (lambda e: e.tensor_copy(scexp[:, 3, :],
                                                    pf[:, 5, 256:512])),
                           deps=[t_ex2, pe.get("scexp2")])
            pe[("pf", 5)] = t_exe2

            n2_toks = []
            for oc in range(32):
                t_n = sch.g(
                    "vector",
                    (lambda e, oc=oc: e.scalar_tensor_tensor(
                        out=qs_g[:, 0:T2], in0=hf[:, oc, :], scalar=0.5,
                        in1=scexp[:, 2, :], op0=ALU.bypass, op1=ALU.mult)),
                    (lambda e: e.tensor_scalar(out=qs_g[:, 0:T2], in0=qs_g[:, 0:T2],
                                               scalar1=0.5, scalar2=None, op0=ALU.add)),
                    (lambda e: e.tensor_copy(qs_i[:, 0:T2], qs_g[:, 0:T2])),
                    (lambda e: e.tensor_copy(qs_f[:, 0:T2], qs_i[:, 0:T2])),
                    (lambda e: e.tensor_tensor(out=qs_g[:, 0:T2], in0=qs_f[:, 0:T2],
                                               in1=qs_g[:, 0:T2], op=ALU.is_gt)),
                    (lambda e: e.tensor_sub(qs_f[:, 0:T2], qs_f[:, 0:T2],
                                            qs_g[:, 0:T2])),
                    (lambda e, oc=oc: e.tensor_scalar(out=n2b[:, oc, :],
                                                      in0=qs_f[:, 0:T2],
                                                      scalar1=-128.0, scalar2=127.0,
                                                      op0=ALU.max, op1=ALU.min)),
                    deps=[t_exe2, h_toks[oc], pe.get("n2b")])
                n2_toks.append(t_n)
            pe["hf"] = n2_toks[-1]

            y2_toks = []
            for half in range(2):
                for kk in range(32):
                    gidx = tt * 64 + half * 32 + kk
                    wtok = w2_tok[gidx]
                    wb = w2l(gidx % 4)
                    nxt = gidx + 1
                    if nxt < NT2 * 64:
                        nh = (nxt % 64) // 32
                        nk = nxt % 32
                        w2_tok[nxt] = sch.dma(
                            "sync" if nxt % 2 == 0 else "gpsimd",
                            lambda e, nxt=nxt, nh=nh, nk=nk: e.dma_start(
                                out=w2l(nxt % 4),
                                in_=w2qs[sl(nk, 128), nh * 512:nh * 512 + 512]),
                            deps=[pe.get(("w2u", nxt % 4))])
                    mms = [(lambda e, kk=kk, oc=oc, wb=wb: e.matmul(
                        pf[:, oc, 0:T2], wb[:, sl(oc, 128)],
                        n2b[:, kk, :], start=(kk == 0), stop=(kk == 31)))
                        for oc in range(4)]
                    t_mm2 = sch.g("tensor", *mms,
                                  deps=[wtok, n2_toks[kk]]
                                  + ([pe.get(("pf", b_, 0)) for b_ in range(4)]
                                     if kk == 0 else []))
                    pe[("w2u", gidx % 4)] = t_mm2
                for oc in range(4):
                    t_e2v = sch.g("vector",
                                  (lambda e, oc=oc, half=half: e.tensor_mul(
                                      y2f[:, half * 4 + oc, :], pf[:, oc, 0:T2],
                                      scexp[:, 3, :])),
                                  (lambda e, oc=oc, half=half: e.tensor_add(
                                      y2f[:, half * 4 + oc, :],
                                      y2f[:, half * 4 + oc, :],
                                      hyfc[half * 4 + oc])),
                                  deps=[t_mm2, t_exe2, pe.get("y2f")])
                    pe[("pf", oc, 0)] = t_e2v
                    y2_toks.append(t_e2v)
            if tt + 1 < NT2:
                y_ld = load_y(tt + 1,
                              [y2_toks[-1],
                               y1_toks[min(tt + 1, len(y1_toks) - 1)]])

            t_y2b = sch.g("scalar",
                          *[(lambda e, i=i: e.copy(ypb[:, i, 0:T2], y2f[:, i, :]))
                            for i in range(8)],
                          deps=y2_toks + [pe.get("ypb2")])
            t_s1b = sch.g("tensor",
                          *[(lambda e, i=i: e.matmul(pf[0:1, 4, 0:T2], onescb[:],
                                                     ypb[:, i, 0:T2],
                                                     start=(i == 0), stop=(i == 7)))
                            for i in range(8)],
                          deps=[t_y2b, pe.get(("pf", 4))])
            t_mu2 = sch.g("vector",
                          lambda e: e.tensor_scalar(out=r1a[0:1, 0:T2],
                                                    in0=pf[0:1, 4, 0:T2],
                                                    scalar1=1.0 / D, scalar2=None,
                                                    op0=ALU.mult),
                          deps=[t_s1b, pe.get("r1a2")])
            pe[("pf", 4)] = t_mu2
            t_sq2b = sch.g("scalar",
                           *[(lambda e, i=i: e.activation(sq1[:, i, 0:T2],
                                                          ypb[:, i, 0:T2], AF.Square))
                             for i in range(8)],
                           deps=[t_y2b, pe.get("sq1")])
            t_s2b = sch.g("tensor",
                          *[(lambda e, i=i: e.matmul(pf[0:1, 4, 256:256 + T2], onescb[:],
                                                     sq1[:, i, 0:T2],
                                                     start=(i == 0), stop=(i == 7)))
                            for i in range(8)],
                          deps=[t_sq2b, t_mu2])
            pe["sq1"] = t_s2b
            t_var2 = sch.g("vector",
                           (lambda e: e.tensor_scalar(out=r1b[0:1, 0:T2],
                                                      in0=pf[0:1, 4, 256:256 + T2],
                                                      scalar1=1.0 / D, scalar2=None,
                                                      op0=ALU.mult)),
                           (lambda e: e.tensor_mul(pf[0:1, 4, 256:256 + T2],
                                                   r1a[0:1, 0:T2], r1a[0:1, 0:T2])),
                           (lambda e: e.tensor_sub(r1b[0:1, 0:T2], r1b[0:1, 0:T2],
                                                   pf[0:1, 4, 256:256 + T2])),
                           (lambda e: e.tensor_scalar(out=r1b[0:1, 0:T2],
                                                      in0=r1b[0:1, 0:T2],
                                                      scalar1=1e-5, scalar2=None,
                                                      op0=ALU.add)),
                           deps=[t_s2b, pe.get("r1b2")])
            pe[("pf", 4)] = t_var2
            t_inv2 = sch.g("scalar",
                           lambda e: e.activation(r1b[0:1, 0:T2], r1b[0:1, 0:T2], AF.Ln),
                           lambda e: e.activation(r1b[0:1, 0:T2], r1b[0:1, 0:T2],
                                                  AF.Exp, scale=-0.5),
                           deps=[t_var2])
            t_mexp2 = sch.g("tensor",
                            lambda e: e.matmul(pf[:, 4, 0:T2], onesrf[:],
                                               r1a[0:1, 0:T2], start=True, stop=True),
                            deps=[t_inv2, t_var2], mk=[(0, 1)])
            t_iexp2 = sch.g("tensor",
                            lambda e: e.matmul(pf[:, 5, 0:T2], onesrf[:],
                                               r1b[0:1, 0:T2], start=True, stop=True),
                            deps=[t_inv2, pe.get(("pf", 5))], mk=[(0, 1)])
            pe["r1a2"] = t_mexp2
            pe["r1b2"] = t_iexp2
            out_ops = []
            for i in range(8):
                out_ops.append(lambda e, i=i: e.tensor_sub(y2f[:, i, :], y2f[:, i, :],
                                                           pf[:, 4, 0:T2]))
                out_ops.append(lambda e, i=i: e.tensor_mul(y2f[:, i, :], y2f[:, i, :],
                                                           pf[:, 5, 0:T2]))
            t_out = sch.g("vector", *out_ops, deps=[t_mexp2, t_iexp2] + y2_toks)
            pe[("pf", 4)] = t_out
            pe[("pf", 5)] = t_out
            pe["ypb2"] = t_s2b
            t_od = sch.dma("gpsimd",
                           lambda e, tt=tt: e.dma_start(
                               out=outT.rearrange("(a p) s -> p a s", p=128)
                               [:, :, sl(tt, T2)],
                               in_=y2f),
                           deps=[t_out])
            pe["y2f"] = t_od
            out_toks.append(t_od)
            pe["n2b"] = t_mm2
            pe["scexp"] = t_exe2
            pe["scexp2"] = t_out
            pe["scrows"] = t_ex2
            pe["p2sc"] = t_rt2

        final_deps = (out_toks[-1:] if out_toks else []) + \
            [(ln, sch.dval[ln]) for ln in DSEM["gpsimd"] if ln in sch.dval]
        t_fin = sch.g("gpsimd", lambda e: e.nop(), deps=final_deps)

        with nc.Block() as block:
            @block.sync
            def _(eng):
                sch.replay("sync", eng, sems)

            @block.tensor
            def _(eng):
                sch.replay("tensor", eng, sems)

            @block.vector
            def _(eng):
                sch.replay("vector", eng, sems)

            @block.scalar
            def _(eng):
                sch.replay("scalar", eng, sems)

            @block.gpsimd
            def _(eng):
                sch.replay("gpsimd", eng, sems)

    _CACHE["nc"] = nc
    return nc


def _consts():
    ii = np.arange(128)
    c = {}
    c["ident"] = np.eye(128, dtype=np.float32)
    c["id64"] = np.vstack([np.eye(64, dtype=np.float32)] * 2)
    c["maskM"] = (ii[None, :] < ii[:, None]).astype(np.float32)
    c["maskMT"] = (ii[:, None] < ii[None, :]).astype(np.float32)
    c["maskAT"] = (ii[:, None] <= ii[None, :]).astype(np.float32)
    bexp = np.zeros((16, 8 * 128), np.float32)
    for i in range(8):
        for p in range(128):
            bexp[(i * 128 + p) // 64, i * 128 + p] = 1.0
    c["bexp"] = bexp
    c["bexpT"] = bexp.T.copy()
    c["onesc"] = np.ones((128, 1), np.float32)
    c["onesr"] = np.ones((1, 128), np.float32)
    sel16 = np.zeros((16, 16 * 128), np.float32)
    for j in range(16):
        r = 8 * (j % 2) + j // 2
        sel16[r, j * 128:(j + 1) * 128] = 1.0
    c["sel16"] = sel16
    return c


def kernel(**inputs):
    nc = build_nc()
    c = _consts()
    f32 = np.float32

    def A(v):
        return np.ascontiguousarray(np.asarray(v), dtype=f32)

    x = A(inputs["L0_attn_x"])
    wq = A(inputs["L0_attn_Wq"]); wk = A(inputs["L0_attn_Wk"])
    wv = A(inputs["L0_attn_Wv"]); wo = A(inputs["L0_attn_Wo"])
    cq = A(inputs["L0_attn_conv_q"]); ck = A(inputs["L0_attn_conv_k"])
    cv = A(inputs["L0_attn_conv_v"])
    wf = A(inputs["L0_attn_Wf"]); bf = A(inputs["L0_attn_bf"])
    wl = A(inputs["L0_attn_Wl"]); bl = A(inputs["L0_attn_bl"])
    wrq = A(inputs["L0_attn_Wrq"]); wrk = A(inputs["L0_attn_Wrk"])
    gs = A(inputs["L0_attn_graph_scale"])
    qn = A(inputs["L0_attn_qn_w"]); kn = A(inputs["L0_attn_kn_w"])
    on = A(inputs["L0_attn_on_w"])
    w1 = A(inputs["L0_ffn_w1"]); w2 = A(inputs["L0_ffn_w2"])

    wfl = np.zeros((D, 64), f32)
    wfl[:, 0:16] = wf
    wfl[:, 32:48] = wl
    bfl = np.zeros((64, 1), f32)
    bfl[0:16, 0] = bf
    bfl[32:48, 0] = bl
    wrq_p = np.zeros((D, D), f32)
    wrk_p = np.zeros((D, D), f32)
    RR_ = 48
    wrq3 = wrq.reshape(D, 16, RR_)
    wrk3 = wrk.reshape(D, 16, RR_)
    wrq_p.reshape(D, 16, 64)[:, :, 0:RR_] = wrq3
    wrk_p.reshape(D, 16, 64)[:, :, 0:RR_] = wrk3
    qnw = np.tile(qn, 2).reshape(128, 1)
    knw = np.tile(kn, 2).reshape(128, 1)
    onw = np.ascontiguousarray(on.reshape(8, 128).T)
    w1T = np.ascontiguousarray(w1.T)
    w2T = np.ascontiguousarray(w2.T)

    def cvl(cw_):
        return np.ascontiguousarray(cw_.reshape(8, 128, KC).transpose(1, 0, 2)
                                    .reshape(128, 8 * KC))
    shared = dict(wq=wq, wk=wk, wv=wv, wo=wo, wrq=wrq_p, wrk=wrk_p, wfl=wfl,
                  bfl=bfl, convq=cvl(cq), convk=cvl(ck), convv=cvl(cv),
                  gsc=gs.reshape(16, 1), qnw=qnw, knw=knw, onw=onw,
                  w1T=w1T, w2T=w2T, **c)
    in_maps = []
    for b in range(8):
        m = dict(shared)
        m["xT"] = np.ascontiguousarray(x[b].T)
        in_maps.append(m)

    res = run_bass_kernel_spmd(nc, in_maps, core_ids=list(range(8)))
    out = np.empty((8, S, D), np.float32)
    for b in range(8):
        out[b] = res.results[b]["outT"].T
    return out


# version 12
# speedup vs baseline: 1.3627x; 1.0096x over previous
import os

import numpy as np

import concourse.bass as bass
import concourse.mybir as mybir
from concourse.bass_utils import run_bass_kernel_spmd

F32 = mybir.dt.float32
BF16 = mybir.dt.bfloat16
I32 = mybir.dt.int32
AF = mybir.ActivationFunctionType
ALU = mybir.AluOpType
AXX = mybir.AxisListType.X

S = 4096
D = 1024
H = 16
DH = 64
KC = 4
DFF = 4096
T1 = 256
NST = int(os.environ.get("KN_NST", S // T1))
C = 128
NCH = T1 // C
T2 = 256
NT2 = int(os.environ.get("KN_NT2", S // T2))
KN_P15 = int(os.environ.get("KN_P15", 1))
KN_DN = int(os.environ.get("KN_DN", 1))
KN_STAGE = int(os.environ.get("KN_STAGE", 9))
KN_ST2 = int(os.environ.get("KN_ST2", 9))
KN_DN3 = int(os.environ.get("KN_DN3", 9))
KN_DN4 = int(os.environ.get("KN_DN4", 9))
KN_DN5 = int(os.environ.get("KN_DN5", 9))
KN_DN6 = int(os.environ.get("KN_DN6", 9))
KN_SOLVE = int(os.environ.get("KN_SOLVE", 7))
KN_DBG = int(os.environ.get("KN_DBG", 0))
NCT2 = T2 // 128
HB = 8

ENG = ("sync", "tensor", "vector", "scalar", "gpsimd")
CSEM = {"tensor": "st", "vector": "sv", "scalar": "ss", "gpsimd": "sg", "sync": "sy"}
NDL = {"sync": 8, "gpsimd": 4}
DSEM = {"sync": ["di%d" % i for i in range(8)], "gpsimd": ["do%d" % i for i in range(4)]}

DEBUG = False


class Sch:

    SERIAL = ("vector", "scalar", "gpsimd")

    def __init__(self):
        self.stream = {e: [] for e in ENG}
        self.cnt = {e: 0 for e in ENG}
        self.didx = {"sync": 0, "gpsimd": 0}
        self.dval = {}
        self.lastw = {e: {} for e in ENG}
        self.lastp = None
        self.dummy_fn = None

    def _waits(self, e, deps):
        ws = []
        flat = []
        for d in deps:
            if d is None:
                continue
            if isinstance(d, list):
                flat.extend(d)
            else:
                flat.append(d)
        for dtok in flat:
            if dtok is None:
                continue
            sem, val = dtok
            if sem == CSEM.get(e) and e not in self.SERIAL:
                continue
            if sem == CSEM.get(e) and e in self.SERIAL:
                continue
            if self.lastw[e].get(sem, -1) >= val:
                continue
            ws.append((sem, val))
            self.lastw[e][sem] = val
        if ws:
            self.stream[e].append(("waits", ws))

    def g(self, e, *fns, deps=(), mk=None):
        assert fns
        self._waits(e, deps)
        if e == "tensor":
            metas = mk if mk is not None else [None] * len(fns)
            out = []
            for fn, m in zip(fns, metas):
                if m is not None and self.lastp is not None and m[0] != self.lastp[0]:
                    out.append(self.dummy_fn)
                self.lastp = m
                out.append(fn)
            fns = tuple(out)
        if e in self.SERIAL:
            for fn in fns:
                self.stream[e].append(("selfwait", self.cnt[e]))
                self.stream[e].append(("opinc", fn))
                self.cnt[e] += 1
        else:
            for fn in fns[:-1]:
                self.stream[e].append(("op", fn))
            self.stream[e].append(("opinc", fns[-1]))
            self.cnt[e] += 1
        return (CSEM[e], self.cnt[e])

    def dma(self, e, *fns, deps=()):
        self._waits(e, deps)
        toks = {}
        for fn in fns:
            lane = DSEM[e][self.didx[e] % NDL[e]]
            self.didx[e] += 1
            prev = self.dval.get(lane, 0)
            if self.lastw[e].get(lane, -1) < prev and prev > 0:
                self.stream[e].append(("waits", [(lane, prev)]))
                self.lastw[e][lane] = prev
            self.dval[lane] = prev + 16
            self.stream[e].append(("dma", fn, lane))
            toks[lane] = prev + 16
        out = [(k, v) for k, v in toks.items()]
        return out if len(out) > 1 else out[0]

    def replay(self, e, eng, sems):
        own = sems.get(CSEM.get(e))
        for item in self.stream[e]:
            k = item[0]
            if k == "waits":
                for (sem, val) in item[1]:
                    eng.wait_ge(sems[sem], val)
            elif k == "selfwait":
                if item[1] > 0:
                    eng.wait_ge(own, item[1])
            elif k == "op":
                item[1](eng)
            elif k == "opinc":
                item[1](eng).then_inc(own, 1)
            else:
                item[1](eng).then_inc(sems[item[2]], 16)


def sl(i, n):
    return slice(i * n, (i + 1) * n)


def _maxtok(*toks):
    best = {}
    for t in toks:
        if t is None:
            continue
        best[t[0]] = max(best.get(t[0], -1), t[1])
    return [(k, v) for k, v in best.items()]


_CACHE = {}
OFFS = {}


def build_nc():
    if "nc" in _CACHE:
        return _CACHE["nc"]
    nc = bass.Bass("TRN2", target_bir_lowering=False, debug=False)

    def din(name, shape, dt=F32):
        return nc.dram_tensor(name, list(shape), dt, kind="ExternalInput").ap()

    def dout_(name, shape, dt=F32):
        return nc.dram_tensor(name, list(shape), dt, kind="ExternalOutput").ap()

    def dscr(name, shape, dt):
        return nc.dram_tensor(name, list(shape), dt, kind="Internal").ap()

    xT = din("xT", [D, S])
    wq_d = din("wq", [D, D]); wk_d = din("wk", [D, D]); wv_d = din("wv", [D, D])
    wo_d = din("wo", [D, D]); wrq_d = din("wrq", [D, D]); wrk_d = din("wrk", [D, D])
    wfl_d = din("wfl", [D, 64])
    bfl_d = din("bfl", [64, 1])
    cvq_d = din("convq", [128, 8 * KC]); cvk_d = din("convk", [128, 8 * KC])
    cvv_d = din("convv", [128, 8 * KC])
    gsc_d = din("gsc", [16, 1])
    qnw_d = din("qnw", [128, 1]); knw_d = din("knw", [128, 1])
    onw_d = din("onw", [128, 8])
    w1T_d = din("w1T", [D, DFF])
    w2T_d = din("w2T", [DFF, D])
    ident_d = din("ident", [128, 128])
    id64_d = din("id64", [128, 64])
    maskM_d = din("maskM", [128, 128])
    maskMT_d = din("maskMT", [128, 128])
    maskAT_d = din("maskAT", [128, 128])
    bexp_d = din("bexp", [16, 8 * 128])
    bexpT_d = din("bexpT", [8 * 128, 16])
    onesc_d = din("onesc", [128, 1])
    onesr_d = din("onesr", [1, 128])
    sel16_d = din("sel16", [16, 16 * 128])

    y1s = dscr("y1s", [D, S], F32)
    w2qs = dscr("w2qs", [DFF, D], BF16)
    wrqs = dscr("wrqs", [D, D], BF16)
    wrks = dscr("wrks", [D, D], BF16)
    wos = dscr("wos", [D, D], BF16)
    outT = dout_("outT", [D, S])
    if KN_DBG:
        dbg_d = dout_("dbgd", [128, 4 * 512])
        dbg_s = dout_("dbgs", [128, NST * 512], BF16)
    dbg = {}
    if DEBUG:
        for nm, shp in [("d_qt", [D, T1]), ("d_kt", [D, T1]), ("d_vc", [D, T1]),
                        ("d_ctm", [128, 128]), ("d_W", [128, HB * 128]),
                        ("d_O", [D, T1]), ("d_y1", [D, T1])]:
            dbg[nm] = dout_(nm, shp, F32)

    sch = Sch()
    import contextlib
    stack = contextlib.ExitStack()
    sb = {}

    def alo(name, shape, dt):
        sb[name] = stack.enter_context(nc.sbuf_tensor(name, list(shape), dt))
        return sb[name]

    with stack:
        NB16 = 73 * 1024
        NF32 = 15320
        AB16 = alo("AB16", [128, NB16], BF16)
        AF32 = alo("AF32", [128, NF32], F32)
        _o = {"b": 0, "f": 0}

        def cb16(shape, at=None, name=None):
            n = int(np.prod(shape[1:])) if len(shape) > 1 else 1
            off = _o["b"] if at is None else at
            ap = AB16.ap()[0:shape[0], off:off + n]
            if at is None:
                _o["b"] += n
            if len(shape) > 2:
                ap = ap.rearrange("p (a n) -> p a n", a=shape[1])
            import traceback
            ln = traceback.extract_stack()[-2].line
            nm = ln.split("=")[0].strip().split(",")[0] if ln else None
            if nm:
                OFFS[nm] = ("b", off, tuple(shape))
            return ap, off

        def cf32(shape, at=None):
            n = int(np.prod(shape[1:])) if len(shape) > 1 else 1
            off = _o["f"] if at is None else at
            ap = AF32.ap()[0:shape[0], off:off + n]
            if at is None:
                _o["f"] += n
            if len(shape) > 2:
                ap = ap.rearrange("p (a n) -> p a n", a=shape[1])
            import traceback
            ln = traceback.extract_stack()[-2].line
            nm = ln.split("=")[0].strip().split(",")[0] if ln else None
            if nm:
                OFFS[nm] = ("f", off, tuple(shape))
            return ap, off

        wqb, o_wqb = cb16([128, 8, 1024])
        wkb, _ = cb16([128, 8, 1024])
        wvb, _ = cb16([128, 8, 1024])
        wsb0, _ = cb16([128, 8, 1024])
        w1qb, _ = cb16([128, 8, DFF], at=o_wqb)
        wsb1, o_wsb1 = cb16([128, 8, 1024])
        n2b, _ = cb16([128, 32, T2], at=o_wsb1)
        wsb = [wsb0, wsb1]
        wflb, _ = cb16([128, 8, 64])
        xbb, _ = cb16([128, 8, T1])
        pre, _ = cb16([128, 8, KC - 1 + T1])
        qc, o_qc = cb16([128, 8, T1])
        n1b, _ = cb16([128, 8, T2], at=o_qc)
        kc_, o_kc = cb16([128, 8, T1])
        w2qc0, _ = cb16([128, 1024], at=o_kc)
        w2qc1, _ = cb16([128, 1024], at=o_kc + 1024)
        w2qc = [w2qc0, w2qc1]
        vc, _ = cb16([128, 8, T1])
        rq, _ = cb16([128, 8, T1])
        rk, _ = cb16([128, 8, T1])
        ofm, _ = cb16([128, 8, T1])
        ypb, _ = cb16([128, 8, T1])
        sq1, _ = cb16([128, 8, T1])
        m_m, _ = cb16([128, HB, 128])
        m_mt, _ = cb16([128, HB, 128])
        m_at, _ = cb16([128, HB, 128])
        gb_, _ = cb16([128, HB, 128])
        gtb_, _ = cb16([128, HB, 128])
        gbuf = [gb_, gb_]
        gtbuf = [gtb_, gtb_]
        wb_, _ = cb16([128, HB, 128])
        wbuf = [wb_, wb_]
        expb, _ = cb16([128, 128])
        gexp, _ = cb16([128, HB, 128])
        kcs, _ = cb16([128, HB, 64])
        ztb, _ = cb16([128, HB, 128])
        awvb, _ = cb16([128, HB, 128])
        pktb, _ = cb16([128, HB // 2, 64])
        stb, _ = cb16([128, H // 2, 64])
        sc16, _ = cb16([16, T1])
        identb, _ = cb16([128, 128])
        maskMb, _ = cb16([128, 128])
        maskMTb, _ = cb16([128, 128])
        maskATb, _ = cb16([128, 128])
        bexpb, _ = cb16([16, 8 * 128])
        bexpTb, _ = cb16([128, 8, 16])
        onescb, _ = cb16([128, 1])
        onesrb, _ = cb16([1, 128])
        lam16, _ = cb16([16, 1])
        halo, _ = cb16([128, 3 * 8, KC - 1])
        assert _o["b"] <= NB16, _o["b"]

        stage, o_st = cf32([128, 1, 1024])
        xf0, _ = cf32([128, 8, T1])
        xf1, _ = cf32([128, 8, T1])
        xf = [xf0, xf1]
        hf, _ = cf32([128, 32, T2], at=o_st)
        ypre, o_yp = cf32([128, 8, T1])
        y1f = ypre
        stAB = [stage.rearrange("p a n -> p (a n)"),
                cf32([128, 1024], at=1024)[0]]
        flf, o_fl = cf32([64, T1])
        lnab, _ = cf32([64, T1])
        tbl1, _ = cf32([128, T1])
        tbl2, _ = cf32([128, T1])
        y2f, _ = cf32([128, 8, T2], at=0)
        ctm, o_ctm = cf32([128, NCH, 128])
        s16a, _ = cf32([16, T1])
        s16b, _ = cf32([16, T1])
        scexp, _ = cf32([128, 4, T2], at=o_ctm)
        cscr, o_cs = cf32([128, T1])
        crows, _ = cf32([16, 128])
        tmax, _ = cf32([128, T2], at=o_cs + T2)
        argf, _ = cf32([128, 128])
        pvf, o_pvf = cf32([128, HB // 2, 64])
        stf, o_stf = cf32([128, H // 2, 64])
        gcf, o_gcf = cf32([128, HB, 1])
        p2sc, _ = cf32([128, 8], at=o_gcf)
        crs_scr, _ = cf32([128, 16])
        r1a, _ = cf32([1, T1])
        r1b, _ = cf32([1, T1])
        qs_g, o_qsg = cf32([128, 512])
        qs_f, o_qsf = cf32([128, 512])
        argf8, _ = cf32([128, 8, 128], at=o_qsg)
        wsum, _ = cf32([128, 12])
        sval, _ = cf32([128, 8])
        identf, _ = cf32([128, 128])
        id64f, _ = cf32([128, 64])
        sel16f, _ = cf32([16, 16 * 128])
        onescf, _ = cf32([128, 1])
        onesrf, _ = cf32([1, 128])
        qnwb, _ = cf32([128, 1])
        knwb, _ = cf32([128, 1])
        onwb, _ = cf32([128, 8])
        lamb, _ = cf32([128, 8])
        convb, _ = cf32([128, 24, KC])
        bflb, _ = cf32([64, 1])
        o_tail = _o["f"]
        hy_pv, _ = cf32([128, 256], at=o_pvf)
        hy_st, _ = cf32([128, 2, 256], at=o_stf)
        hy_qg, _ = cf32([128, 256], at=o_qsg + 256)
        hy_qf, _ = cf32([128, 256], at=o_qsf + 256)
        hy_tl, _ = cf32([128, 3, 256], at=o_tail)
        cscr2 = hy_tl[:, 0, :]
        scrows, _ = cf32([16, 128], at=o_tail + 768)
        hyfc = [hy_pv, hy_st[:, 0, :], hy_st[:, 1, :], hy_qg, hy_qf,
                hy_tl[:, 0, :], hy_tl[:, 1, :], hy_tl[:, 2, :]]
        assert o_tail + 768 + 128 <= NF32, o_tail
        assert _o["f"] <= NF32, _o["f"]
        qs_i = stack.enter_context(nc.sbuf_tensor("qs_i", [128, 512], I32))

        pf = stack.enter_context(nc.psum_tensor("pf", [128, 6, 512], F32))
        pb = stack.enter_context(nc.psum_tensor("pb", [128, 1, 1024], BF16))
        pfd = stack.enter_context(nc.psum_tensor("pfd", [128, 512], F32))
        sch.dummy_fn = lambda e: e.matmul(pfd[:, 0:16], identb[:], identb[:, 0:16],
                                          start=True, stop=True)

        sems = {}
        for s_ in list(CSEM.values()) + DSEM["sync"] + DSEM["gpsimd"]:
            sems[s_] = stack.enter_context(nc.semaphore(s_))

        pe = {}

        t0 = sch.dma(
            "sync",
            lambda e: e.dma_start(out=identf[:], in_=ident_d),
            lambda e: e.dma_start(out=id64f[:], in_=id64_d),
            lambda e: e.dma_start(out=stage[:, 0, 0:128], in_=maskM_d),
            lambda e: e.dma_start(out=stage[:, 0, 128:256], in_=maskMT_d),
            lambda e: e.dma_start(out=stage[:, 0, 256:384], in_=maskAT_d),
            lambda e: e.dma_start(out=stage[:, 0, 384:512].rearrange("p (a n) -> p a n", a=8),
                                  in_=bexpT_d.rearrange("(a p) n -> p a n", p=128)),
            lambda e: e.dma_start(out=onescf[:], in_=onesc_d),
            lambda e: e.dma_start(out=onesrf[:], in_=onesr_d),
            lambda e: e.dma_start(out=sel16f[:], in_=sel16_d),
            lambda e: e.dma_start(out=qnwb[:], in_=qnw_d),
            lambda e: e.dma_start(out=knwb[:], in_=knw_d),
            lambda e: e.dma_start(out=onwb[:], in_=onw_d),
            lambda e: e.dma_start(out=bflb[:], in_=bfl_d),
            lambda e: e.dma_start(out=stage[0:16, 0, 512:513], in_=gsc_d),
            lambda e: e.dma_start(out=convb[:, 0:8, :].rearrange("p a k -> p (a k)"),
                                  in_=cvq_d),
            lambda e: e.dma_start(out=convb[:, 8:16, :].rearrange("p a k -> p (a k)"),
                                  in_=cvk_d),
            lambda e: e.dma_start(out=convb[:, 16:24, :].rearrange("p a k -> p (a k)"),
                                  in_=cvv_d),
        )
        t_cc = sch.g(
            "vector",
            lambda e: e.tensor_copy(identb[:], identf[:]),
            lambda e: e.tensor_copy(maskMb[:], stage[:, 0, 0:128]),
            lambda e: e.tensor_copy(maskMTb[:], stage[:, 0, 128:256]),
            lambda e: e.tensor_copy(maskATb[:], stage[:, 0, 256:384]),
            lambda e: e.tensor_copy(bexpTb[:].rearrange("p a n -> p (a n)"),
                                    stage[:, 0, 384:512]),
            lambda e: e.tensor_copy(onescb[:], onescf[:]),
            lambda e: e.tensor_copy(onesrb[:], onesrf[:]),
            deps=[t0],
        )
        pass
        t_abs = sch.g("scalar",
                      lambda e: e.activation(stage[0:16, 0, 513:514], stage[0:16, 0, 512:513], AF.Abs),
                      deps=[t0])
        t_l16 = sch.g("vector", lambda e: e.tensor_copy(lam16[:], stage[0:16, 0, 513:514]),
                      deps=[t_abs])
        t_bexp_d = sch.dma("sync",
                           lambda e: e.dma_start(out=stage[0:16, 0, 0:1024], in_=bexp_d),
                           deps=[t_cc, t_l16])
        t_bexp = sch.g("vector",
                       lambda e: e.tensor_copy(bexpb[:], stage[0:16, 0, 0:1024]),
                       deps=[t_bexp_d])
        t_lmm = sch.g("tensor",
                      *[(lambda e, i=i: e.matmul(pf[:, 0, sl(i, 1)], bexpb[0:16, sl(i, 128)],
                                                 lam16[:], start=True, stop=True))
                        for i in range(8)],
                      deps=[t_l16, t_bexp], mk=[(0, 16)] * 8)
        t_lam = sch.g("vector", lambda e: e.tensor_copy(lamb[:], pf[:, 0, 0:8]), deps=[t_lmm])
        pe[("pf", 0)] = t_lam

        def w_res(dram, dst, prev):
            tok = None
            for r in range(8):
                t_d = sch.dma("sync",
                              lambda e, dram=dram, r=r: e.dma_start(
                                  out=stage[:, 0, :],
                                  in_=dram.rearrange("(a p) n -> p a n", p=128)[:, r, :]),
                              deps=(prev if r == 0 else [tok]))
                tok = sch.g("vector",
                            lambda e, r=r, dst=dst: e.tensor_copy(dst[:, r, :],
                                                                  stage[:, 0, :]),
                            deps=[t_d])
            return tok

        t_wq = w_res(wq_d, wqb, [t_lam, t_cc, t_bexp])
        t_wk = w_res(wk_d, wkb, [t_wq])
        t_wv = w_res(wv_d, wvb, [t_wk])
        t_fld = sch.dma("sync",
                        lambda e: e.dma_start(
                            out=stage[:, 0, 0:512].rearrange("p (a n) -> p a n", a=8),
                            in_=wfl_d.rearrange("(a p) n -> p a n", p=128)),
                        deps=[t_wv])
        t_wfl = sch.g("vector",
                      *[(lambda e, i=i: e.tensor_copy(
                          wflb[:, i, :],
                          stage[:, 0, 0:512].rearrange("p (a n) -> p a n", a=8)[:, i, :]))
                        for i in range(8)],
                      deps=[t_fld])

        def w_scr(dram, scr, fold, prev):
            tok = None
            for r in range(8):
                t_d = sch.dma("sync",
                              lambda e, dram=dram, r=r: e.dma_start(
                                  out=stage[:, 0, :],
                                  in_=dram.rearrange("(a p) n -> p a n", p=128)[:, r, :]),
                              deps=(prev if r == 0 else [tok]))
                if fold:
                    tok = sch.g("vector",
                                lambda e, r=r: e.tensor_scalar(
                                    out=wsb[0][:, r, :], in0=stage[:, 0, :],
                                    scalar1=onwb[:, r:r + 1], scalar2=None,
                                    op0=ALU.mult),
                                deps=[t_d])
                else:
                    tok = sch.g("vector",
                                lambda e, r=r: e.tensor_copy(wsb[0][:, r, :],
                                                             stage[:, 0, :]),
                                deps=[t_d])
            return sch.dma("gpsimd",
                           lambda e, scr=scr: e.dma_start(
                               out=scr.rearrange("(a p) n -> p a n", p=128),
                               in_=wsb[0]),
                           deps=[tok])

        t_srq = w_scr(wrq_d, wrqs, False, [t_wfl])
        t_srk = w_scr(wrk_d, wrks, False, [t_srq])
        t_swo = w_scr(wo_d, wos, True, [t_srk])

        t_init = sch.g(
            "vector",
            lambda e: e.memset(stf[:].rearrange("p a n -> p (a n)"), 0.0),
            lambda e: e.memset(stb[:].rearrange("p a n -> p (a n)"), 0.0),
            lambda e: e.memset(halo[:].rearrange("p a c -> p (a c)"), 0.0),
            lambda e: e.memset(tbl1[:], 0.0),
            lambda e: e.memset(tbl2[:], 0.0),
        )
        x_tok = [None, None]
        x_tok[0] = sch.dma("sync",
                           lambda e: e.dma_start(
                               out=xf[0],
                               in_=xT.rearrange("(a p) s -> p a s", p=128)[:, :, 0:T1]),
                           deps=[t_swo])
        ws_tok = {}

        def load_ws(which, buf, deps):
            scr = {"rq": wrqs, "rk": wrks, "wo": wos}[which]
            return sch.dma("sync",
                           lambda e, scr=scr, buf=buf: e.dma_start(
                               out=wsb[buf],
                               in_=scr.rearrange("(a p) n -> p a n", p=128)),
                           deps=deps)

        ws_tok[(0, "rq")] = load_ws("rq", 0, [t_swo])
        ws_tok[(0, "rk")] = load_ws("rk", 1, [ws_tok[(0, "rq")]])
        wsbuf_of = lambda st, k: (3 * st + k) % 2
        y1_toks = []

        for st in range(NST):
            cur = st % 2
            tx = x_tok[cur]
            if KN_DBG:
                t_dbgs = sch.dma("gpsimd",
                                 lambda e, st=st: e.dma_start(
                                     out=dbg_s[:, sl(st, 512)],
                                     in_=stb[:].rearrange("p a n -> p (a n)")),
                                 deps=[pe.get("stb_r"), pe.get("dbgs"), t_init])
                pe["dbgs"] = t_dbgs
                pe["stb_r2"] = t_dbgs
            if st + 1 < NST:
                x_tok[(st + 1) % 2] = sch.dma(
                    "sync",
                    lambda e, st=st: e.dma_start(
                        out=xf[(st + 1) % 2],
                        in_=xT.rearrange("(a p) s -> p a s", p=128)[:, :, sl(st + 1, T1)]),
                    deps=[pe.get(("xf", (st + 1) % 2))])

            t_xbv = sch.g("vector",
                          *[(lambda e, i=i, cur=cur: e.tensor_copy(xbb[:, i, :],
                                                                   xf[cur][:, i, :]))
                            for i in range(6)],
                          deps=[tx, pe.get("xbb")])
            t_xbg = sch.g("gpsimd",
                          *[(lambda e, i=i, cur=cur: e.tensor_copy(xbb[:, i, :],
                                                                   xf[cur][:, i, :]))
                            for i in (6, 7)],
                          deps=[tx, pe.get("xbb")])
            t_xb = [t_xbv, t_xbg]

            def proj(wsrc, dst, dst_off, wtok, freedep=None):
                toks = []
                for pr in range(4):
                    t_mm = None
                    for s_ in range(2):
                        oc = 2 * pr + s_
                        mms = [(lambda e, oc=oc, kk=kk, pr=pr, s_=s_, wsrc=wsrc:
                                e.matmul(pf[:, pr, s_ * T1:s_ * T1 + T1],
                                         wsrc[:, kk, sl(oc, 128)],
                                         xbb[:, kk, :], start=(kk == 0),
                                         stop=(kk == 7)))
                               for kk in range(8)]
                        t_mm = sch.g("tensor", *mms,
                                     deps=[t_xb, wtok]
                                     + ([pe.get(("pf", pr))] if s_ == 0 else []))
                    t_ev = sch.g("scalar",
                                 lambda e, pr=pr, dst=dst, off=dst_off:
                                 e.copy(dst[:, 2 * pr:2 * pr + 2, off:off + T1],
                                        pf[:, pr, 0:2 * T1]
                                        .rearrange("p (a n) -> p a n", a=2)),
                                 deps=[t_mm] + ([freedep] if freedep else []))
                    pe[("pf", pr)] = t_ev
                    toks.append(t_ev)
                return toks

            def halo_in(tix, deps):
                return sch.g("vector",
                             *[(lambda e, i=i, tix=tix: e.tensor_copy(
                                 pre[:, i, 0:KC - 1], halo[:, tix * 8 + i, :]))
                               for i in range(8)],
                             deps=deps + [t_init])

            def halo_out(tix, deps):
                return sch.g("vector",
                             *[(lambda e, i=i, tix=tix: e.tensor_copy(
                                 halo[:, tix * 8 + i, :], pre[:, i, T1:T1 + KC - 1]))
                               for i in range(8)],
                             deps=deps)

            def conv_emit(dst, widx, deps, freedep=None):
                toks = []
                for i in range(8):
                    eng = "vector"
                    buf = cscr
                    ops = [lambda e, i=i, widx=widx, buf=buf: e.tensor_scalar(
                        out=buf[:], in0=pre[:, i, 0:T1],
                        scalar1=convb[:, widx * 8 + i, 0:1], scalar2=None, op0=ALU.mult)]
                    for j in range(1, KC):
                        ops.append(lambda e, i=i, j=j, widx=widx, buf=buf:
                                   e.scalar_tensor_tensor(
                                       out=buf[:], in0=pre[:, i, j:j + T1],
                                       scalar=convb[:, widx * 8 + i, j:j + 1], in1=buf[:],
                                       op0=ALU.mult, op1=ALU.add))
                    ops.append(lambda e, i=i, dst=dst, buf=buf:
                               e.tensor_copy(dst[:, i, :], buf[:]))
                    toks.append(sch.g(eng, *ops,
                                      deps=deps + ([freedep] if freedep else [])))
                return toks

            t_hq = halo_in(0, [pe.get("pre")])
            tq = proj(wqb, pre, KC - 1, t_wq, pe.get("pre"))
            t_hqs = halo_out(0, tq)
            tcq = conv_emit(qc, 0, tq + [t_hq], pe.get("qcF"))
            t_hk = halo_in(1, [tcq[-1], t_hqs])
            tk = proj(wkb, pre, KC - 1, t_wk, tcq[-1])
            t_hks = halo_out(1, tk)
            tck = conv_emit(kc_, 1, tk + [t_hk], pe.get("kcF"))
            t_hv = halo_in(2, [tck[-1], t_hks])
            tv = proj(wvb, pre, KC - 1, t_wv, tck[-1])
            t_hvs = halo_out(2, tv)
            tcv = conv_emit(vc, 2, tv + [t_hv], pe.get("vcF"))
            pe["pre"] = tcv[-1]

            trq = proj(wsb[wsbuf_of(st, 0)], rq, 0, ws_tok[(st, "rq")], pe.get("rqF"))
            trk = proj(wsb[wsbuf_of(st, 1)], rk, 0, ws_tok[(st, "rk")], pe.get("rkF"))
            ws_tok[(st, "wo")] = load_ws("wo", wsbuf_of(st, 2), trq)
            if st + 1 < NST:
                ws_tok[(st + 1, "rq")] = load_ws("rq", wsbuf_of(st + 1, 0), trk)

            t_flmm = sch.g("tensor",
                           *[(lambda e, kk=kk: e.matmul(pf[0:64, 4, 0:T1], wflb[:, kk, :],
                                                        xbb[:, kk, :],
                                                        start=(kk == 0), stop=(kk == 7)))
                             for kk in range(8)],
                           deps=[t_xb, t_wfl, pe.get(("pf", 4))])
            t_fl = sch.g("vector",
                         lambda e: e.tensor_scalar(out=flf[:], in0=pf[0:64, 4, 0:T1],
                                                   scalar1=bflb[:], scalar2=None, op0=ALU.add),
                         deps=[t_flmm, pe.get("flf")])
            pe[("pf", 4)] = t_fl

            def headsums(src, deps):
                t_sq = sch.g("scalar",
                             *[(lambda e, i=i, src=src: e.activation(sq1[:, i, :],
                                                                     src[:, i, :], AF.Square))
                               for i in range(8)],
                             deps=deps + [pe.get("sq1")])
                t_mm = sch.g("tensor",
                             *[(lambda e, i=i: e.matmul(pf[0:16, 4, 0:T1], bexpTb[:, i, :],
                                                        sq1[:, i, :],
                                                        start=(i == 0), stop=(i == 7)))
                               for i in range(8)],
                             deps=[t_sq, pe.get(("pf", 4))])
                pe["sq1"] = t_mm
                return t_mm

            def expand_apply(apply_fn_list, deps):
                toks = []
                for i in range(8):
                    t_e = sch.g("tensor",
                                lambda e, i=i: e.matmul(pf[:, 5, 0:T1],
                                                        bexpb[0:16, sl(i, 128)], sc16[:],
                                                        start=True, stop=True),
                                deps=deps + [pe.get(("pf", 5))], mk=[(0, 16)])
                    t_a = apply_fn_list[i](t_e)
                    pe[("pf", 5)] = t_a
                    toks.append(t_a)
                return toks

            t_qs = headsums(qc, tcq)
            t_qsc = sch.g("vector",
                          lambda e: e.tensor_scalar(out=s16a[:], in0=pf[0:16, 4, 0:T1],
                                                    scalar1=1.0 / DH, scalar2=1e-6,
                                                    op0=ALU.mult, op1=ALU.add),
                          deps=[t_qs])
            pe[("pf", 4)] = t_qsc
            t_q2 = sch.g("scalar",
                         lambda e: e.activation(s16b[:], s16a[:], AF.Ln),
                         lambda e: e.activation(s16a[:], s16b[:], AF.Exp, scale=-0.5),
                         deps=[t_qsc])
            t_qb = sch.g("vector", lambda e: e.tensor_copy(sc16[:], s16a[:]),
                         deps=[t_q2, pe.get("sc16")])

            def mkqfn(i):
                def f(t_e):
                    return sch.g("vector",
                                 (lambda e, i=i: e.scalar_tensor_tensor(
                                     out=qc[:, i, :], in0=qc[:, i, :], scalar=qnwb[:],
                                     in1=pf[:, 5, 0:T1], op0=ALU.mult, op1=ALU.mult)),
                                 (lambda e, i=i: e.scalar_tensor_tensor(
                                     out=qc[:, i, :], in0=rq[:, i, :],
                                     scalar=lamb[:, i:i + 1], in1=qc[:, i, :],
                                     op0=ALU.mult, op1=ALU.add)),
                                 deps=[t_e] + tcq + trq)
                return f

            tqt = expand_apply([mkqfn(i) for i in range(8)], [t_qb])
            pe["sc16"] = tqt[-1]

            t_ks = headsums(kc_, tck)
            t_ksc = sch.g("vector",
                          lambda e: e.tensor_scalar(out=s16a[:], in0=pf[0:16, 4, 0:T1],
                                                    scalar1=1.0 / DH, scalar2=1e-6,
                                                    op0=ALU.mult, op1=ALU.add),
                          deps=[t_ks])
            pe[("pf", 4)] = t_ksc
            t_k2 = sch.g("scalar",
                         lambda e: e.activation(s16b[:], s16a[:], AF.Ln),
                         lambda e: e.activation(s16a[:], s16b[:], AF.Exp, scale=-0.5),
                         deps=[t_ksc])
            t_kb = sch.g("vector", lambda e: e.tensor_copy(sc16[:], s16a[:]),
                         deps=[t_k2, pe.get("sc16")])

            def mkkfn(i):
                def f(t_e):
                    return sch.g("vector",
                                 (lambda e, i=i: e.scalar_tensor_tensor(
                                     out=kc_[:, i, :], in0=kc_[:, i, :], scalar=knwb[:],
                                     in1=pf[:, 5, 0:T1], op0=ALU.mult, op1=ALU.mult)),
                                 (lambda e, i=i: e.scalar_tensor_tensor(
                                     out=kc_[:, i, :], in0=rk[:, i, :],
                                     scalar=lamb[:, i:i + 1], in1=kc_[:, i, :],
                                     op0=ALU.mult, op1=ALU.add)),
                                 deps=[t_e] + tck + trk)
                return f

            tks_ = expand_apply([mkkfn(i) for i in range(8)], [t_kb])
            pe["sc16"] = tks_[-1]

            t_k2s = headsums(kc_, tks_)
            t_k2l = sch.g("scalar",
                          lambda e: e.activation(s16b[:], pf[0:16, 4, 0:T1], AF.Ln),
                          lambda e: e.activation(s16a[:], s16b[:], AF.Exp, scale=0.5),
                          deps=[t_k2s])
            pe[("pf", 4)] = t_k2l
            t_k2c = sch.g("vector",
                          lambda e: e.tensor_scalar(out=s16a[:], in0=s16a[:], scalar1=1e-12,
                                                    scalar2=None, op0=ALU.max),
                          lambda e: e.reciprocal(s16b[:], s16a[:]),
                          lambda e: e.tensor_copy(sc16[:], s16b[:]),
                          deps=[t_k2l, pe.get("sc16")])

            def mknfn(i):
                def f(t_e):
                    return sch.g("vector",
                                 lambda e, i=i: e.tensor_mul(kc_[:, i, :], kc_[:, i, :],
                                                             pf[:, 5, 0:T1]),
                                 deps=[t_e])
                return f

            tkt = expand_apply([mknfn(i) for i in range(8)], [t_k2c])
            pe["sc16"] = tkt[-1]

            if KN_STAGE < 2:
                continue
            t_t = sch.g("scalar",
                        lambda e: e.activation(lnab[:], flf[:], AF.Exp, scale=-1.0),
                        deps=[t_fl, pe.get("lnab")])
            pe["flf"] = t_t
            t_t2 = sch.g("vector",
                         lambda e: e.tensor_scalar(out=lnab[:], in0=lnab[:], scalar1=1.0,
                                                   scalar2=None, op0=ALU.add),
                         deps=[t_t])
            t_t3 = sch.g("scalar", lambda e: e.activation(lnab[:], lnab[:], AF.Ln),
                         deps=[t_t2])
            t_t4 = sch.g("vector",
                         lambda e: e.tensor_scalar(out=lnab[:], in0=lnab[:], scalar1=-1.0,
                                                   scalar2=None, op0=ALU.mult),
                         deps=[t_t3])
            if KN_ST2 < 1:
                continue
            ctm_toks = []
            for ch in range(NCH):
                cs = sl(ch, C)
                t_c = sch.g("vector",
                            (lambda e, cs=cs: e.tensor_tensor_scan(
                                out=tbl1[0:16, cs], data0=lnab[0:16, cs],
                                data1=lnab[0:16, cs], initial=0.0,
                                op0=ALU.add, op1=ALU.bypass)),
                            (lambda e, cs=cs: e.tensor_sub(tbl1[32:48, cs], tbl1[0:16, cs],
                                                           lnab[0:16, cs])),
                            (lambda e, cs=cs: e.tensor_add(tbl1[32:48, cs], tbl1[32:48, cs],
                                                           lnab[32:48, cs])),
                            deps=[t_t4, pe.get("tbl1")])
                if KN_ST2 < 2:
                    ctm_toks.append(t_c)
                    continue
                t_bx = sch.g("scalar",
                             (lambda e, cs=cs: e.activation(tbl1[64:80, cs], lnab[32:48, cs],
                                                            AF.Exp)),
                             (lambda e, cs=cs: e.activation(tbl1[96:112, cs],
                                                            tbl1[32:48, cs], AF.Exp)),
                             deps=[t_c])
                t_g1 = sch.g("vector",
                             lambda e, cs=cs, ch=ch: e.tensor_scalar(
                                 out=tbl2[0:16, cs], in0=tbl1[0:16, cs],
                                 scalar1=tbl1[0:16, ch * C + C - 1:ch * C + C],
                                 scalar2=None, op0=ALU.subtract),
                             deps=[t_c, pe.get("tbl2")])
                t_g2 = sch.g("scalar",
                             lambda e, cs=cs: e.activation(tbl2[0:16, cs], tbl2[0:16, cs],
                                                           AF.Exp, scale=-1.0),
                             deps=[t_g1])
                if KN_ST2 < 3:
                    ctm_toks.append(t_g2)
                    continue
                t_tr = sch.g("tensor",
                             lambda e, cs=cs: e.transpose(pf[:, 4, 0:128], tbl1[:, cs],
                                                          identf[:]),
                             deps=[t_bx, t_init, pe.get(("pf", 4))])
                t_ctm = sch.g("vector",
                              lambda e, ch=ch: e.tensor_copy(ctm[:, ch, :],
                                                             pf[:, 4, 0:128]),
                              deps=[t_tr, pe.get("ctm%d" % ch)])
                pe[("pf", 4)] = t_ctm
                t_tr2 = sch.g("tensor",
                              lambda e, cs=cs: e.transpose(pf[:, 4, 128:256], tbl2[:, cs],
                                                           identf[:]),
                              deps=[t_g2, t_ctm, pe.get(("pf", 4, "b"))])
                t_ctm2 = sch.g("vector",
                               lambda e, ch=ch: e.tensor_copy(ctm[:, ch, 112:128],
                                                              pf[:, 4, 128:144]),
                               deps=[t_tr2])
                pe[("pf", 4, "b")] = t_ctm2
                ctm_toks.append(t_ctm2)
            pe["tbl1"] = ctm_toks[-1]
            pe["tbl2"] = ctm_toks[-1]
            pe["lnab"] = ctm_toks[-1]

            if KN_STAGE < 3:
                continue
            ofm_toks = []
            for ch in range(NCH):
                cs = sl(ch, C)
                tctm = ctm_toks[ch]
                for bb in range(2):
                    heads = list(range(bb * HB, (bb + 1) * HB))
                    t_crc = sch.g(
                        "vector",
                        lambda e, ch=ch, bb=bb: e.tensor_copy(
                            crs_scr[:].rearrange("p (g r) -> p g r", g=2),
                            ctm[:, ch, :].rearrange("p (g r) -> p g r", g=4)
                            [:, 0:2, sl(bb, 8)]),
                        deps=[tctm, pe.get("crs_scr")])
                    t_cr = sch.g(
                        "tensor",
                        lambda e: e.transpose(pf[0:16, 4, 256:384], crs_scr[:],
                                              identf[:]),
                        deps=[t_crc, pe.get(("pf", 4, "cr"))])
                    pe["crs_scr"] = t_cr
                    t_crv = sch.g("vector",
                                  lambda e: e.tensor_copy(crows[:], pf[0:16, 4, 256:384]),
                                  deps=[t_cr, pe.get("crows")])
                    pe[("pf", 4, "cr")] = t_crv

                    if (KN_DBG and int(os.environ.get('KN_DBGSEL', 0)) == 3
                            and st == NST - 1 and bb == 1
                            and ch == int(os.environ.get('KN_DBGCH', NCH - 1))):
                        t_dbg3 = sch.g(
                            "vector",
                            (lambda e: e.tensor_copy(ypre[0:16, 0, 0:128], crows[:])),
                            (lambda e, ch=ch: e.tensor_copy(ypre[:, 1, 0:128],
                                                            ctm[:, ch, :])),
                            (lambda e, ch=ch: e.tensor_copy(ypre[:, 2, 0:128],
                                                            tbl1[:, sl(ch, C)])),
                            (lambda e, ch=ch: e.tensor_copy(ypre[0:64, 3, 0:128],
                                                            lnab[:, sl(ch, C)])),
                            deps=[t_crv, tctm, pe.get("ypre")])
                        t_dbg3d = sch.dma("gpsimd",
                                          lambda e: e.dma_start(
                                              out=dbg_d,
                                              in_=ypre[:, 0:8, :]
                                              .rearrange("p a n -> p (a n)")),
                                          deps=[t_dbg3])
                        pe["ypre"] = t_dbg3d
                    if KN_DN4 < 2:
                        ofm_toks.append(t_crv)
                        continue
                    sk_mms, sk_meta = [], []
                    for par in range(2):
                        for jj, h in enumerate(heads):
                            if h % 2 != par:
                                continue
                            ti = h // 2
                            rs = slice(par * 64, par * 64 + 64)
                            bank, qtr = jj // 4, jj % 4
                            sk_mms.append(lambda e, ti=ti, rs=rs, bank=bank, qtr=qtr, cs=cs:
                                          e.matmul(pf[:, bank, sl(qtr, 128)],
                                                   kc_[rs, ti, cs], kc_[rs, ti, cs],
                                                   start=True, stop=True,
                                                   tile_position=(rs.start, 0)))
                            sk_meta.append((par * 64, 64))
                            sk_mms.append(lambda e, ti=ti, rs=rs, bank=bank, qtr=qtr, cs=cs:
                                          e.matmul(pf[:, 2 + bank, sl(qtr, 128)],
                                                   kc_[rs, ti, cs], qc[rs, ti, cs],
                                                   start=True, stop=True,
                                                   tile_position=(rs.start, 0)))
                            sk_meta.append((par * 64, 64))
                    t_skk = sch.g("tensor", *sk_mms,
                                  deps=tqt + tkt + [pe.get(("pf", 0)), pe.get(("pf", 1)),
                                                    pe.get(("pf", 2)), pe.get(("pf", 3))],
                                  mk=sk_meta)
                    t_skq = t_skk

                    if KN_DN4 < 3:
                        ofm_toks.append(t_skk)
                        pe[("pf", 0)] = t_skk
                        pe[("pf", 2)] = t_skk
                        continue
                    tr_mms = []
                    for pi in range(4):
                        ti = bb * 4 + pi
                        tr_mms.append(lambda e, ti=ti, pi=pi, cs=cs: e.transpose(
                            pb[:, 0, sl(pi, 128)], kc_[:, ti, cs], identb[:]))
                        tr_mms.append(lambda e, ti=ti, pi=pi, cs=cs: e.transpose(
                            pb[:, 0, sl(4 + pi, 128)], vc[:, ti, cs], identb[:]))
                    t_tr = sch.g("tensor", *tr_mms,
                                 deps=tkt + tcv + [pe.get(("pb", 0)),
                                                   pe.get(("pb", 0, "s"))])

                    if KN_DN5 < 2:
                        ofm_toks.append(t_tr)
                        pe[("pb", 0)] = t_tr
                        pe[("pb", 0, "s")] = t_tr
                        continue
                    w0_ops, kcs_ops = [], []
                    for jj, h in enumerate(heads):
                        pi, par = jj // 2, jj % 2
                        koff = pi * 128 + par * 64
                        w0_ops.append(lambda e, jj=jj, koff=koff, ch=ch, h=h:
                                      e.tensor_scalar(
                                          out=wbuf[0][:, jj, 0:64],
                                          in0=pb[:, 0, 512 + koff:512 + koff + 64],
                                          scalar1=ctm[:, ch, 64 + h:65 + h],
                                          scalar2=None, op0=ALU.mult))
                        w0_ops.append(lambda e, jj=jj, koff=koff, ch=ch, h=h:
                                      e.tensor_scalar(
                                          out=wbuf[0][:, jj, 64:128],
                                          in0=pb[:, 0, koff:koff + 64],
                                          scalar1=ctm[:, ch, 96 + h:97 + h],
                                          scalar2=None, op0=ALU.mult))
                        kcs_ops.append(lambda e, jj=jj, koff=koff, ch=ch, h=h:
                                       e.tensor_scalar(
                                           out=kcs[:, jj, :],
                                           in0=pb[:, 0, koff:koff + 64],
                                           scalar1=ctm[:, ch, 112 + h:113 + h],
                                           scalar2=None, op0=ALU.mult))
                    t_w0 = sch.g("vector", *w0_ops, deps=[t_tr, pe.get("wbuf0")])
                    if KN_DN5 < 3:
                        t_kcs = t_w0
                    else:
                        t_kcs = sch.g("vector", *kcs_ops, deps=[t_tr, pe.get("kcs")])
                    pe[("pb", 0)] = t_w0
                    pe[("pb", 0, "s")] = t_kcs

                    if KN_DN4 < 4:
                        ofm_toks.append(t_kcs)
                        pe[("pf", 0)] = t_skk
                        pe[("pf", 2)] = t_skq
                        continue
                    def selq(jj):
                        return pf[:, 4 + jj // 4, sl(jj % 4, 128)]

                    t_selc = sch.g(
                        "tensor",
                        *[(lambda e, jj=jj: e.matmul(selq(jj),
                                                     sel16f[:, sl(2 * jj, 128)],
                                                     crows[:], start=True, stop=True))
                          for jj in range(HB)],
                        deps=[t_crv, pe.get(("pf", 4)), pe.get(("pf", 5)),
                              pe.get(("pf", 4, "s")), pe.get(("pf", 5, "s"))],
                        mk=[(0, 16)] * HB)
                    t_gx = sch.g(
                        "scalar",
                        (lambda e: e.activation(
                            gexp[:].rearrange("p a n -> p (a n)"),
                            pf[:, 4:6, :].rearrange("p a n -> p (a n)"), AF.Exp)),
                        (lambda e: e.activation(
                            gcf[:].rearrange("p a n -> p (a n)"),
                            pf[:, 4:6, :].rearrange("p a n -> p (a n)")
                            .rearrange("p (a n) -> p a n", a=8)[:, :, 127:128]
                            .rearrange("p a n -> p (a n)"), AF.Exp)),
                        deps=[t_selc, pe.get("gexp")])
                    if bb == 0:
                        t_negc = sch.g(
                            "vector",
                            lambda e, ch=ch: e.tensor_scalar(
                                out=argf8[:, 0, 0:16], in0=ctm[:, ch, 0:16],
                                scalar1=-1.0, scalar2=None, op0=ALU.mult),
                            deps=[tctm, pe.get("argf8")])
                        pe["negc"] = t_negc
                    t_am = sch.g(
                        "scalar",
                        *[(lambda e, jj=jj, ch=ch, h=bb * HB + jj: e.activation(
                            awvb[:, jj, :], selq(jj), AF.Exp,
                            bias=ctm[:, ch, 32 + h:33 + h], scale=-1.0))
                          for jj in range(HB)],
                        deps=[t_selc, t_gx, pe.get("awvb")])
                    t_msk1 = sch.g(
                        "gpsimd",
                        *[(lambda e, jj=jj: e.tensor_mul(awvb[:, jj, :],
                                                         awvb[:, jj, :], maskMb[:]))
                          for jj in range(HB)],
                        deps=[t_am, pe.get("mskf")])
                    t_mm1g = sch.g(
                        "vector",
                        *[(lambda e, jj=jj: e.tensor_mul(
                            m_m[:, jj, :], awvb[:, jj, :],
                            pf[:, jj // 4, sl(jj % 4, 128)]))
                          for jj in range(HB)],
                        deps=[t_msk1, t_skk, pe.get("m_m")])
                    t_aa = sch.g(
                        "scalar",
                        *[(lambda e, jj=jj, h=bb * HB + jj: e.activation(
                            gb_[:, jj, :], selq(jj), AF.Exp,
                            bias=argf8[:, 0, h:h + 1], scale=1.0))
                          for jj in range(HB)],
                        deps=[t_am, pe.get("negc"), pe.get("gb_f")])
                    t_msk3 = sch.g(
                        "gpsimd",
                        *[(lambda e, jj=jj: e.tensor_mul(gb_[:, jj, :],
                                                         gb_[:, jj, :], maskATb[:]))
                          for jj in range(HB)],
                        deps=[t_aa])
                    t_mm3g = sch.g(
                        "vector",
                        *[(lambda e, jj=jj: e.tensor_mul(
                            m_at[:, jj, :], gb_[:, jj, :],
                            pf[:, 2 + jj // 4, sl(jj % 4, 128)]))
                          for jj in range(HB)],
                        deps=[t_msk3, t_skk, pe.get("m_at")])
                    t_sell = sch.g(
                        "tensor",
                        *[(lambda e, jj=jj: e.matmul(selq(jj),
                                                     sel16f[:, sl(2 * jj + 1, 128)],
                                                     crows[:], start=True, stop=True))
                          for jj in range(HB)],
                        deps=[t_crv, t_aa, t_gx], mk=[(0, 16)] * HB)
                    t_amt = sch.g(
                        "scalar",
                        *[(lambda e, jj=jj, h=bb * HB + jj: e.activation(
                            gtb_[:, jj, :], selq(jj), AF.Exp,
                            bias=argf8[:, 0, h:h + 1], scale=1.0))
                          for jj in range(HB)],
                        deps=[t_sell, pe.get("negc"), pe.get("gtb_f")])
                    t_msk2 = sch.g(
                        "gpsimd",
                        *[(lambda e, jj=jj: e.tensor_mul(gtb_[:, jj, :],
                                                         gtb_[:, jj, :], maskMTb[:]))
                          for jj in range(HB)],
                        deps=[t_amt])
                    t_mm2g = sch.g(
                        "vector",
                        *[(lambda e, jj=jj: e.tensor_mul(
                            m_mt[:, jj, :], gtb_[:, jj, :],
                            pf[:, jj // 4, sl(jj % 4, 128)]))
                          for jj in range(HB)],
                        deps=[t_msk2, pe.get("m_mt")])
                    arg_last = t_mm2g
                    arg_last_s = t_amt
                    pe[("pf", 4)] = t_amt
                    pe[("pf", 5)] = t_amt
                    pe[("pf", 4, "s")] = t_amt
                    pe[("pf", 5, "s")] = t_amt
                    pe["argf8"] = t_amt
                    pe["awvb"] = t_mm1g
                    pe["mskf"] = t_mm2g
                    pe["gb_f"] = t_mm3g
                    pe["gtb_f"] = t_mm2g
                    pe["m_m"] = None
                    pe["m_mt"] = None
                    pe["m_at"] = None
                    pe["gexp"] = None
                    for b_ in range(4):
                        pe[("pf", b_)] = arg_last
                    pe["gexp"] = arg_last
                    pe["crows"] = arg_last

                    dbg_gate = None
                    if (KN_DBG and int(os.environ.get('KN_DBGSEL', 0)) in (1, 2)
                            and st == NST - 1 and bb == 1
                            and ch == int(os.environ.get('KN_DBGCH', NCH - 1))):
                        _dsel = int(os.environ.get('KN_DBGSEL', 0))
                        if _dsel == 1:
                            _src = [m_m, m_mt]
                        else:
                            _src = [m_at, wbuf[0]]
                        t_dbga = sch.g(
                            "vector",
                            (lambda e, _src=_src: e.tensor_copy(
                                ypre[:, 0:4, :].rearrange("p a n -> p (a n)"),
                                _src[0][:].rearrange("p a n -> p (a n)"))),
                            (lambda e, _src=_src: e.tensor_copy(
                                ypre[:, 4:8, :].rearrange("p a n -> p (a n)"),
                                _src[1][:].rearrange("p a n -> p (a n)"))),
                            deps=[arg_last, arg_last_s, t_w0, pe.get("ypre")])
                        t_dbgad = sch.dma("gpsimd",
                                          lambda e: e.dma_start(
                                              out=dbg_d,
                                              in_=ypre[:, 0:8, :]
                                              .rearrange("p a n -> p (a n)")),
                                          deps=[t_dbga])
                        pe["ypre"] = t_dbgad
                        dbg_gate = t_dbga
                    if KN_DN3 < 2:
                        ofm_toks.append(arg_last)
                        continue
                    cw = 0
                    t_wcur = t_w0
                    g_cur = (m_m, m_mt)
                    t_gdep = [arg_last]
                    t_app_dep = [arg_last, t_kcs, arg_last_s] + ([dbg_gate] if dbg_gate else [])
                    t_gn_dep = [arg_last, arg_last_s]
                    t_gtn_dep = [arg_last, arg_last_s]
                    for ks in range(KN_SOLVE):
                        gsrc, gtsrc = g_cur
                        t_app = sch.g("tensor",
                                      *[(lambda e, jj=jj, gtsrc=gtsrc, cw=cw: e.matmul(
                                          pf[:, jj // 4, sl(jj % 4, 128)],
                                          gtsrc[:, jj, :],
                                          wbuf[cw][:, jj, :], start=True, stop=True))
                                        for jj in range(HB)],
                                      deps=[t_wcur] + t_gdep + t_app_dep)
                        t_app_dep = []
                        opa = ALU.subtract if ks == 0 else ALU.add
                        t_wn = sch.g("vector",
                                     lambda e, cw=cw, opa=opa: e.tensor_tensor(
                                         out=wbuf[1 - cw].rearrange("p a n -> p (a n)"),
                                         in0=wbuf[cw].rearrange("p a n -> p (a n)"),
                                         in1=pf[:, 0:2, :].rearrange("p a n -> p (a n)"),
                                         op=opa),
                                     deps=[t_app, pe.get("wbuf%d" % (1 - cw))])
                        pe["wbuf%d" % cw] = t_wn
                        cw = 1 - cw
                        t_wcur = t_wn
                        t_app_dep = [t_wn]
                        if ks < 6:
                            t_gn = sch.g("tensor",
                                         *[(lambda e, jj=jj, gsrc=gsrc, gtsrc=gtsrc:
                                            e.matmul(pf[:, 2 + jj // 4, sl(jj % 4, 128)],
                                                     gtsrc[:, jj, :],
                                                     gsrc[:, jj, :], start=True,
                                                     stop=True))
                                           for jj in range(HB)],
                                         deps=t_gdep + t_gn_dep)
                            t_gtn = sch.g("tensor",
                                          *[(lambda e, jj=jj, gsrc=gsrc, gtsrc=gtsrc:
                                             e.matmul(pf[:, 4 + jj // 4, sl(jj % 4, 128)],
                                                      gsrc[:, jj, :],
                                                      gtsrc[:, jj, :], start=True,
                                                      stop=True))
                                            for jj in range(HB)],
                                          deps=t_gtn_dep)
                            t_gne = sch.g("vector",
                                          lambda e: e.tensor_copy(
                                              gbuf[0].rearrange("p a n -> p (a n)"),
                                              pf[:, 2:4, :].rearrange("p a n -> p (a n)")),
                                          deps=[t_gn, t_gtn])
                            t_gtne = sch.g("scalar",
                                           lambda e: e.copy(
                                               gtbuf[0].rearrange("p a n -> p (a n)"),
                                               pf[:, 4:6, :].rearrange("p a n -> p (a n)")),
                                           deps=[t_gtn, t_gn])
                            g_cur = (gbuf[0], gtbuf[0])
                            t_gdep = [t_gne, t_gtne]
                            t_gn_dep = [t_gne]
                            t_gtn_dep = [t_gtne]
                    t_wfin = t_wcur
                    if KN_SOLVE == 0:
                        t_gne = t_w0
                        t_gtne = t_w0
                    pe[("pf", 2)] = t_gne
                    pe[("pf", 3)] = t_gne
                    pe[("pf", 4)] = t_gtne
                    pe[("pf", 5)] = t_gtne
                    pe[("pf", 0)] = t_wfin
                    pe[("pf", 1)] = t_wfin

                    if KN_DN3 < 3:
                        ofm_toks.append(t_wfin)
                        continue
                    aw_mms = []
                    for jj, h in enumerate(heads):
                        par = h % 2
                        aw_mms.append(lambda e, jj=jj, par=par, cw=cw: e.matmul(
                            pf[slice(par * 64, par * 64 + 64), 2, sl(jj // 2, 128)],
                            wbuf[cw][:, jj, 0:64], m_at[:, jj, :],
                            start=True, stop=True, tile_position=(0, par * 64)))
                        aw_mms.append(lambda e, jj=jj, par=par, cw=cw: e.matmul(
                            pf[slice(par * 64, par * 64 + 64), 3, sl(jj // 2, 128)],
                            wbuf[cw][:, jj, 64:128], m_at[:, jj, :],
                            start=True, stop=True, tile_position=(0, par * 64)))
                    t_aw = sch.g("tensor", *aw_mms,
                                 deps=[t_wfin, arg_last, pe.get(("pf", 2)),
                                       pe.get(("pf", 3))])
                    awv_ops = []
                    zt_ops = []
                    for jj, h in enumerate(heads):
                        par = h % 2
                        prow = slice(par * 64, par * 64 + 64)
                        ti = h // 2
                        awv_ops.append(lambda e, jj=jj, prow=prow: e.copy(
                            awvb[prow, jj, :], pf[prow, 2, sl(jj // 2, 128)]))
                        zt_ops.append(lambda e, jj=jj, prow=prow, ti=ti, cs=cs:
                                      e.tensor_mul(ztb[prow, jj, :], qc[prow, ti, cs],
                                                   gexp[prow, jj, :]))
                        zt_ops.append(lambda e, jj=jj, prow=prow: e.tensor_sub(
                            ztb[prow, jj, :], ztb[prow, jj, :],
                            pf[prow, 3, sl(jj // 2, 128)]))
                    t_awv = sch.g("scalar", *awv_ops, deps=[t_aw, pe.get("awvb")])
                    t_zt = sch.g("vector", *zt_ops,
                                 deps=[t_aw, pe.get("ztb")] + tqt)
                    pe[("pf", 2)] = t_awv
                    pe[("pf", 3)] = t_zt

                    pv_mms, pkt_mms = [], []
                    for jj, h in enumerate(heads):
                        par = h % 2
                        pv_mms.append(lambda e, jj=jj, par=par, cw=cw: e.matmul(
                            pf[slice(par * 64, par * 64 + 64), 4, sl(jj // 2, 64)],
                            kcs[:, jj, :], wbuf[cw][:, jj, 0:64],
                            start=True, stop=True, tile_position=(0, par * 64)))
                        pkt_mms.append(lambda e, jj=jj, par=par, cw=cw: e.matmul(
                            pf[slice(par * 64, par * 64 + 64), 5, sl(jj // 2, 64)],
                            wbuf[cw][:, jj, 64:128], kcs[:, jj, :],
                            start=True, stop=True, tile_position=(0, par * 64)))
                    t_pv = sch.g("tensor", *pv_mms,
                                 deps=[t_wfin, t_kcs, pe.get(("pf", 4))] + t_gdep)
                    t_pkt = sch.g("tensor", *pkt_mms,
                                  deps=[t_wfin, t_kcs, pe.get(("pf", 5))] + t_gdep)
                    pv_ops, pkt_ops = [], []
                    for jj, h in enumerate(heads):
                        par = h % 2
                        prow = slice(par * 64, par * 64 + 64)
                        pv_ops.append(lambda e, jj=jj, prow=prow: e.tensor_copy(
                            pvf[prow, jj // 2, :], pf[prow, 4, sl(jj // 2, 64)]))
                        pkt_ops.append(lambda e, jj=jj, prow=prow: e.scalar_tensor_tensor(
                            out=pktb[prow, jj // 2, :], in0=id64f[prow, :],
                            scalar=gcf[prow, jj, 0:1], in1=pf[prow, 5, sl(jj // 2, 64)],
                            op0=ALU.mult, op1=ALU.subtract))
                    t_pve = sch.g("vector", *pv_ops, deps=[t_pv, pe.get("pvf")])
                    t_pkte = sch.g("vector", *pkt_ops, deps=[t_pkt, pe.get("pktb")])
                    if (KN_DBG and int(os.environ.get('KN_DBGSEL', 0)) == 0
                            and st == NST - 1 and bb == 1
                            and ch == int(os.environ.get('KN_DBGCH', NCH - 1))):
                        t_dbgc = sch.g(
                            "vector",
                            (lambda e: e.tensor_copy(
                                ypre[:, 0:2, :].rearrange("p a n -> p (a n)")[:, 0:256],
                                pvf[:].rearrange("p a n -> p (a n)"))),
                            (lambda e: e.tensor_copy(
                                ypre[:, 0:2, :].rearrange("p a n -> p (a n)")[:, 256:512],
                                pktb[:].rearrange("p a n -> p (a n)"))),
                            (lambda e: e.tensor_copy(
                                ypre[:, 2:6, :].rearrange("p a n -> p (a n)")[:, 0:1024],
                                wbuf[0][:].rearrange("p a n -> p (a n)"))),
                            (lambda e: e.tensor_copy(
                                ypre[:, 6:8, :].rearrange("p a n -> p (a n)")[:, 0:512],
                                kcs[:].rearrange("p a n -> p (a n)"))),
                            deps=[t_pve, t_pkte, t_wfin, t_kcs, pe.get("ypre")])
                        t_dbgcd = sch.dma("gpsimd",
                                          lambda e: e.dma_start(
                                              out=dbg_d,
                                              in_=ypre[:, 0:8, :]
                                              .rearrange("p a n -> p (a n)")),
                                          deps=[t_dbgc])
                        pe["ypre"] = t_dbgcd
                        pe["pvf"] = t_dbgc
                        pe["pktb"] = t_dbgc
                        pe["wbuf0"] = t_dbgc
                        pe["wbuf1"] = t_dbgc
                        pe["kcs"] = t_dbgc
                    pe[("pf", 4)] = t_pve
                    pe[("pf", 4, "cr")] = t_pve
                    pe[("pf", 4, "b")] = t_pve
                    pe[("pf", 5)] = t_pkte
                    pe[("pf", 4, "s")] = None
                    pe[("pf", 5, "s")] = None
                    pe["wbuf0"] = t_pkt
                    pe["wbuf1"] = t_pkt
                    pe["kcs"] = t_pkt
                    pe["m_m"] = t_aw
                    pe["m_mt"] = t_aw
                    pe["m_at"] = t_aw

                    if KN_DN3 < 4:
                        ofm_toks.append(t_pkte)
                        continue
                    y_mms, s_mms, ys_meta = [], [], []
                    for par in range(2):
                        for jj, h in enumerate(heads):
                            if h % 2 != par:
                                continue
                            prow = slice(par * 64, par * 64 + 64)
                            pi = h // 2
                            pj = jj // 2
                            y_mms.append(lambda e, prow=prow, pi=pi, jj=jj, pj=pj, par=par:
                                         e.matmul(pf[prow, 0, sl(pj, 128)],
                                                  stb[prow, pi, :], ztb[prow, jj, :],
                                                  start=True, stop=True,
                                                  tile_position=(par * 64, par * 64)))
                            s_mms.append(lambda e, prow=prow, pi=pi, jj=jj, pj=pj, par=par:
                                         e.matmul(pf[prow, 1, sl(pj, 64)],
                                                  pktb[prow, pj, :], stb[prow, pi, :],
                                                  start=True, stop=True,
                                                  tile_position=(par * 64, par * 64)))
                            ys_meta.append((par * 64, 64))
                    t_ymm = sch.g("tensor", *y_mms,
                                  deps=[t_zt, pe.get(("pf", 0)), pe.get("stb_r")],
                                  mk=ys_meta)
                    t_smm = sch.g("tensor", *s_mms,
                                  deps=[t_pkte, pe.get(("pf", 1)), t_ymm], mk=ys_meta)
                    o_ops, st_ops, stb_ops = [], [], []
                    for jj, h in enumerate(heads):
                        par = h % 2
                        prow = slice(par * 64, par * 64 + 64)
                        pi = h // 2
                        pj = jj // 2
                        o_ops.append(lambda e, prow=prow, pi=pi, jj=jj, pj=pj, cs=cs:
                                     e.tensor_add(ofm[prow, pi, cs],
                                                  pf[prow, 0, sl(pj, 128)],
                                                  awvb[prow, jj, :]))
                        st_ops.append(lambda e, prow=prow, pi=pi, pj=pj:
                                      e.tensor_add(stf[prow, pi, :],
                                                   pf[prow, 1, sl(pj, 64)],
                                                   pvf[prow, pj, :]))
                        stb_ops.append(lambda e, prow=prow, pi=pi:
                                       e.tensor_copy(stb[prow, pi, :], stf[prow, pi, :]))
                    t_oe = sch.g("vector", *o_ops, deps=[t_ymm, t_awv, pe.get("ofm")])
                    t_ste = sch.g("vector", *st_ops, deps=[t_smm, t_pve])
                    t_stb = sch.g("vector", *stb_ops, deps=[t_ste, pe.get("stb_r2")])
                    pe[("pf", 0)] = t_oe
                    pe[("pf", 1)] = t_ste
                    pe["stb_r"] = t_stb
                    pe["ztb"] = t_ymm
                    pe["pvf"] = t_ste
                    pe["pktb"] = t_smm
                    pe["awvb"] = t_oe
                    ofm_toks.append(t_oe)
                pe["ctm%d" % ch] = ofm_toks[-1]

            if KN_STAGE < 4:
                continue
            t_osq = sch.g("scalar",
                          *[(lambda e, i=i: e.activation(sq1[:, i, :], ofm[:, i, :],
                                                         AF.Square))
                            for i in range(8)],
                          deps=ofm_toks + [pe.get("sq1")])
            t_omm = sch.g("tensor",
                          *[(lambda e, i=i: e.matmul(pf[0:1, 4, 0:T1], onescb[:],
                                                     sq1[:, i, :],
                                                     start=(i == 0), stop=(i == 7)))
                            for i in range(8)],
                          deps=[t_osq, pe.get(("pf", 4))])
            pe["sq1"] = t_omm
            t_os = sch.g("vector",
                         lambda e: e.tensor_scalar(out=r1a[:], in0=pf[0:1, 4, 0:T1],
                                                   scalar1=1.0 / D, scalar2=1e-6,
                                                   op0=ALU.mult, op1=ALU.add),
                         deps=[t_omm, pe.get("r1a")])
            pe[("pf", 4)] = t_os
            t_ol = sch.g("scalar",
                         lambda e: e.activation(r1b[:], r1a[:], AF.Ln),
                         lambda e: e.activation(r1a[:], r1b[:], AF.Exp, scale=-0.5),
                         deps=[t_os, pe.get("r1b")])
            t_oexp = sch.g("tensor",
                           lambda e: e.matmul(pf[:, 5, 0:T1], onesrf[:], r1a[:],
                                              start=True, stop=True),
                           deps=[t_ol, pe.get(("pf", 5))], mk=[(0, 1)])
            t_on = sch.g("vector",
                         *[(lambda e, i=i: e.tensor_mul(ofm[:, i, :], ofm[:, i, :],
                                                        pf[:, 5, 0:T1]))
                           for i in range(8)],
                         deps=[t_oexp] + ofm_toks)
            pe[("pf", 5)] = t_on
            pe["r1a"] = t_oexp

            yp_toks = []
            for oc in range(8):
                bank = oc % 4
                mms = [(lambda e, oc=oc, kk=kk, bank=bank, st=st:
                        e.matmul(pf[:, bank, 0:T1],
                                 wsb[wsbuf_of(st, 2)][:, kk, sl(oc, 128)],
                                 ofm[:, kk, :], start=(kk == 0), stop=(kk == 7)))
                       for kk in range(8)]
                t_mm = sch.g("tensor", *mms,
                             deps=[t_on, ws_tok[(st, "wo")], pe.get(("pf", bank))])
                t_ev = sch.g("vector",
                             lambda e, oc=oc, bank=bank, cur=cur:
                             e.tensor_add(ypre[:, oc, :], pf[:, bank, 0:T1],
                                          xf[cur][:, oc, :]),
                             deps=[t_mm, tx, pe.get("ypre")])
                pe[("pf", bank)] = t_ev
                yp_toks.append(t_ev)
            pe[("xf", cur)] = yp_toks[-1]
            if st + 1 < NST:
                ws_tok[(st + 1, "rk")] = load_ws("rk", wsbuf_of(st + 1, 1), yp_toks)

            t_ypb = sch.g("scalar",
                          *[(lambda e, i=i: e.copy(ypb[:, i, :], ypre[:, i, :]))
                            for i in range(8)],
                          deps=yp_toks + [pe.get("ypb")])
            t_s1 = sch.g("tensor",
                         *[(lambda e, i=i: e.matmul(pf[0:1, 4, 0:T1], onescb[:],
                                                    ypb[:, i, :],
                                                    start=(i == 0), stop=(i == 7)))
                           for i in range(8)],
                         deps=[t_ypb, pe.get(("pf", 4))])
            t_mu = sch.g("vector",
                         lambda e: e.tensor_scalar(out=r1a[:], in0=pf[0:1, 4, 0:T1],
                                                   scalar1=1.0 / D, scalar2=None,
                                                   op0=ALU.mult),
                         deps=[t_s1, pe.get("r1a")])
            pe[("pf", 4)] = t_mu
            t_sq2 = sch.g("scalar",
                          *[(lambda e, i=i: e.activation(sq1[:, i, :], ypb[:, i, :],
                                                         AF.Square))
                            for i in range(8)],
                          deps=[t_ypb, pe.get("sq1")])
            t_s2 = sch.g("tensor",
                         *[(lambda e, i=i: e.matmul(pf[0:1, 4, 256:256 + T1], onescb[:],
                                                    sq1[:, i, :],
                                                    start=(i == 0), stop=(i == 7)))
                           for i in range(8)],
                         deps=[t_sq2, t_mu])
            pe["sq1"] = t_s2
            t_var = sch.g("vector",
                          (lambda e: e.tensor_scalar(out=r1b[:], in0=pf[0:1, 4, 256:256 + T1],
                                                     scalar1=1.0 / D, scalar2=None,
                                                     op0=ALU.mult)),
                          (lambda e: e.tensor_mul(pf[0:1, 4, 256:256 + T1], r1a[:], r1a[:])),
                          (lambda e: e.tensor_sub(r1b[:], r1b[:], pf[0:1, 4, 256:256 + T1])),
                          (lambda e: e.tensor_scalar(out=r1b[:], in0=r1b[:], scalar1=1e-5,
                                                     scalar2=None, op0=ALU.add)),
                          deps=[t_s2, pe.get("r1b")])
            pe[("pf", 4)] = t_var
            t_inv = sch.g("scalar",
                          lambda e: e.activation(r1b[:], r1b[:], AF.Ln),
                          lambda e: e.activation(r1b[:], r1b[:], AF.Exp, scale=-0.5),
                          deps=[t_var])
            t_mexp = sch.g("tensor",
                           lambda e: e.matmul(pf[:, 4, 0:T1], onesrf[:], r1a[:],
                                              start=True, stop=True),
                           deps=[t_inv, t_var], mk=[(0, 1)])
            t_iexp = sch.g("tensor",
                           lambda e: e.matmul(pf[:, 5, 0:T1], onesrf[:], r1b[:],
                                              start=True, stop=True),
                           deps=[t_inv, pe.get(("pf", 5))], mk=[(0, 1)])
            pe["r1a"] = t_mexp
            pe["r1b"] = t_iexp
            y1_ops = []
            for i in range(8):
                y1_ops.append(lambda e, i=i: e.tensor_sub(y1f[:, i, :], ypre[:, i, :],
                                                          pf[:, 4, 0:T1]))
                y1_ops.append(lambda e, i=i: e.tensor_mul(y1f[:, i, :], y1f[:, i, :],
                                                          pf[:, 5, 0:T1]))
            t_y1 = sch.g("vector", *y1_ops,
                         deps=[t_mexp, t_iexp] + yp_toks + [pe.get("y1f")])
            pe[("pf", 4)] = t_y1
            pe[("pf", 5)] = t_y1
            pe["ypre"] = t_y1
            pe["ypb"] = t_s2
            pe["ofm"] = t_mu
            pe["ofm"] = yp_toks[-1]
            t_y1d = sch.dma("gpsimd",
                            lambda e, st=st: e.dma_start(
                                out=y1s.rearrange("(a p) s -> p a s", p=128)[:, :, sl(st, T1)],
                                in_=y1f),
                            deps=[t_y1])
            pe["y1f"] = t_y1d
            pe["ypre"] = t_y1d
            y1_toks.append(t_y1d)
            pe["qcF"] = t_skq if False else None
            pe["qcF"] = t_y1
            pe["kcF"] = t_y1
            pe["vcF"] = t_y1
            pe["rqF"] = tqt[-1]
            pe["rkF"] = tks_[-1]
            if DEBUG and st == 0:
                sch.dma("gpsimd",
                        lambda e: e.dma_start(out=dbg["d_qt"].rearrange(
                            "(a p) n -> p (a n)", p=128),
                            in_=qc[:].rearrange("p a n -> p (a n)")),
                        lambda e: e.dma_start(out=dbg["d_kt"].rearrange(
                            "(a p) n -> p (a n)", p=128),
                            in_=kc_[:].rearrange("p a n -> p (a n)")),
                        lambda e: e.dma_start(out=dbg["d_vc"].rearrange(
                            "(a p) n -> p (a n)", p=128),
                            in_=vc[:].rearrange("p a n -> p (a n)")),
                        lambda e: e.dma_start(out=dbg["d_ctm"], in_=ctm[:, 0, :]),
                        lambda e: e.dma_start(out=dbg["d_O"].rearrange(
                            "(a p) n -> p (a n)", p=128),
                            in_=ofm[:].rearrange("p a n -> p (a n)")),
                        lambda e: e.dma_start(out=dbg["d_y1"].rearrange(
                            "(a p) n -> p (a n)", p=128),
                            in_=y1f[:].rearrange("p a n -> p (a n)")),
                        deps=[t_y1])

        t_q_last = None
        if not KN_P15:
            sch.dma("gpsimd",
                    lambda e: e.dma_start(
                        out=outT.rearrange("(a p) s -> p a s", p=128)[:, :, 0:T1],
                        in_=y1f),
                    deps=y1_toks[-1:] if y1_toks else [])
        t_wz = sch.g("vector", lambda e: e.memset(wsum[:], 0.0), deps=[])
        t_sum_last = t_wz
        red_tok = {}
        s_units = []
        for i in range(8):
            for hlf in range(4):
                s_units.append(("w1", i, hlf))
        for j in range(32):
            s_units.append(("w2", j, 0))
        for u, (kind, i, hlf) in enumerate(s_units if KN_P15 else []):
            buf = stAB[u % 2]
            if kind == "w1":
                src = w1T_d[sl(i, 128), sl(hlf, 1024)]
                tcol, acol = 10, i
            else:
                src = w2T_d[sl(i, 128), :]
                tcol, acol = 9, 8
            t_d = sch.dma(
                "sync",
                lambda e, buf=buf, src=src: e.dma_start(out=buf[:], in_=src),
                deps=[red_tok[u - 2]] if u >= 2 else
                ([t_swo] if u == 0 else [yp_toks[-1]]))
            t_r = sch.g("vector",
                        lambda e, buf=buf, tcol=tcol: e.tensor_reduce(
                            out=wsum[:, tcol:tcol + 1], in_=buf[:],
                            axis=AXX, op=ALU.add, apply_absolute_value=True),
                        deps=[t_d, t_wz])
            red_tok[u] = sch.g("vector",
                               lambda e, tcol=tcol, acol=acol: e.tensor_add(
                                   wsum[:, acol:acol + 1], wsum[:, acol:acol + 1],
                                   wsum[:, tcol:tcol + 1]),
                               deps=[t_r])
            t_sum_last = red_tok[u]
        t_tot = sch.g("tensor",
                      (lambda e: e.matmul(pf[0:1, 4, 0:12], onescf[:], wsum[:],
                                          start=True, stop=True)),
                      deps=[t_sum_last, pe.get(("pf", 4))])
        t_sc = sch.g("vector",
                     (lambda e: e.tensor_reduce(out=pf[0:1, 4, 16:17], in_=pf[0:1, 4, 0:8],
                                                axis=AXX, op=ALU.add)),
                     (lambda e: e.tensor_scalar(out=pf[0:1, 4, 17:18], in0=pf[0:1, 4, 16:17],
                                                scalar1=1.0 / (D * DFF), scalar2=1e-5,
                                                op0=ALU.mult, op1=ALU.max)),
                     (lambda e: e.tensor_scalar(out=pf[0:1, 4, 18:19], in0=pf[0:1, 4, 8:9],
                                                scalar1=1.0 / (D * DFF), scalar2=1e-5,
                                                op0=ALU.mult, op1=ALU.max)),
                     deps=[t_tot])
        t_scv = sch.g("vector",
                      lambda e: e.tensor_copy(sval[0:1, 6:8], pf[0:1, 4, 17:19]),
                      deps=[t_sc])
        t_bc = sch.g("tensor",
                     lambda e: e.matmul(pf[:, 5, 0:2], onesrf[:], sval[0:1, 6:8],
                                        start=True, stop=True),
                     deps=[t_scv, pe.get(("pf", 5))], mk=[(0, 1)])
        t_sv = sch.g("vector",
                     (lambda e: e.tensor_copy(sval[:, 4:6], pf[:, 5, 0:2])),
                     (lambda e: e.reciprocal(sval[:, 0:1], sval[:, 4:5])),
                     (lambda e: e.reciprocal(sval[:, 1:2], sval[:, 5:6])),
                     (lambda e: e.tensor_scalar(out=sval[:, 2:3], in0=sval[:, 4:5],
                                                scalar1=1.0 / 127.0, scalar2=None,
                                                op0=ALU.mult)),
                     (lambda e: e.tensor_scalar(out=sval[:, 3:4], in0=sval[:, 5:6],
                                                scalar1=1.0 / 127.0, scalar2=None,
                                                op0=ALU.mult)),
                     deps=[t_bc])
        pe[("pf", 4)] = t_sv
        pe[("pf", 5)] = t_sv

        def quant_chunk(dst_ap, src_ap, invcol, clip_lo, clip_hi, deps):
            return sch.g(
                "vector",
                (lambda e: e.tensor_scalar(out=qs_g[:, 0:src_ap.shape[-1]], in0=src_ap,
                                           scalar1=invcol, scalar2=0.5,
                                           op0=ALU.mult, op1=ALU.add)),
                (lambda e: e.tensor_copy(qs_i[:, 0:src_ap.shape[-1]],
                                         qs_g[:, 0:src_ap.shape[-1]])),
                (lambda e: e.tensor_copy(qs_f[:, 0:src_ap.shape[-1]],
                                         qs_i[:, 0:src_ap.shape[-1]])),
                (lambda e: e.tensor_tensor(out=qs_g[:, 0:src_ap.shape[-1]],
                                           in0=qs_f[:, 0:src_ap.shape[-1]],
                                           in1=qs_g[:, 0:src_ap.shape[-1]],
                                           op=ALU.is_gt)),
                (lambda e: e.tensor_sub(qs_f[:, 0:src_ap.shape[-1]],
                                        qs_f[:, 0:src_ap.shape[-1]],
                                        qs_g[:, 0:src_ap.shape[-1]])),
                (lambda e: e.tensor_scalar(out=dst_ap, in0=qs_f[:, 0:src_ap.shape[-1]],
                                           scalar1=float(clip_lo), scalar2=float(clip_hi),
                                           op0=ALU.max, op1=ALU.min)),
                deps=deps)

        t_q_last = t_sv
        qdone = {}
        NU1 = 32 if KN_P15 else 0
        for u in range(NU1):
            i, hlf = u // 4, u % 4
            dep0 = [qdone[u - 2]] if u >= 2 else \
                ([red_tok[62 + u]] if red_tok else [])
            t_d = sch.dma("sync",
                          lambda e, i=i, hlf=hlf, u=u: e.dma_start(
                              out=stAB[u % 2][:],
                              in_=w1T_d[sl(i, 128), sl(hlf, 1024)]),
                          deps=dep0)
            for s2_ in range(2):
                t_q_last = quant_chunk(
                    w1qb[:, i, hlf * 1024 + s2_ * 512:hlf * 1024 + s2_ * 512 + 512],
                    stAB[u % 2][:, sl(s2_, 512)],
                    sval[:, 0:1], -1, 1, [t_d, t_q_last])
            qdone[u] = t_q_last
        for j in range(32 if KN_P15 else 0):
            u = NU1 + j
            t_d = sch.dma("sync",
                          lambda e, j=j, u=u: e.dma_start(out=stAB[u % 2][:],
                                                          in_=w2T_d[sl(j, 128), :]),
                          deps=[qdone[u - 2]] if u >= 2 else [])
            for s2_ in range(2):
                t_q_last = quant_chunk(w2qc[j % 2][:, sl(s2_, 512)],
                                       stAB[u % 2][:, sl(s2_, 512)], sval[:, 1:2],
                                       -1, 1,
                                       [t_d, t_q_last]
                                       + ([pe.get(("w2st", j % 2))]
                                          if s2_ == 0 else []))
            qdone[u] = t_q_last
            t_wout = sch.dma("gpsimd",
                             lambda e, j=j: e.dma_start(out=w2qs[sl(j, 128), :],
                                                        in_=w2qc[j % 2][:, :]),
                             deps=[t_q_last])
            pe[("w2st", j % 2)] = t_wout

        y1r = y1s.rearrange("(a p) s -> p a s", p=128)

        def load_y(tt, deps):
            return sch.dma(
                "sync",
                (lambda e, tt=tt: e.dma_start(out=hy_pv,
                                              in_=y1r[:, 0, sl(tt, T2)])),
                (lambda e, tt=tt: e.dma_start(out=hy_st,
                                              in_=y1r[:, 1:3, sl(tt, T2)])),
                (lambda e, tt=tt: e.dma_start(out=hy_qg,
                                              in_=y1r[:, 3, sl(tt, T2)])),
                (lambda e, tt=tt: e.dma_start(out=hy_qf,
                                              in_=y1r[:, 4, sl(tt, T2)])),
                (lambda e, tt=tt: e.dma_start(out=hy_tl,
                                              in_=y1r[:, 5:8, sl(tt, T2)])),
                deps=deps)

        y_ld = None
        if NT2 > 0:
            y_ld = load_y(0, [t_q_last] + y1_toks[0:1]
                          + (tcv[5:] if NST > 0 else []))
        w2_tok = {}

        def w2l(i):
            return w2qc[i // 2][:, (i % 2) * 512:(i % 2) * 512 + 512]

        if NT2 > 0:
            w2_tok[0] = sch.dma("sync",
                                lambda e: e.dma_start(out=w2l(0),
                                                      in_=w2qs[0:128, 0:512]),
                                deps=[pe.get(("w2st", 0)), t_q_last])
            pe[("w2u", 1)] = pe.get(("w2st", 0))
            pe[("w2u", 2)] = pe.get(("w2st", 1))
            pe[("w2u", 3)] = pe.get(("w2st", 1))
        out_toks = []

        for tt in range(NT2):
            ty = y_ld

            t_ab = sch.g("scalar",
                         *[(lambda e, i=i: e.activation(
                             sq1[:, i, 0:T2], hyfc[i], AF.Abs))
                           for i in range(8)],
                         deps=[ty, pe.get("sq1")])
            mx_ops = [lambda e: e.tensor_max(tmax[:], sq1[:, 0, 0:T2], sq1[:, 1, 0:T2])]
            for i in range(2, 8):
                mx_ops.append(lambda e, i=i: e.tensor_max(tmax[:], tmax[:],
                                                          sq1[:, i, 0:T2]))
            t_mx = sch.g("vector", *mx_ops, deps=[t_ab, pe.get("tmax")])
            pe["sq1"] = t_mx
            t_mt = sch.g("tensor",
                         (lambda e: e.transpose(pf[:, 4, 0:128], tmax[:, 0:128],
                                                identf[:])),
                         (lambda e: e.transpose(pf[:, 4, 128:256], tmax[:, 128:256],
                                                identf[:])),
                         deps=[t_mx, pe.get(("pf", 4))])
            t_pm = sch.g("vector",
                         (lambda e: e.tensor_reduce(out=p2sc[:, 0:1],
                                                    in_=pf[:, 4, 0:128],
                                                    axis=AXX, op=ALU.max)),
                         (lambda e: e.tensor_reduce(out=p2sc[:, 1:2],
                                                    in_=pf[:, 4, 128:256],
                                                    axis=AXX, op=ALU.max)),
                         (lambda e: e.tensor_scalar(out=p2sc[:, 0:2], in0=p2sc[:, 0:2],
                                                    scalar1=1e-5, scalar2=None,
                                                    op0=ALU.max)),
                         (lambda e: e.reciprocal(p2sc[:, 2:4], p2sc[:, 0:2])),
                         (lambda e: e.tensor_scalar(out=p2sc[:, 2:4], in0=p2sc[:, 2:4],
                                                    scalar1=127.0, scalar2=None,
                                                    op0=ALU.mult)),
                         (lambda e: e.tensor_scalar(out=p2sc[:, 0:2], in0=p2sc[:, 0:2],
                                                    scalar1=sval[:, 2:3], scalar2=None,
                                                    op0=ALU.mult)),
                         deps=[t_mt, pe.get("p2sc")])
            pe[("pf", 4)] = t_pm
            t_rt = sch.g("tensor",
                         lambda e: e.transpose(pf[0:4, 4, 256:384], p2sc[:, 0:4],
                                               identf[:]),
                         deps=[t_pm, pe.get(("pf", 4, "b"))])
            t_rv = sch.g("vector",
                         lambda e: e.tensor_copy(scrows[0:4, :], pf[0:4, 4, 256:384]),
                         deps=[t_rt, pe.get("scrows")])
            pe[("pf", 4, "b")] = t_rv
            t_ex = sch.g("tensor",
                         (lambda e: e.matmul(pf[:, 5, 0:128], sel16f[:, sl(4, 128)],
                                             scrows[:], start=True, stop=True)),
                         (lambda e: e.matmul(pf[:, 5, 128:256], sel16f[:, sl(6, 128)],
                                             scrows[:], start=True, stop=True)),
                         (lambda e: e.matmul(pf[:, 5, 256:384], sel16f[:, sl(0, 128)],
                                             scrows[:], start=True, stop=True)),
                         (lambda e: e.matmul(pf[:, 5, 384:512], sel16f[:, sl(2, 128)],
                                             scrows[:], start=True, stop=True)),
                         deps=[t_rv, pe.get(("pf", 5))], mk=[(0, 16)] * 4)
            t_exe = sch.g("vector",
                          (lambda e: e.tensor_copy(scexp[:, 0, :], pf[:, 5, 0:256])),
                          (lambda e: e.tensor_copy(scexp[:, 1, :],
                                                   pf[:, 5, 256:512])),
                          deps=[t_ex, pe.get("scexp")])
            pe[("pf", 5)] = t_exe

            n1_toks = []
            for i in range(8):
                t_n = sch.g(
                    "vector",
                    (lambda e, i=i: e.scalar_tensor_tensor(
                        out=qs_g[:, 0:T2], in0=hyfc[i], scalar=0.5,
                        in1=scexp[:, 0, :], op0=ALU.bypass, op1=ALU.mult)),
                    (lambda e: e.tensor_scalar(out=qs_g[:, 0:T2], in0=qs_g[:, 0:T2],
                                               scalar1=0.5, scalar2=None, op0=ALU.add)),
                    (lambda e: e.tensor_copy(qs_i[:, 0:T2], qs_g[:, 0:T2])),
                    (lambda e: e.tensor_copy(qs_f[:, 0:T2], qs_i[:, 0:T2])),
                    (lambda e: e.tensor_tensor(out=qs_g[:, 0:T2], in0=qs_f[:, 0:T2],
                                               in1=qs_g[:, 0:T2], op=ALU.is_gt)),
                    (lambda e: e.tensor_sub(qs_f[:, 0:T2], qs_f[:, 0:T2],
                                            qs_g[:, 0:T2])),
                    (lambda e, i=i: e.tensor_scalar(out=n1b[:, i, :], in0=qs_f[:, 0:T2],
                                                    scalar1=-128.0, scalar2=127.0,
                                                    op0=ALU.max, op1=ALU.min)),
                    deps=[t_exe, ty, pe.get("n1b")])
                n1_toks.append(t_n)

            h_toks = []
            for oc in range(32):
                bank = oc % 4
                po = ((oc // 4) % 2) * 256
                mms = [(lambda e, oc=oc, kk=kk, bank=bank, po=po:
                        e.matmul(pf[:, bank, po:po + T2], w1qb[:, kk, sl(oc, 128)],
                                 n1b[:, kk, :], start=(kk == 0), stop=(kk == 7)))
                       for kk in range(8)]
                t_mm = sch.g("tensor", *mms,
                             deps=n1_toks + [t_q_last, pe.get(("pf", bank, po))])
                pe["n1b"] = t_mm
                t_dq = sch.g("vector",
                             lambda e, oc=oc, bank=bank, po=po: e.tensor_mul(
                                 hf[:, oc, :], pf[:, bank, po:po + T2],
                                 scexp[:, 1, :]),
                             deps=[t_mm, pe.get("hf")]
                             + ([pe.get("y2f")] if oc == 0 else [])
                             + ([y1_toks[-1]] if oc == 20 and y1_toks else []))
                t_g = sch.g("scalar",
                            lambda e, oc=oc: e.activation(hf[:, oc, :], hf[:, oc, :],
                                                          AF.Gelu),
                            deps=[t_dq])
                pe[("pf", bank, po)] = t_g
                h_toks.append(t_g)

            t_mx2 = None
            for grp in range(4):
                t_ab2 = sch.g("scalar",
                              *[(lambda e, i=i, grp=grp: e.activation(
                                  sq1[:, i, 0:T2], hf[:, grp * 8 + i, :], AF.Abs))
                                for i in range(8)],
                              deps=h_toks[grp * 8:grp * 8 + 8]
                              + ([t_mx2] if t_mx2 else [pe.get("sq1")]))
                ops = []
                if grp == 0:
                    ops.append(lambda e: e.tensor_max(tmax[:], sq1[:, 0, 0:T2],
                                                      sq1[:, 1, 0:T2]))
                    rng = range(2, 8)
                else:
                    rng = range(0, 8)
                for i in rng:
                    ops.append(lambda e, i=i: e.tensor_max(tmax[:], tmax[:],
                                                           sq1[:, i, 0:T2]))
                t_mx2 = sch.g("vector", *ops,
                              deps=[t_ab2] + ([t_exe] if grp == 0 else []))
            pe["sq1"] = t_mx2
            pe["tmax"] = t_mx2
            t_mt2 = sch.g("tensor",
                          (lambda e: e.transpose(pf[:, 4, 0:128], tmax[:, 0:128],
                                                 identf[:])),
                          (lambda e: e.transpose(pf[:, 4, 128:256], tmax[:, 128:256],
                                                 identf[:])),
                          deps=[t_mx2, pe.get(("pf", 4))])
            t_pm2 = sch.g("vector",
                          (lambda e: e.tensor_reduce(out=p2sc[:, 4:5],
                                                     in_=pf[:, 4, 0:128],
                                                     axis=AXX, op=ALU.max)),
                          (lambda e: e.tensor_reduce(out=p2sc[:, 5:6],
                                                     in_=pf[:, 4, 128:256],
                                                     axis=AXX, op=ALU.max)),
                          (lambda e: e.tensor_scalar(out=p2sc[:, 4:6], in0=p2sc[:, 4:6],
                                                     scalar1=1e-5, scalar2=None,
                                                     op0=ALU.max)),
                          (lambda e: e.reciprocal(p2sc[:, 6:8], p2sc[:, 4:6])),
                          (lambda e: e.tensor_scalar(out=p2sc[:, 6:8], in0=p2sc[:, 6:8],
                                                     scalar1=127.0, scalar2=None,
                                                     op0=ALU.mult)),
                          (lambda e: e.tensor_scalar(out=p2sc[:, 4:6], in0=p2sc[:, 4:6],
                                                     scalar1=sval[:, 3:4], scalar2=None,
                                                     op0=ALU.mult)),
                          deps=[t_mt2, t_rv])
            pe[("pf", 4)] = t_pm2
            t_rt2 = sch.g("tensor",
                          lambda e: e.transpose(pf[0:4, 4, 256:384], p2sc[:, 4:8],
                                                identf[:]),
                          deps=[t_pm2, pe.get(("pf", 4, "b"))])
            t_rv2 = sch.g("vector",
                          lambda e: e.tensor_copy(scrows[0:4, :], pf[0:4, 4, 256:384]),
                          deps=[t_rt2, t_exe, t_ex])
            pe[("pf", 4, "b")] = t_rv2
            t_ex2 = sch.g("tensor",
                          (lambda e: e.matmul(pf[:, 5, 0:128], sel16f[:, sl(4, 128)],
                                              scrows[:], start=True, stop=True)),
                          (lambda e: e.matmul(pf[:, 5, 128:256], sel16f[:, sl(6, 128)],
                                              scrows[:], start=True, stop=True)),
                          (lambda e: e.matmul(pf[:, 5, 256:384], sel16f[:, sl(0, 128)],
                                              scrows[:], start=True, stop=True)),
                          (lambda e: e.matmul(pf[:, 5, 384:512], sel16f[:, sl(2, 128)],
                                              scrows[:], start=True, stop=True)),
                          deps=[t_rv2, pe.get(("pf", 5))], mk=[(0, 16)] * 4)
            t_exe2 = sch.g("vector",
                           (lambda e: e.tensor_copy(scexp[:, 2, :], pf[:, 5, 0:256])),
                           (lambda e: e.tensor_copy(scexp[:, 3, :],
                                                    pf[:, 5, 256:512])),
                           deps=[t_ex2, pe.get("scexp2")])
            pe[("pf", 5)] = t_exe2

            n2_toks = []
            for oc in range(32):
                t_n = sch.g(
                    "vector",
                    (lambda e, oc=oc: e.scalar_tensor_tensor(
                        out=qs_g[:, 0:T2], in0=hf[:, oc, :], scalar=0.5,
                        in1=scexp[:, 2, :], op0=ALU.bypass, op1=ALU.mult)),
                    (lambda e: e.tensor_scalar(out=qs_g[:, 0:T2], in0=qs_g[:, 0:T2],
                                               scalar1=0.5, scalar2=None, op0=ALU.add)),
                    (lambda e: e.tensor_copy(qs_i[:, 0:T2], qs_g[:, 0:T2])),
                    (lambda e: e.tensor_copy(qs_f[:, 0:T2], qs_i[:, 0:T2])),
                    (lambda e: e.tensor_tensor(out=qs_g[:, 0:T2], in0=qs_f[:, 0:T2],
                                               in1=qs_g[:, 0:T2], op=ALU.is_gt)),
                    (lambda e: e.tensor_sub(qs_f[:, 0:T2], qs_f[:, 0:T2],
                                            qs_g[:, 0:T2])),
                    (lambda e, oc=oc: e.tensor_scalar(out=n2b[:, oc, :],
                                                      in0=qs_f[:, 0:T2],
                                                      scalar1=-128.0, scalar2=127.0,
                                                      op0=ALU.max, op1=ALU.min)),
                    deps=[t_exe2, h_toks[oc], pe.get("n2b")])
                n2_toks.append(t_n)
            pe["hf"] = n2_toks[-1]

            y2_toks = []
            for half in range(2):
                for kk in range(32):
                    gidx = tt * 64 + half * 32 + kk
                    wtok = w2_tok[gidx]
                    wb = w2l(gidx % 4)
                    nxt = gidx + 1
                    if nxt < NT2 * 64:
                        nh = (nxt % 64) // 32
                        nk = nxt % 32
                        w2_tok[nxt] = sch.dma(
                            "sync" if nxt % 2 == 0 else "gpsimd",
                            lambda e, nxt=nxt, nh=nh, nk=nk: e.dma_start(
                                out=w2l(nxt % 4),
                                in_=w2qs[sl(nk, 128), nh * 512:nh * 512 + 512]),
                            deps=[pe.get(("w2u", nxt % 4))])
                    mms = [(lambda e, kk=kk, oc=oc, wb=wb: e.matmul(
                        pf[:, oc, 0:T2], wb[:, sl(oc, 128)],
                        n2b[:, kk, :], start=(kk == 0), stop=(kk == 31)))
                        for oc in range(4)]
                    t_mm2 = sch.g("tensor", *mms,
                                  deps=[wtok, n2_toks[kk]]
                                  + ([pe.get(("pf", b_, 0)) for b_ in range(4)]
                                     if kk == 0 else []))
                    pe[("w2u", gidx % 4)] = t_mm2
                for oc in range(4):
                    t_e2v = sch.g("vector",
                                  (lambda e, oc=oc, half=half: e.tensor_mul(
                                      y2f[:, half * 4 + oc, :], pf[:, oc, 0:T2],
                                      scexp[:, 3, :])),
                                  (lambda e, oc=oc, half=half: e.tensor_add(
                                      y2f[:, half * 4 + oc, :],
                                      y2f[:, half * 4 + oc, :],
                                      hyfc[half * 4 + oc])),
                                  deps=[t_mm2, t_exe2, pe.get("y2f")])
                    pe[("pf", oc, 0)] = t_e2v
                    y2_toks.append(t_e2v)
            if tt + 1 < NT2:
                y_ld = load_y(tt + 1,
                              [y2_toks[-1],
                               y1_toks[min(tt + 1, len(y1_toks) - 1)]])

            t_y2b = sch.g("scalar",
                          *[(lambda e, i=i: e.copy(ypb[:, i, 0:T2], y2f[:, i, :]))
                            for i in range(8)],
                          deps=y2_toks + [pe.get("ypb2")])
            t_s1b = sch.g("tensor",
                          *[(lambda e, i=i: e.matmul(pf[0:1, 4, 0:T2], onescb[:],
                                                     ypb[:, i, 0:T2],
                                                     start=(i == 0), stop=(i == 7)))
                            for i in range(8)],
                          deps=[t_y2b, pe.get(("pf", 4))])
            t_mu2 = sch.g("vector",
                          lambda e: e.tensor_scalar(out=r1a[0:1, 0:T2],
                                                    in0=pf[0:1, 4, 0:T2],
                                                    scalar1=1.0 / D, scalar2=None,
                                                    op0=ALU.mult),
                          deps=[t_s1b, pe.get("r1a2")])
            pe[("pf", 4)] = t_mu2
            t_sq2b = sch.g("scalar",
                           *[(lambda e, i=i: e.activation(sq1[:, i, 0:T2],
                                                          ypb[:, i, 0:T2], AF.Square))
                             for i in range(8)],
                           deps=[t_y2b, pe.get("sq1")])
            t_s2b = sch.g("tensor",
                          *[(lambda e, i=i: e.matmul(pf[0:1, 4, 256:256 + T2], onescb[:],
                                                     sq1[:, i, 0:T2],
                                                     start=(i == 0), stop=(i == 7)))
                            for i in range(8)],
                          deps=[t_sq2b, t_mu2])
            pe["sq1"] = t_s2b
            t_var2 = sch.g("vector",
                           (lambda e: e.tensor_scalar(out=r1b[0:1, 0:T2],
                                                      in0=pf[0:1, 4, 256:256 + T2],
                                                      scalar1=1.0 / D, scalar2=None,
                                                      op0=ALU.mult)),
                           (lambda e: e.tensor_mul(pf[0:1, 4, 256:256 + T2],
                                                   r1a[0:1, 0:T2], r1a[0:1, 0:T2])),
                           (lambda e: e.tensor_sub(r1b[0:1, 0:T2], r1b[0:1, 0:T2],
                                                   pf[0:1, 4, 256:256 + T2])),
                           (lambda e: e.tensor_scalar(out=r1b[0:1, 0:T2],
                                                      in0=r1b[0:1, 0:T2],
                                                      scalar1=1e-5, scalar2=None,
                                                      op0=ALU.add)),
                           deps=[t_s2b, pe.get("r1b2")])
            pe[("pf", 4)] = t_var2
            t_inv2 = sch.g("scalar",
                           lambda e: e.activation(r1b[0:1, 0:T2], r1b[0:1, 0:T2], AF.Ln),
                           lambda e: e.activation(r1b[0:1, 0:T2], r1b[0:1, 0:T2],
                                                  AF.Exp, scale=-0.5),
                           deps=[t_var2])
            t_mexp2 = sch.g("tensor",
                            lambda e: e.matmul(pf[:, 4, 0:T2], onesrf[:],
                                               r1a[0:1, 0:T2], start=True, stop=True),
                            deps=[t_inv2, t_var2], mk=[(0, 1)])
            t_iexp2 = sch.g("tensor",
                            lambda e: e.matmul(pf[:, 5, 0:T2], onesrf[:],
                                               r1b[0:1, 0:T2], start=True, stop=True),
                            deps=[t_inv2, pe.get(("pf", 5))], mk=[(0, 1)])
            pe["r1a2"] = t_mexp2
            pe["r1b2"] = t_iexp2
            out_ops = []
            for i in range(8):
                out_ops.append(lambda e, i=i: e.tensor_sub(y2f[:, i, :], y2f[:, i, :],
                                                           pf[:, 4, 0:T2]))
                out_ops.append(lambda e, i=i: e.tensor_mul(y2f[:, i, :], y2f[:, i, :],
                                                           pf[:, 5, 0:T2]))
            t_out = sch.g("vector", *out_ops, deps=[t_mexp2, t_iexp2] + y2_toks)
            pe[("pf", 4)] = t_out
            pe[("pf", 5)] = t_out
            pe["ypb2"] = t_s2b
            t_od = sch.dma("gpsimd",
                           lambda e, tt=tt: e.dma_start(
                               out=outT.rearrange("(a p) s -> p a s", p=128)
                               [:, :, sl(tt, T2)],
                               in_=y2f),
                           deps=[t_out])
            pe["y2f"] = t_od
            out_toks.append(t_od)
            pe["n2b"] = t_mm2
            pe["scexp"] = t_exe2
            pe["scexp2"] = t_out
            pe["scrows"] = t_ex2
            pe["p2sc"] = t_rt2

        final_deps = (out_toks[-1:] if out_toks else []) + \
            [(ln, sch.dval[ln]) for ln in DSEM["gpsimd"] if ln in sch.dval]
        t_fin = sch.g("gpsimd", lambda e: e.nop(), deps=final_deps)

        with nc.Block() as block:
            @block.sync
            def _(eng):
                sch.replay("sync", eng, sems)

            @block.tensor
            def _(eng):
                sch.replay("tensor", eng, sems)

            @block.vector
            def _(eng):
                sch.replay("vector", eng, sems)

            @block.scalar
            def _(eng):
                sch.replay("scalar", eng, sems)

            @block.gpsimd
            def _(eng):
                sch.replay("gpsimd", eng, sems)

    _CACHE["nc"] = nc
    return nc


def _consts():
    ii = np.arange(128)
    c = {}
    c["ident"] = np.eye(128, dtype=np.float32)
    c["id64"] = np.vstack([np.eye(64, dtype=np.float32)] * 2)
    c["maskM"] = (ii[None, :] < ii[:, None]).astype(np.float32)
    c["maskMT"] = (ii[:, None] < ii[None, :]).astype(np.float32)
    c["maskAT"] = (ii[:, None] <= ii[None, :]).astype(np.float32)
    bexp = np.zeros((16, 8 * 128), np.float32)
    for i in range(8):
        for p in range(128):
            bexp[(i * 128 + p) // 64, i * 128 + p] = 1.0
    c["bexp"] = bexp
    c["bexpT"] = bexp.T.copy()
    c["onesc"] = np.ones((128, 1), np.float32)
    c["onesr"] = np.ones((1, 128), np.float32)
    sel16 = np.zeros((16, 16 * 128), np.float32)
    for j in range(16):
        r = 8 * (j % 2) + j // 2
        sel16[r, j * 128:(j + 1) * 128] = 1.0
    c["sel16"] = sel16
    return c


def kernel(**inputs):
    nc = build_nc()
    c = _consts()
    f32 = np.float32

    def A(v):
        return np.ascontiguousarray(np.asarray(v), dtype=f32)

    x = A(inputs["L0_attn_x"])
    wq = A(inputs["L0_attn_Wq"]); wk = A(inputs["L0_attn_Wk"])
    wv = A(inputs["L0_attn_Wv"]); wo = A(inputs["L0_attn_Wo"])
    cq = A(inputs["L0_attn_conv_q"]); ck = A(inputs["L0_attn_conv_k"])
    cv = A(inputs["L0_attn_conv_v"])
    wf = A(inputs["L0_attn_Wf"]); bf = A(inputs["L0_attn_bf"])
    wl = A(inputs["L0_attn_Wl"]); bl = A(inputs["L0_attn_bl"])
    wrq = A(inputs["L0_attn_Wrq"]); wrk = A(inputs["L0_attn_Wrk"])
    gs = A(inputs["L0_attn_graph_scale"])
    qn = A(inputs["L0_attn_qn_w"]); kn = A(inputs["L0_attn_kn_w"])
    on = A(inputs["L0_attn_on_w"])
    w1 = A(inputs["L0_ffn_w1"]); w2 = A(inputs["L0_ffn_w2"])

    wfl = np.zeros((D, 64), f32)
    wfl[:, 0:16] = wf
    wfl[:, 32:48] = wl
    bfl = np.zeros((64, 1), f32)
    bfl[0:16, 0] = bf
    bfl[32:48, 0] = bl
    wrq_p = np.zeros((D, D), f32)
    wrk_p = np.zeros((D, D), f32)
    RR_ = 48
    wrq3 = wrq.reshape(D, 16, RR_)
    wrk3 = wrk.reshape(D, 16, RR_)
    wrq_p.reshape(D, 16, 64)[:, :, 0:RR_] = wrq3
    wrk_p.reshape(D, 16, 64)[:, :, 0:RR_] = wrk3
    qnw = np.tile(qn, 2).reshape(128, 1)
    knw = np.tile(kn, 2).reshape(128, 1)
    onw = np.ascontiguousarray(on.reshape(8, 128).T)
    w1T = np.ascontiguousarray(w1.T)
    w2T = np.ascontiguousarray(w2.T)

    def cvl(cw_):
        return np.ascontiguousarray(cw_.reshape(8, 128, KC).transpose(1, 0, 2)
                                    .reshape(128, 8 * KC))
    shared = dict(wq=wq, wk=wk, wv=wv, wo=wo, wrq=wrq_p, wrk=wrk_p, wfl=wfl,
                  bfl=bfl, convq=cvl(cq), convk=cvl(ck), convv=cvl(cv),
                  gsc=gs.reshape(16, 1), qnw=qnw, knw=knw, onw=onw,
                  w1T=w1T, w2T=w2T, **c)
    in_maps = []
    for b in range(8):
        m = dict(shared)
        m["xT"] = np.ascontiguousarray(x[b].T)
        in_maps.append(m)

    res = run_bass_kernel_spmd(nc, in_maps, core_ids=list(range(8)))
    out = np.empty((8, S, D), np.float32)
    for b in range(8):
        out[b] = res.results[b]["outT"].T
    return out


# version 14 (final)
# speedup vs baseline: 1.4097x; 1.0345x over previous
import os

import numpy as np

import concourse.bass as bass
import concourse.mybir as mybir
from concourse.bass_utils import run_bass_kernel_spmd

F32 = mybir.dt.float32
BF16 = mybir.dt.bfloat16
I32 = mybir.dt.int32
AF = mybir.ActivationFunctionType
ALU = mybir.AluOpType
AXX = mybir.AxisListType.X

S = 4096
D = 1024
H = 16
DH = 64
KC = 4
DFF = 4096
T1 = 256
NST = int(os.environ.get("KN_NST", S // T1))
C = 128
NCH = T1 // C
T2 = 256
NT2 = int(os.environ.get("KN_NT2", S // T2))
KN_P15 = int(os.environ.get("KN_P15", 1))
KN_DN = int(os.environ.get("KN_DN", 1))
KN_STAGE = int(os.environ.get("KN_STAGE", 9))
KN_ST2 = int(os.environ.get("KN_ST2", 9))
KN_DN3 = int(os.environ.get("KN_DN3", 9))
KN_DN4 = int(os.environ.get("KN_DN4", 9))
KN_DN5 = int(os.environ.get("KN_DN5", 9))
KN_DN6 = int(os.environ.get("KN_DN6", 9))
KN_SOLVE = int(os.environ.get("KN_SOLVE", 7))
KN_DBG = int(os.environ.get("KN_DBG", 0))
NCT2 = T2 // 128
HB = 8

ENG = ("sync", "tensor", "vector", "scalar", "gpsimd")
CSEM = {"tensor": "st", "vector": "sv", "scalar": "ss", "gpsimd": "sg", "sync": "sy"}
NDL = {"sync": 8, "gpsimd": 4}
DSEM = {"sync": ["di%d" % i for i in range(8)], "gpsimd": ["do%d" % i for i in range(4)]}

DEBUG = False


class Sch:

    SERIAL = ("vector", "scalar", "gpsimd")

    def __init__(self):
        self.stream = {e: [] for e in ENG}
        self.cnt = {e: 0 for e in ENG}
        self.didx = {"sync": 0, "gpsimd": 0}
        self.dval = {}
        self.lastw = {e: {} for e in ENG}
        self.lastp = None
        self.dummy_fn = None

    def _waits(self, e, deps):
        ws = []
        flat = []
        for d in deps:
            if d is None:
                continue
            if isinstance(d, list):
                flat.extend(d)
            else:
                flat.append(d)
        for dtok in flat:
            if dtok is None:
                continue
            sem, val = dtok
            if sem == CSEM.get(e) and e not in self.SERIAL:
                continue
            if sem == CSEM.get(e) and e in self.SERIAL:
                continue
            if self.lastw[e].get(sem, -1) >= val:
                continue
            ws.append((sem, val))
            self.lastw[e][sem] = val
        if ws:
            self.stream[e].append(("waits", ws))

    def g(self, e, *fns, deps=(), mk=None):
        assert fns
        self._waits(e, deps)
        if e == "tensor":
            metas = mk if mk is not None else [None] * len(fns)
            out = []
            for fn, m in zip(fns, metas):
                if m is not None and self.lastp is not None and m[0] != self.lastp[0]:
                    out.append(self.dummy_fn)
                self.lastp = m
                out.append(fn)
            fns = tuple(out)
        if e in self.SERIAL:
            for fn in fns:
                self.stream[e].append(("selfwait", self.cnt[e]))
                self.stream[e].append(("opinc", fn))
                self.cnt[e] += 1
        else:
            for fn in fns[:-1]:
                self.stream[e].append(("op", fn))
            self.stream[e].append(("opinc", fns[-1]))
            self.cnt[e] += 1
        return (CSEM[e], self.cnt[e])

    def dma(self, e, *fns, deps=()):
        self._waits(e, deps)
        toks = {}
        for fn in fns:
            lane = DSEM[e][self.didx[e] % NDL[e]]
            self.didx[e] += 1
            prev = self.dval.get(lane, 0)
            if self.lastw[e].get(lane, -1) < prev and prev > 0:
                self.stream[e].append(("waits", [(lane, prev)]))
                self.lastw[e][lane] = prev
            self.dval[lane] = prev + 16
            self.stream[e].append(("dma", fn, lane))
            toks[lane] = prev + 16
        out = [(k, v) for k, v in toks.items()]
        return out if len(out) > 1 else out[0]

    def replay(self, e, eng, sems):
        own = sems.get(CSEM.get(e))
        for item in self.stream[e]:
            k = item[0]
            if k == "waits":
                for (sem, val) in item[1]:
                    eng.wait_ge(sems[sem], val)
            elif k == "selfwait":
                if item[1] > 0:
                    eng.wait_ge(own, item[1])
            elif k == "op":
                item[1](eng)
            elif k == "opinc":
                item[1](eng).then_inc(own, 1)
            else:
                item[1](eng).then_inc(sems[item[2]], 16)


def sl(i, n):
    return slice(i * n, (i + 1) * n)


def _maxtok(*toks):
    best = {}
    for t in toks:
        if t is None:
            continue
        best[t[0]] = max(best.get(t[0], -1), t[1])
    return [(k, v) for k, v in best.items()]


_CACHE = {}
OFFS = {}


def build_nc():
    if "nc" in _CACHE:
        return _CACHE["nc"]
    nc = bass.Bass("TRN2", target_bir_lowering=False, debug=False)

    def din(name, shape, dt=F32):
        return nc.dram_tensor(name, list(shape), dt, kind="ExternalInput").ap()

    def dout_(name, shape, dt=F32):
        return nc.dram_tensor(name, list(shape), dt, kind="ExternalOutput").ap()

    def dscr(name, shape, dt):
        return nc.dram_tensor(name, list(shape), dt, kind="Internal").ap()

    xT = din("xT", [D, S])
    wq_d = din("wq", [D, D]); wk_d = din("wk", [D, D]); wv_d = din("wv", [D, D])
    wo_d = din("wo", [D, D]); wrq_d = din("wrq", [D, D]); wrk_d = din("wrk", [D, D])
    wfl_d = din("wfl", [D, 64])
    bfl_d = din("bfl", [64, 1])
    cvq_d = din("convq", [128, 8 * KC]); cvk_d = din("convk", [128, 8 * KC])
    cvv_d = din("convv", [128, 8 * KC])
    gsc_d = din("gsc", [16, 1])
    qnw_d = din("qnw", [128, 1]); knw_d = din("knw", [128, 1])
    onw_d = din("onw", [128, 8])
    w1T_d = din("w1T", [D, DFF])
    w2T_d = din("w2T", [DFF, D])
    ident_d = din("ident", [128, 128])
    id64_d = din("id64", [128, 64])
    maskM_d = din("maskM", [128, 128])
    maskMT_d = din("maskMT", [128, 128])
    maskAT_d = din("maskAT", [128, 128])
    bexp_d = din("bexp", [16, 8 * 128])
    bexpT_d = din("bexpT", [8 * 128, 16])
    onesc_d = din("onesc", [128, 1])
    onesr_d = din("onesr", [1, 128])
    sel16_d = din("sel16", [16, 16 * 128])

    y1s = dscr("y1s", [D, S], F32)
    w2qs = dscr("w2qs", [DFF, D], BF16)
    wrqs = dscr("wrqs", [D, D], BF16)
    wrks = dscr("wrks", [D, D], BF16)
    wos = dscr("wos", [D, D], BF16)
    outT = dout_("outT", [D, S])
    if KN_DBG:
        dbg_d = dout_("dbgd", [128, 4 * 512])
        dbg_s = dout_("dbgs", [128, NST * 512], BF16)
    dbg = {}
    if DEBUG:
        for nm, shp in [("d_qt", [D, T1]), ("d_kt", [D, T1]), ("d_vc", [D, T1]),
                        ("d_ctm", [128, 128]), ("d_W", [128, HB * 128]),
                        ("d_O", [D, T1]), ("d_y1", [D, T1])]:
            dbg[nm] = dout_(nm, shp, F32)

    sch = Sch()
    import contextlib
    stack = contextlib.ExitStack()
    sb = {}

    def alo(name, shape, dt):
        sb[name] = stack.enter_context(nc.sbuf_tensor(name, list(shape), dt))
        return sb[name]

    with stack:
        NB16 = 73 * 1024
        NF32 = 15320
        AB16 = alo("AB16", [128, NB16], BF16)
        AF32 = alo("AF32", [128, NF32], F32)
        _o = {"b": 0, "f": 0}

        def cb16(shape, at=None, name=None):
            n = int(np.prod(shape[1:])) if len(shape) > 1 else 1
            off = _o["b"] if at is None else at
            ap = AB16.ap()[0:shape[0], off:off + n]
            if at is None:
                _o["b"] += n
            if len(shape) > 2:
                ap = ap.rearrange("p (a n) -> p a n", a=shape[1])
            import traceback
            ln = traceback.extract_stack()[-2].line
            nm = ln.split("=")[0].strip().split(",")[0] if ln else None
            if nm:
                OFFS[nm] = ("b", off, tuple(shape))
            return ap, off

        def cf32(shape, at=None):
            n = int(np.prod(shape[1:])) if len(shape) > 1 else 1
            off = _o["f"] if at is None else at
            ap = AF32.ap()[0:shape[0], off:off + n]
            if at is None:
                _o["f"] += n
            if len(shape) > 2:
                ap = ap.rearrange("p (a n) -> p a n", a=shape[1])
            import traceback
            ln = traceback.extract_stack()[-2].line
            nm = ln.split("=")[0].strip().split(",")[0] if ln else None
            if nm:
                OFFS[nm] = ("f", off, tuple(shape))
            return ap, off

        wqb, o_wqb = cb16([128, 8, 1024])
        wkb, _ = cb16([128, 8, 1024])
        wvb, _ = cb16([128, 8, 1024])
        wsb0, _ = cb16([128, 8, 1024])
        w1qb, _ = cb16([128, 8, DFF], at=o_wqb)
        wsb1, o_wsb1 = cb16([128, 8, 1024])
        n2b, _ = cb16([128, 32, T2], at=o_wsb1)
        wsb = [wsb0, wsb1]
        wflb, _ = cb16([128, 8, 64])
        xbb, _ = cb16([128, 8, T1])
        pre, _ = cb16([128, 8, KC - 1 + T1])
        qc, o_qc = cb16([128, 8, T1])
        n1b, _ = cb16([128, 8, T2], at=o_qc)
        kc_, o_kc = cb16([128, 8, T1])
        w2qc0, _ = cb16([128, 1024], at=o_kc)
        w2qc1, _ = cb16([128, 1024], at=o_kc + 1024)
        w2qc = [w2qc0, w2qc1]
        vc, o_vc = cb16([128, 8, T1])
        w2qc2, _ = cb16([128, 1024], at=o_vc)
        w2qc3, _ = cb16([128, 1024], at=o_vc + 1024)
        w2qs4 = [w2qc0, w2qc1, w2qc2, w2qc3]
        rq, _ = cb16([128, 8, T1])
        rk, _ = cb16([128, 8, T1])
        ofm, _ = cb16([128, 8, T1])
        ypb, _ = cb16([128, 8, T1])
        sq1, _ = cb16([128, 8, T1])
        m_m, _ = cb16([128, HB, 128])
        m_mt, _ = cb16([128, HB, 128])
        m_at, _ = cb16([128, HB, 128])
        gb_, _ = cb16([128, HB, 128])
        gtb_, _ = cb16([128, HB, 128])
        gbuf = [gb_, gb_]
        gtbuf = [gtb_, gtb_]
        wb_, _ = cb16([128, HB, 128])
        wbuf = [wb_, wb_]
        expb, _ = cb16([128, 128])
        gexp, _ = cb16([128, HB, 128])
        kcs, _ = cb16([128, HB, 64])
        ztb, _ = cb16([128, HB, 128])
        awvb, _ = cb16([128, HB, 128])
        pktb, _ = cb16([128, HB // 2, 64])
        stb, _ = cb16([128, H // 2, 64])
        sc16, _ = cb16([16, T1])
        identb, _ = cb16([128, 128])
        maskMb, _ = cb16([128, 128])
        maskMTb, _ = cb16([128, 128])
        maskATb, _ = cb16([128, 128])
        bexpb, _ = cb16([16, 8 * 128])
        bexpTb, _ = cb16([128, 8, 16])
        onescb, _ = cb16([128, 1])
        onesrb, _ = cb16([1, 128])
        lam16, _ = cb16([16, 1])
        halo, _ = cb16([128, 3 * 8, KC - 1])
        assert _o["b"] <= NB16, _o["b"]

        stage, o_st = cf32([128, 1, 1024])
        xf0, _ = cf32([128, 8, T1])
        xf1, _ = cf32([128, 8, T1])
        xf = [xf0, xf1]
        hf, _ = cf32([128, 32, T2], at=o_st)
        ypre, o_yp = cf32([128, 8, T1])
        y1f = ypre
        stAB = [stage.rearrange("p a n -> p (a n)"),
                cf32([128, 1024], at=1024)[0]]
        flf, o_fl = cf32([64, T1])
        lnab, _ = cf32([64, T1])
        tbl1, _ = cf32([128, T1])
        tbl2, _ = cf32([128, T1])
        y2f, _ = cf32([128, 8, T2], at=0)
        ctm, o_ctm = cf32([128, NCH, 128])
        s16a, _ = cf32([16, T1])
        s16b, _ = cf32([16, T1])
        scexp, _ = cf32([128, 4, T2], at=o_ctm)
        cscr, o_cs = cf32([128, T1])
        crows, _ = cf32([16, 128])
        tmax, _ = cf32([128, T2], at=o_cs + T2)
        argf, _ = cf32([128, 128])
        pvf, o_pvf = cf32([128, HB // 2, 64])
        stf, o_stf = cf32([128, H // 2, 64])
        gcf, o_gcf = cf32([128, HB, 1])
        p2sc, _ = cf32([128, 8], at=o_gcf)
        crs_scr, _ = cf32([128, 16])
        r1a, _ = cf32([1, T1])
        r1b, _ = cf32([1, T1])
        qs_g, o_qsg = cf32([128, 512])
        qs_f, o_qsf = cf32([128, 512])
        argf8, _ = cf32([128, 8, 128], at=o_qsg)
        wsum, _ = cf32([128, 12])
        sval, _ = cf32([128, 8])
        identf, _ = cf32([128, 128])
        id64f, _ = cf32([128, 64])
        sel16f, _ = cf32([16, 16 * 128])
        onescf, _ = cf32([128, 1])
        onesrf, _ = cf32([1, 128])
        qnwb, _ = cf32([128, 1])
        knwb, _ = cf32([128, 1])
        onwb, _ = cf32([128, 8])
        lamb, _ = cf32([128, 8])
        convb, _ = cf32([128, 24, KC])
        bflb, _ = cf32([64, 1])
        o_tail = _o["f"]
        hy_pv, _ = cf32([128, 256], at=o_pvf)
        hy_st, _ = cf32([128, 2, 256], at=o_stf)
        hy_qg, _ = cf32([128, 256], at=o_qsg + 256)
        hy_qf, _ = cf32([128, 256], at=o_qsf + 256)
        hy_tl, _ = cf32([128, 3, 256], at=o_tail)
        cscr2 = hy_tl[:, 0, :]
        scrows, _ = cf32([16, 128], at=o_tail + 768)
        hyfc = [hy_pv, hy_st[:, 0, :], hy_st[:, 1, :], hy_qg, hy_qf,
                hy_tl[:, 0, :], hy_tl[:, 1, :], hy_tl[:, 2, :]]
        assert o_tail + 768 + 128 <= NF32, o_tail
        assert _o["f"] <= NF32, _o["f"]
        qs_i = stack.enter_context(nc.sbuf_tensor("qs_i", [128, 512], I32))

        pf = stack.enter_context(nc.psum_tensor("pf", [128, 6, 512], F32))
        pb = stack.enter_context(nc.psum_tensor("pb", [128, 1, 1024], BF16))
        pfd = stack.enter_context(nc.psum_tensor("pfd", [128, 512], F32))
        sch.dummy_fn = lambda e: e.matmul(pfd[:, 0:16], identb[:], identb[:, 0:16],
                                          start=True, stop=True)

        sems = {}
        for s_ in list(CSEM.values()) + DSEM["sync"] + DSEM["gpsimd"]:
            sems[s_] = stack.enter_context(nc.semaphore(s_))

        pe = {}

        t0 = sch.dma(
            "sync",
            lambda e: e.dma_start(out=identf[:], in_=ident_d),
            lambda e: e.dma_start(out=id64f[:], in_=id64_d),
            lambda e: e.dma_start(out=stage[:, 0, 0:128], in_=maskM_d),
            lambda e: e.dma_start(out=stage[:, 0, 128:256], in_=maskMT_d),
            lambda e: e.dma_start(out=stage[:, 0, 256:384], in_=maskAT_d),
            lambda e: e.dma_start(out=stage[:, 0, 384:512].rearrange("p (a n) -> p a n", a=8),
                                  in_=bexpT_d.rearrange("(a p) n -> p a n", p=128)),
            lambda e: e.dma_start(out=onescf[:], in_=onesc_d),
            lambda e: e.dma_start(out=onesrf[:], in_=onesr_d),
            lambda e: e.dma_start(out=sel16f[:], in_=sel16_d),
            lambda e: e.dma_start(out=qnwb[:], in_=qnw_d),
            lambda e: e.dma_start(out=knwb[:], in_=knw_d),
            lambda e: e.dma_start(out=onwb[:], in_=onw_d),
            lambda e: e.dma_start(out=bflb[:], in_=bfl_d),
            lambda e: e.dma_start(out=stage[0:16, 0, 512:513], in_=gsc_d),
            lambda e: e.dma_start(out=convb[:, 0:8, :].rearrange("p a k -> p (a k)"),
                                  in_=cvq_d),
            lambda e: e.dma_start(out=convb[:, 8:16, :].rearrange("p a k -> p (a k)"),
                                  in_=cvk_d),
            lambda e: e.dma_start(out=convb[:, 16:24, :].rearrange("p a k -> p (a k)"),
                                  in_=cvv_d),
        )
        t_cc = sch.g(
            "vector",
            lambda e: e.tensor_copy(identb[:], identf[:]),
            lambda e: e.tensor_copy(maskMb[:], stage[:, 0, 0:128]),
            lambda e: e.tensor_copy(maskMTb[:], stage[:, 0, 128:256]),
            lambda e: e.tensor_copy(maskATb[:], stage[:, 0, 256:384]),
            lambda e: e.tensor_copy(bexpTb[:].rearrange("p a n -> p (a n)"),
                                    stage[:, 0, 384:512]),
            lambda e: e.tensor_copy(onescb[:], onescf[:]),
            lambda e: e.tensor_copy(onesrb[:], onesrf[:]),
            deps=[t0],
        )
        pass
        t_abs = sch.g("scalar",
                      lambda e: e.activation(stage[0:16, 0, 513:514], stage[0:16, 0, 512:513], AF.Abs),
                      deps=[t0])
        t_l16 = sch.g("vector", lambda e: e.tensor_copy(lam16[:], stage[0:16, 0, 513:514]),
                      deps=[t_abs])
        t_bexp_d = sch.dma("sync",
                           lambda e: e.dma_start(out=stage[0:16, 0, 0:1024], in_=bexp_d),
                           deps=[t_cc, t_l16])
        t_bexp = sch.g("vector",
                       lambda e: e.tensor_copy(bexpb[:], stage[0:16, 0, 0:1024]),
                       deps=[t_bexp_d])
        t_lmm = sch.g("tensor",
                      *[(lambda e, i=i: e.matmul(pf[:, 0, sl(i, 1)], bexpb[0:16, sl(i, 128)],
                                                 lam16[:], start=True, stop=True))
                        for i in range(8)],
                      deps=[t_l16, t_bexp], mk=[(0, 16)] * 8)
        t_lam = sch.g("vector", lambda e: e.tensor_copy(lamb[:], pf[:, 0, 0:8]), deps=[t_lmm])
        pe[("pf", 0)] = t_lam

        def w_res(dram, dst, prev):
            tok = None
            for r in range(8):
                t_d = sch.dma("sync",
                              lambda e, dram=dram, r=r: e.dma_start(
                                  out=stage[:, 0, :],
                                  in_=dram.rearrange("(a p) n -> p a n", p=128)[:, r, :]),
                              deps=(prev if r == 0 else [tok]))
                tok = sch.g("vector",
                            lambda e, r=r, dst=dst: e.tensor_copy(dst[:, r, :],
                                                                  stage[:, 0, :]),
                            deps=[t_d])
            return tok

        t_wq = w_res(wq_d, wqb, [t_lam, t_cc, t_bexp])
        t_wk = w_res(wk_d, wkb, [t_wq])
        t_wv = w_res(wv_d, wvb, [t_wk])
        t_fld = sch.dma("sync",
                        lambda e: e.dma_start(
                            out=stage[:, 0, 0:512].rearrange("p (a n) -> p a n", a=8),
                            in_=wfl_d.rearrange("(a p) n -> p a n", p=128)),
                        deps=[t_wv])
        t_wfl = sch.g("vector",
                      *[(lambda e, i=i: e.tensor_copy(
                          wflb[:, i, :],
                          stage[:, 0, 0:512].rearrange("p (a n) -> p a n", a=8)[:, i, :]))
                        for i in range(8)],
                      deps=[t_fld])

        def w_scr(dram, scr, fold, prev):
            tok = None
            for r in range(8):
                t_d = sch.dma("sync",
                              lambda e, dram=dram, r=r: e.dma_start(
                                  out=stage[:, 0, :],
                                  in_=dram.rearrange("(a p) n -> p a n", p=128)[:, r, :]),
                              deps=(prev if r == 0 else [tok]))
                if fold:
                    tok = sch.g("vector",
                                lambda e, r=r: e.tensor_scalar(
                                    out=wsb[0][:, r, :], in0=stage[:, 0, :],
                                    scalar1=onwb[:, r:r + 1], scalar2=None,
                                    op0=ALU.mult),
                                deps=[t_d])
                else:
                    tok = sch.g("vector",
                                lambda e, r=r: e.tensor_copy(wsb[0][:, r, :],
                                                             stage[:, 0, :]),
                                deps=[t_d])
            return sch.dma("gpsimd",
                           lambda e, scr=scr: e.dma_start(
                               out=scr.rearrange("(a p) n -> p a n", p=128),
                               in_=wsb[0]),
                           deps=[tok])

        t_srq = w_scr(wrq_d, wrqs, False, [t_wfl])
        t_srk = w_scr(wrk_d, wrks, False, [t_srq])
        t_swo = w_scr(wo_d, wos, True, [t_srk])

        t_init = sch.g(
            "vector",
            lambda e: e.memset(stf[:].rearrange("p a n -> p (a n)"), 0.0),
            lambda e: e.memset(stb[:].rearrange("p a n -> p (a n)"), 0.0),
            lambda e: e.memset(halo[:].rearrange("p a c -> p (a c)"), 0.0),
            lambda e: e.memset(tbl1[:], 0.0),
            lambda e: e.memset(tbl2[:], 0.0),
        )
        x_tok = [None, None]
        x_tok[0] = sch.dma("sync",
                           lambda e: e.dma_start(
                               out=xf[0],
                               in_=xT.rearrange("(a p) s -> p a s", p=128)[:, :, 0:T1]),
                           deps=[t_swo])
        ws_tok = {}

        def load_ws(which, buf, deps):
            scr = {"rq": wrqs, "rk": wrks, "wo": wos}[which]
            return sch.dma("sync",
                           lambda e, scr=scr, buf=buf: e.dma_start(
                               out=wsb[buf],
                               in_=scr.rearrange("(a p) n -> p a n", p=128)),
                           deps=deps)

        ws_tok[(0, "rq")] = load_ws("rq", 0, [t_swo])
        ws_tok[(0, "rk")] = load_ws("rk", 1, [ws_tok[(0, "rq")]])
        wsbuf_of = lambda st, k: (3 * st + k) % 2
        y1_toks = []

        for st in range(NST):
            cur = st % 2
            tx = x_tok[cur]
            if KN_DBG:
                t_dbgs = sch.dma("gpsimd",
                                 lambda e, st=st: e.dma_start(
                                     out=dbg_s[:, sl(st, 512)],
                                     in_=stb[:].rearrange("p a n -> p (a n)")),
                                 deps=[pe.get("stb_r"), pe.get("dbgs"), t_init])
                pe["dbgs"] = t_dbgs
                pe["stb_r2"] = t_dbgs
            if st + 1 < NST:
                x_tok[(st + 1) % 2] = sch.dma(
                    "sync",
                    lambda e, st=st: e.dma_start(
                        out=xf[(st + 1) % 2],
                        in_=xT.rearrange("(a p) s -> p a s", p=128)[:, :, sl(st + 1, T1)]),
                    deps=[pe.get(("xf", (st + 1) % 2))])

            t_xbv = sch.g("vector",
                          *[(lambda e, i=i, cur=cur: e.tensor_copy(xbb[:, i, :],
                                                                   xf[cur][:, i, :]))
                            for i in range(6)],
                          deps=[tx, pe.get("xbb")])
            t_xbg = sch.g("gpsimd",
                          *[(lambda e, i=i, cur=cur: e.tensor_copy(xbb[:, i, :],
                                                                   xf[cur][:, i, :]))
                            for i in (6, 7)],
                          deps=[tx, pe.get("xbb")])
            t_xb = [t_xbv, t_xbg]

            def proj(wsrc, dst, dst_off, wtok, freedep=None):
                toks = []
                for pr in range(4):
                    t_mm = None
                    for s_ in range(2):
                        oc = 2 * pr + s_
                        mms = [(lambda e, oc=oc, kk=kk, pr=pr, s_=s_, wsrc=wsrc:
                                e.matmul(pf[:, pr, s_ * T1:s_ * T1 + T1],
                                         wsrc[:, kk, sl(oc, 128)],
                                         xbb[:, kk, :], start=(kk == 0),
                                         stop=(kk == 7)))
                               for kk in range(8)]
                        t_mm = sch.g("tensor", *mms,
                                     deps=[t_xb, wtok]
                                     + ([pe.get(("pf", pr))] if s_ == 0 else []))
                    t_ev = sch.g("scalar",
                                 lambda e, pr=pr, dst=dst, off=dst_off:
                                 e.copy(dst[:, 2 * pr:2 * pr + 2, off:off + T1],
                                        pf[:, pr, 0:2 * T1]
                                        .rearrange("p (a n) -> p a n", a=2)),
                                 deps=[t_mm] + ([freedep] if freedep else []))
                    pe[("pf", pr)] = t_ev
                    toks.append(t_ev)
                return toks

            def halo_in(tix, deps):
                return sch.g("vector",
                             lambda e, tix=tix: e.tensor_copy(
                                 pre[:, :, 0:KC - 1],
                                 halo[:, tix * 8:tix * 8 + 8, :]),
                             deps=deps + [t_init])

            def halo_out(tix, deps):
                return sch.g("vector",
                             lambda e, tix=tix: e.tensor_copy(
                                 halo[:, tix * 8:tix * 8 + 8, :],
                                 pre[:, :, T1:T1 + KC - 1]),
                             deps=deps)

            def conv_emit(dst, widx, deps, freedep=None):
                toks = []
                for i in range(8):
                    eng = "vector"
                    buf = cscr
                    ops = [lambda e, i=i, widx=widx, buf=buf: e.tensor_scalar(
                        out=buf[:], in0=pre[:, i, 0:T1],
                        scalar1=convb[:, widx * 8 + i, 0:1], scalar2=None, op0=ALU.mult)]
                    for j in range(1, KC - 1):
                        ops.append(lambda e, i=i, j=j, widx=widx, buf=buf:
                                   e.scalar_tensor_tensor(
                                       out=buf[:], in0=pre[:, i, j:j + T1],
                                       scalar=convb[:, widx * 8 + i, j:j + 1], in1=buf[:],
                                       op0=ALU.mult, op1=ALU.add))
                    ops.append(lambda e, i=i, dst=dst, widx=widx, buf=buf:
                               e.scalar_tensor_tensor(
                                   out=dst[:, i, :], in0=pre[:, i, KC - 1:KC - 1 + T1],
                                   scalar=convb[:, widx * 8 + i, KC - 1:KC], in1=buf[:],
                                   op0=ALU.mult, op1=ALU.add))
                    toks.append(sch.g(eng, *ops,
                                      deps=deps + ([freedep] if freedep else [])))
                return toks

            t_hq = halo_in(0, [pe.get("pre")])
            tq = proj(wqb, pre, KC - 1, t_wq, pe.get("pre"))
            t_hqs = halo_out(0, tq)
            tcq = conv_emit(qc, 0, tq + [t_hq], pe.get("qcF"))
            t_hk = halo_in(1, [tcq[-1], t_hqs])
            tk = proj(wkb, pre, KC - 1, t_wk, tcq[-1])
            t_hks = halo_out(1, tk)
            tck = conv_emit(kc_, 1, tk + [t_hk], pe.get("kcF"))
            t_hv = halo_in(2, [tck[-1], t_hks])
            tv = proj(wvb, pre, KC - 1, t_wv, tck[-1])
            t_hvs = halo_out(2, tv)
            tcv = conv_emit(vc, 2, tv + [t_hv], pe.get("vcF"))
            pe["pre"] = tcv[-1]

            trq = proj(wsb[wsbuf_of(st, 0)], rq, 0, ws_tok[(st, "rq")], pe.get("rqF"))
            trk = proj(wsb[wsbuf_of(st, 1)], rk, 0, ws_tok[(st, "rk")], pe.get("rkF"))
            ws_tok[(st, "wo")] = load_ws("wo", wsbuf_of(st, 2), trq)
            if st + 1 < NST:
                ws_tok[(st + 1, "rq")] = load_ws("rq", wsbuf_of(st + 1, 0), trk)

            t_flmm = sch.g("tensor",
                           *[(lambda e, kk=kk: e.matmul(pf[0:64, 4, 0:T1], wflb[:, kk, :],
                                                        xbb[:, kk, :],
                                                        start=(kk == 0), stop=(kk == 7)))
                             for kk in range(8)],
                           deps=[t_xb, t_wfl, pe.get(("pf", 4))])
            t_fl = sch.g("vector",
                         lambda e: e.tensor_scalar(out=flf[:], in0=pf[0:64, 4, 0:T1],
                                                   scalar1=bflb[:], scalar2=None, op0=ALU.add),
                         deps=[t_flmm, pe.get("flf")])
            pe[("pf", 4)] = t_fl

            def headsums(src, deps):
                t_sq = sch.g("scalar",
                             *[(lambda e, i=i, src=src: e.activation(sq1[:, i, :],
                                                                     src[:, i, :], AF.Square))
                               for i in range(8)],
                             deps=deps + [pe.get("sq1")])
                t_mm = sch.g("tensor",
                             *[(lambda e, i=i: e.matmul(pf[0:16, 4, 0:T1], bexpTb[:, i, :],
                                                        sq1[:, i, :],
                                                        start=(i == 0), stop=(i == 7)))
                               for i in range(8)],
                             deps=[t_sq, pe.get(("pf", 4))])
                pe["sq1"] = t_mm
                return t_mm

            def expand_apply(apply_fn_list, deps):
                toks = []
                for i in range(8):
                    t_e = sch.g("tensor",
                                lambda e, i=i: e.matmul(pf[:, 5, 0:T1],
                                                        bexpb[0:16, sl(i, 128)], sc16[:],
                                                        start=True, stop=True),
                                deps=deps + [pe.get(("pf", 5))], mk=[(0, 16)])
                    t_a = apply_fn_list[i](t_e)
                    pe[("pf", 5)] = t_a
                    toks.append(t_a)
                return toks

            t_qs = headsums(qc, tcq)
            t_qsc = sch.g("vector",
                          lambda e: e.tensor_scalar(out=s16a[:], in0=pf[0:16, 4, 0:T1],
                                                    scalar1=1.0 / DH, scalar2=1e-6,
                                                    op0=ALU.mult, op1=ALU.add),
                          deps=[t_qs])
            pe[("pf", 4)] = t_qsc
            t_q2 = sch.g("scalar",
                         lambda e: e.activation(s16b[:], s16a[:], AF.Ln),
                         lambda e: e.activation(s16a[:], s16b[:], AF.Exp, scale=-0.5),
                         deps=[t_qsc])
            t_qb = sch.g("vector", lambda e: e.tensor_copy(sc16[:], s16a[:]),
                         deps=[t_q2, pe.get("sc16")])

            def mkqfn(i):
                def f(t_e):
                    return sch.g("vector",
                                 (lambda e, i=i: e.scalar_tensor_tensor(
                                     out=qc[:, i, :], in0=qc[:, i, :], scalar=qnwb[:],
                                     in1=pf[:, 5, 0:T1], op0=ALU.mult, op1=ALU.mult)),
                                 (lambda e, i=i: e.scalar_tensor_tensor(
                                     out=qc[:, i, :], in0=rq[:, i, :],
                                     scalar=lamb[:, i:i + 1], in1=qc[:, i, :],
                                     op0=ALU.mult, op1=ALU.add)),
                                 deps=[t_e] + tcq + trq)
                return f

            tqt = expand_apply([mkqfn(i) for i in range(8)], [t_qb])
            pe["sc16"] = tqt[-1]

            t_ks = headsums(kc_, tck)
            t_ksc = sch.g("vector",
                          lambda e: e.tensor_scalar(out=s16a[:], in0=pf[0:16, 4, 0:T1],
                                                    scalar1=1.0 / DH, scalar2=1e-6,
                                                    op0=ALU.mult, op1=ALU.add),
                          deps=[t_ks])
            pe[("pf", 4)] = t_ksc
            t_k2 = sch.g("scalar",
                         lambda e: e.activation(s16b[:], s16a[:], AF.Ln),
                         lambda e: e.activation(s16a[:], s16b[:], AF.Exp, scale=-0.5),
                         deps=[t_ksc])
            t_kb = sch.g("vector", lambda e: e.tensor_copy(sc16[:], s16a[:]),
                         deps=[t_k2, pe.get("sc16")])

            def mkkfn(i):
                def f(t_e):
                    return sch.g("vector",
                                 (lambda e, i=i: e.scalar_tensor_tensor(
                                     out=kc_[:, i, :], in0=kc_[:, i, :], scalar=knwb[:],
                                     in1=pf[:, 5, 0:T1], op0=ALU.mult, op1=ALU.mult)),
                                 (lambda e, i=i: e.scalar_tensor_tensor(
                                     out=kc_[:, i, :], in0=rk[:, i, :],
                                     scalar=lamb[:, i:i + 1], in1=kc_[:, i, :],
                                     op0=ALU.mult, op1=ALU.add)),
                                 deps=[t_e] + tck + trk)
                return f

            tks_ = expand_apply([mkkfn(i) for i in range(8)], [t_kb])
            pe["sc16"] = tks_[-1]

            t_k2s = headsums(kc_, tks_)
            t_k2l = sch.g("scalar",
                          lambda e: e.activation(s16b[:], pf[0:16, 4, 0:T1], AF.Ln),
                          lambda e: e.activation(s16a[:], s16b[:], AF.Exp, scale=0.5),
                          deps=[t_k2s])
            pe[("pf", 4)] = t_k2l
            t_k2c = sch.g("vector",
                          lambda e: e.tensor_scalar(out=s16a[:], in0=s16a[:], scalar1=1e-12,
                                                    scalar2=None, op0=ALU.max),
                          lambda e: e.reciprocal(s16b[:], s16a[:]),
                          lambda e: e.tensor_copy(sc16[:], s16b[:]),
                          deps=[t_k2l, pe.get("sc16")])

            def mknfn(i):
                def f(t_e):
                    return sch.g("vector",
                                 lambda e, i=i: e.tensor_mul(kc_[:, i, :], kc_[:, i, :],
                                                             pf[:, 5, 0:T1]),
                                 deps=[t_e])
                return f

            tkt = expand_apply([mknfn(i) for i in range(8)], [t_k2c])
            pe["sc16"] = tkt[-1]

            if KN_STAGE < 2:
                continue
            t_t = sch.g("scalar",
                        lambda e: e.activation(lnab[:], flf[:], AF.Exp, scale=-1.0),
                        deps=[t_fl, pe.get("lnab")])
            pe["flf"] = t_t
            t_t2 = sch.g("vector",
                         lambda e: e.tensor_scalar(out=lnab[:], in0=lnab[:], scalar1=1.0,
                                                   scalar2=None, op0=ALU.add),
                         deps=[t_t])
            t_t3 = sch.g("scalar", lambda e: e.activation(lnab[:], lnab[:], AF.Ln),
                         deps=[t_t2])
            t_t4 = sch.g("vector",
                         lambda e: e.tensor_scalar(out=lnab[:], in0=lnab[:], scalar1=-1.0,
                                                   scalar2=None, op0=ALU.mult),
                         deps=[t_t3])
            if KN_ST2 < 1:
                continue
            ctm_toks = []
            for ch in range(NCH):
                cs = sl(ch, C)
                t_c = sch.g("vector",
                            (lambda e, cs=cs: e.tensor_tensor_scan(
                                out=tbl1[0:16, cs], data0=lnab[0:16, cs],
                                data1=lnab[0:16, cs], initial=0.0,
                                op0=ALU.add, op1=ALU.bypass)),
                            (lambda e, cs=cs: e.tensor_sub(tbl1[32:48, cs], tbl1[0:16, cs],
                                                           lnab[0:16, cs])),
                            (lambda e, cs=cs: e.tensor_add(tbl1[32:48, cs], tbl1[32:48, cs],
                                                           lnab[32:48, cs])),
                            deps=[t_t4, pe.get("tbl1")])
                if KN_ST2 < 2:
                    ctm_toks.append(t_c)
                    continue
                t_bx = sch.g("scalar",
                             (lambda e, cs=cs: e.activation(tbl1[64:80, cs], lnab[32:48, cs],
                                                            AF.Exp)),
                             (lambda e, cs=cs: e.activation(tbl1[96:112, cs],
                                                            tbl1[32:48, cs], AF.Exp)),
                             deps=[t_c])
                t_g1 = sch.g("vector",
                             lambda e, cs=cs, ch=ch: e.tensor_scalar(
                                 out=tbl2[0:16, cs], in0=tbl1[0:16, cs],
                                 scalar1=tbl1[0:16, ch * C + C - 1:ch * C + C],
                                 scalar2=None, op0=ALU.subtract),
                             deps=[t_c, pe.get("tbl2")])
                t_g2 = sch.g("scalar",
                             lambda e, cs=cs: e.activation(tbl2[0:16, cs], tbl2[0:16, cs],
                                                           AF.Exp, scale=-1.0),
                             deps=[t_g1])
                if KN_ST2 < 3:
                    ctm_toks.append(t_g2)
                    continue
                t_tr = sch.g("tensor",
                             lambda e, cs=cs: e.transpose(pf[:, 4, 0:128], tbl1[:, cs],
                                                          identf[:]),
                             deps=[t_bx, t_init, pe.get(("pf", 4))])
                t_ctm = sch.g("vector",
                              lambda e, ch=ch: e.tensor_copy(ctm[:, ch, :],
                                                             pf[:, 4, 0:128]),
                              deps=[t_tr, pe.get("ctm%d" % ch)])
                pe[("pf", 4)] = t_ctm
                t_tr2 = sch.g("tensor",
                              lambda e, cs=cs: e.transpose(pf[:, 4, 128:256], tbl2[:, cs],
                                                           identf[:]),
                              deps=[t_g2, t_ctm, pe.get(("pf", 4, "b"))])
                t_ctm2 = sch.g("vector",
                               lambda e, ch=ch: e.tensor_copy(ctm[:, ch, 112:128],
                                                              pf[:, 4, 128:144]),
                               deps=[t_tr2])
                pe[("pf", 4, "b")] = t_ctm2
                ctm_toks.append(t_ctm2)
            pe["tbl1"] = ctm_toks[-1]
            pe["tbl2"] = ctm_toks[-1]
            pe["lnab"] = ctm_toks[-1]

            if KN_STAGE < 3:
                continue
            ofm_toks = []
            for ch in range(NCH):
                cs = sl(ch, C)
                tctm = ctm_toks[ch]
                for bb in range(2):
                    heads = list(range(bb * HB, (bb + 1) * HB))
                    t_crc = sch.g(
                        "vector",
                        lambda e, ch=ch, bb=bb: e.tensor_copy(
                            crs_scr[:].rearrange("p (g r) -> p g r", g=2),
                            ctm[:, ch, :].rearrange("p (g r) -> p g r", g=4)
                            [:, 0:2, sl(bb, 8)]),
                        deps=[tctm, pe.get("crs_scr")])
                    t_cr = sch.g(
                        "tensor",
                        lambda e: e.transpose(pf[0:16, 4, 256:384], crs_scr[:],
                                              identf[:]),
                        deps=[t_crc, pe.get(("pf", 4, "cr"))])
                    pe["crs_scr"] = t_cr
                    t_crv = sch.g("vector",
                                  lambda e: e.tensor_copy(crows[:], pf[0:16, 4, 256:384]),
                                  deps=[t_cr, pe.get("crows")])
                    pe[("pf", 4, "cr")] = t_crv

                    if (KN_DBG and int(os.environ.get('KN_DBGSEL', 0)) == 3
                            and st == NST - 1 and bb == 1
                            and ch == int(os.environ.get('KN_DBGCH', NCH - 1))):
                        t_dbg3 = sch.g(
                            "vector",
                            (lambda e: e.tensor_copy(ypre[0:16, 0, 0:128], crows[:])),
                            (lambda e, ch=ch: e.tensor_copy(ypre[:, 1, 0:128],
                                                            ctm[:, ch, :])),
                            (lambda e, ch=ch: e.tensor_copy(ypre[:, 2, 0:128],
                                                            tbl1[:, sl(ch, C)])),
                            (lambda e, ch=ch: e.tensor_copy(ypre[0:64, 3, 0:128],
                                                            lnab[:, sl(ch, C)])),
                            deps=[t_crv, tctm, pe.get("ypre")])
                        t_dbg3d = sch.dma("gpsimd",
                                          lambda e: e.dma_start(
                                              out=dbg_d,
                                              in_=ypre[:, 0:8, :]
                                              .rearrange("p a n -> p (a n)")),
                                          deps=[t_dbg3])
                        pe["ypre"] = t_dbg3d
                    if KN_DN4 < 2:
                        ofm_toks.append(t_crv)
                        continue
                    sk_mms, sk_meta = [], []
                    for par in range(2):
                        for jj, h in enumerate(heads):
                            if h % 2 != par:
                                continue
                            ti = h // 2
                            rs = slice(par * 64, par * 64 + 64)
                            bank, qtr = jj // 4, jj % 4
                            sk_mms.append(lambda e, ti=ti, rs=rs, bank=bank, qtr=qtr, cs=cs:
                                          e.matmul(pf[:, bank, sl(qtr, 128)],
                                                   kc_[rs, ti, cs], kc_[rs, ti, cs],
                                                   start=True, stop=True,
                                                   tile_position=(rs.start, 0)))
                            sk_meta.append((par * 64, 64))
                            sk_mms.append(lambda e, ti=ti, rs=rs, bank=bank, qtr=qtr, cs=cs:
                                          e.matmul(pf[:, 2 + bank, sl(qtr, 128)],
                                                   kc_[rs, ti, cs], qc[rs, ti, cs],
                                                   start=True, stop=True,
                                                   tile_position=(rs.start, 0)))
                            sk_meta.append((par * 64, 64))
                    t_skk = sch.g("tensor", *sk_mms,
                                  deps=tqt + tkt + [pe.get(("pf", 0)), pe.get(("pf", 1)),
                                                    pe.get(("pf", 2)), pe.get(("pf", 3))],
                                  mk=sk_meta)
                    t_skq = t_skk

                    if KN_DN4 < 3:
                        ofm_toks.append(t_skk)
                        pe[("pf", 0)] = t_skk
                        pe[("pf", 2)] = t_skk
                        continue
                    tr_mms = []
                    for pi in range(4):
                        ti = bb * 4 + pi
                        tr_mms.append(lambda e, ti=ti, pi=pi, cs=cs: e.transpose(
                            pb[:, 0, sl(pi, 128)], kc_[:, ti, cs], identb[:]))
                        tr_mms.append(lambda e, ti=ti, pi=pi, cs=cs: e.transpose(
                            pb[:, 0, sl(4 + pi, 128)], vc[:, ti, cs], identb[:]))
                    t_tr = sch.g("tensor", *tr_mms,
                                 deps=tkt + tcv + [pe.get(("pb", 0)),
                                                   pe.get(("pb", 0, "s"))])

                    if KN_DN5 < 2:
                        ofm_toks.append(t_tr)
                        pe[("pb", 0)] = t_tr
                        pe[("pb", 0, "s")] = t_tr
                        continue
                    w0_ops, kcs_ops = [], []
                    for jj, h in enumerate(heads):
                        pi, par = jj // 2, jj % 2
                        koff = pi * 128 + par * 64
                        w0_ops.append(lambda e, jj=jj, koff=koff, ch=ch, h=h:
                                      e.tensor_scalar(
                                          out=wbuf[0][:, jj, 0:64],
                                          in0=pb[:, 0, 512 + koff:512 + koff + 64],
                                          scalar1=ctm[:, ch, 64 + h:65 + h],
                                          scalar2=None, op0=ALU.mult))
                        w0_ops.append(lambda e, jj=jj, koff=koff, ch=ch, h=h:
                                      e.tensor_scalar(
                                          out=wbuf[0][:, jj, 64:128],
                                          in0=pb[:, 0, koff:koff + 64],
                                          scalar1=ctm[:, ch, 96 + h:97 + h],
                                          scalar2=None, op0=ALU.mult))
                        kcs_ops.append(lambda e, jj=jj, koff=koff, ch=ch, h=h:
                                       e.tensor_scalar(
                                           out=kcs[:, jj, :],
                                           in0=pb[:, 0, koff:koff + 64],
                                           scalar1=ctm[:, ch, 112 + h:113 + h],
                                           scalar2=None, op0=ALU.mult))
                    t_w0 = sch.g("vector", *w0_ops, deps=[t_tr, pe.get("wbuf0")])
                    if KN_DN5 < 3:
                        t_kcs = t_w0
                    else:
                        t_kcs = sch.g("vector", *kcs_ops, deps=[t_tr, pe.get("kcs")])
                    pe[("pb", 0)] = t_w0
                    pe[("pb", 0, "s")] = t_kcs

                    if KN_DN4 < 4:
                        ofm_toks.append(t_kcs)
                        pe[("pf", 0)] = t_skk
                        pe[("pf", 2)] = t_skq
                        continue
                    def selq(jj):
                        return pf[:, 4 + jj // 4, sl(jj % 4, 128)]

                    t_selc = sch.g(
                        "tensor",
                        *[(lambda e, jj=jj: e.matmul(selq(jj),
                                                     sel16f[:, sl(2 * jj, 128)],
                                                     crows[:], start=True, stop=True))
                          for jj in range(HB)],
                        deps=[t_crv, pe.get(("pf", 4)), pe.get(("pf", 5)),
                              pe.get(("pf", 4, "s")), pe.get(("pf", 5, "s"))],
                        mk=[(0, 16)] * HB)
                    t_gx = sch.g(
                        "scalar",
                        (lambda e: e.activation(
                            gexp[:].rearrange("p a n -> p (a n)"),
                            pf[:, 4:6, :].rearrange("p a n -> p (a n)"), AF.Exp)),
                        (lambda e: e.activation(
                            gcf[:].rearrange("p a n -> p (a n)"),
                            pf[:, 4:6, :].rearrange("p a n -> p (a n)")
                            .rearrange("p (a n) -> p a n", a=8)[:, :, 127:128]
                            .rearrange("p a n -> p (a n)"), AF.Exp)),
                        deps=[t_selc, pe.get("gexp")])
                    if bb == 0:
                        t_negc = sch.g(
                            "vector",
                            lambda e, ch=ch: e.tensor_scalar(
                                out=argf8[:, 0, 0:16], in0=ctm[:, ch, 0:16],
                                scalar1=-1.0, scalar2=None, op0=ALU.mult),
                            deps=[tctm, pe.get("argf8")])
                        pe["negc"] = t_negc
                    t_am = sch.g(
                        "scalar",
                        *[(lambda e, jj=jj, ch=ch, h=bb * HB + jj: e.activation(
                            awvb[:, jj, :], selq(jj), AF.Exp,
                            bias=ctm[:, ch, 32 + h:33 + h], scale=-1.0))
                          for jj in range(HB)],
                        deps=[t_selc, t_gx, pe.get("awvb")])
                    t_msk1 = sch.g(
                        "gpsimd",
                        *[(lambda e, jj=jj: e.tensor_mul(awvb[:, jj, :],
                                                         awvb[:, jj, :], maskMb[:]))
                          for jj in range(HB)],
                        deps=[t_am, pe.get("mskf")])
                    t_mm1g = sch.g(
                        "vector",
                        *[(lambda e, jj=jj: e.tensor_mul(
                            m_m[:, jj, :], awvb[:, jj, :],
                            pf[:, jj // 4, sl(jj % 4, 128)]))
                          for jj in range(HB)],
                        deps=[t_msk1, t_skk, pe.get("m_m")])
                    t_aa = sch.g(
                        "scalar",
                        *[(lambda e, jj=jj, h=bb * HB + jj: e.activation(
                            gb_[:, jj, :], selq(jj), AF.Exp,
                            bias=argf8[:, 0, h:h + 1], scale=1.0))
                          for jj in range(HB)],
                        deps=[t_am, pe.get("negc"), pe.get("gb_f")])
                    t_msk3 = sch.g(
                        "gpsimd",
                        *[(lambda e, jj=jj: e.tensor_mul(gb_[:, jj, :],
                                                         gb_[:, jj, :], maskATb[:]))
                          for jj in range(HB)],
                        deps=[t_aa])
                    t_mm3g = sch.g(
                        "vector",
                        *[(lambda e, jj=jj: e.tensor_mul(
                            m_at[:, jj, :], gb_[:, jj, :],
                            pf[:, 2 + jj // 4, sl(jj % 4, 128)]))
                          for jj in range(HB)],
                        deps=[t_msk3, t_skk, pe.get("m_at")])
                    t_sell = sch.g(
                        "tensor",
                        *[(lambda e, jj=jj: e.matmul(selq(jj),
                                                     sel16f[:, sl(2 * jj + 1, 128)],
                                                     crows[:], start=True, stop=True))
                          for jj in range(HB)],
                        deps=[t_crv, t_aa, t_gx], mk=[(0, 16)] * HB)
                    t_amt = sch.g(
                        "scalar",
                        *[(lambda e, jj=jj, h=bb * HB + jj: e.activation(
                            gtb_[:, jj, :], selq(jj), AF.Exp,
                            bias=argf8[:, 0, h:h + 1], scale=1.0))
                          for jj in range(HB)],
                        deps=[t_sell, pe.get("negc"), pe.get("gtb_f")])
                    t_msk2 = sch.g(
                        "gpsimd",
                        *[(lambda e, jj=jj: e.tensor_mul(gtb_[:, jj, :],
                                                         gtb_[:, jj, :], maskMTb[:]))
                          for jj in range(HB)],
                        deps=[t_amt])
                    t_mm2g = sch.g(
                        "vector",
                        *[(lambda e, jj=jj: e.tensor_mul(
                            m_mt[:, jj, :], gtb_[:, jj, :],
                            pf[:, jj // 4, sl(jj % 4, 128)]))
                          for jj in range(HB)],
                        deps=[t_msk2, pe.get("m_mt")])
                    arg_last = t_mm2g
                    arg_last_s = t_amt
                    pe[("pf", 4)] = t_amt
                    pe[("pf", 5)] = t_amt
                    pe[("pf", 4, "s")] = t_amt
                    pe[("pf", 5, "s")] = t_amt
                    pe["argf8"] = t_amt
                    pe["awvb"] = t_mm1g
                    pe["mskf"] = t_mm2g
                    pe["gb_f"] = t_mm3g
                    pe["gtb_f"] = t_mm2g
                    pe["m_m"] = None
                    pe["m_mt"] = None
                    pe["m_at"] = None
                    pe["gexp"] = None
                    for b_ in range(4):
                        pe[("pf", b_)] = arg_last
                    pe["gexp"] = arg_last
                    pe["crows"] = arg_last

                    dbg_gate = None
                    if (KN_DBG and int(os.environ.get('KN_DBGSEL', 0)) in (1, 2)
                            and st == NST - 1 and bb == 1
                            and ch == int(os.environ.get('KN_DBGCH', NCH - 1))):
                        _dsel = int(os.environ.get('KN_DBGSEL', 0))
                        if _dsel == 1:
                            _src = [m_m, m_mt]
                        else:
                            _src = [m_at, wbuf[0]]
                        t_dbga = sch.g(
                            "vector",
                            (lambda e, _src=_src: e.tensor_copy(
                                ypre[:, 0:4, :].rearrange("p a n -> p (a n)"),
                                _src[0][:].rearrange("p a n -> p (a n)"))),
                            (lambda e, _src=_src: e.tensor_copy(
                                ypre[:, 4:8, :].rearrange("p a n -> p (a n)"),
                                _src[1][:].rearrange("p a n -> p (a n)"))),
                            deps=[arg_last, arg_last_s, t_w0, pe.get("ypre")])
                        t_dbgad = sch.dma("gpsimd",
                                          lambda e: e.dma_start(
                                              out=dbg_d,
                                              in_=ypre[:, 0:8, :]
                                              .rearrange("p a n -> p (a n)")),
                                          deps=[t_dbga])
                        pe["ypre"] = t_dbgad
                        dbg_gate = t_dbga
                    if KN_DN3 < 2:
                        ofm_toks.append(arg_last)
                        continue
                    cw = 0
                    t_wcur = t_w0
                    g_cur = (m_m, m_mt)
                    t_gdep = [arg_last]
                    t_app_dep = [arg_last, t_kcs, arg_last_s] + ([dbg_gate] if dbg_gate else [])
                    t_gn_dep = [arg_last, arg_last_s]
                    t_gtn_dep = [arg_last, arg_last_s]
                    for ks in range(KN_SOLVE):
                        gsrc, gtsrc = g_cur
                        t_app = sch.g("tensor",
                                      *[(lambda e, jj=jj, gtsrc=gtsrc, cw=cw: e.matmul(
                                          pf[:, jj // 4, sl(jj % 4, 128)],
                                          gtsrc[:, jj, :],
                                          wbuf[cw][:, jj, :], start=True, stop=True))
                                        for jj in range(HB)],
                                      deps=[t_wcur] + t_gdep + t_app_dep)
                        t_app_dep = []
                        opa = ALU.subtract if ks == 0 else ALU.add
                        t_wn = sch.g("vector",
                                     lambda e, cw=cw, opa=opa: e.tensor_tensor(
                                         out=wbuf[1 - cw].rearrange("p a n -> p (a n)"),
                                         in0=wbuf[cw].rearrange("p a n -> p (a n)"),
                                         in1=pf[:, 0:2, :].rearrange("p a n -> p (a n)"),
                                         op=opa),
                                     deps=[t_app, pe.get("wbuf%d" % (1 - cw))])
                        pe["wbuf%d" % cw] = t_wn
                        cw = 1 - cw
                        t_wcur = t_wn
                        t_app_dep = [t_wn]
                        if ks < 6:
                            t_gn = sch.g("tensor",
                                         *[(lambda e, jj=jj, gsrc=gsrc, gtsrc=gtsrc:
                                            e.matmul(pf[:, 2 + jj // 4, sl(jj % 4, 128)],
                                                     gtsrc[:, jj, :],
                                                     gsrc[:, jj, :], start=True,
                                                     stop=True))
                                           for jj in range(HB)],
                                         deps=t_gdep + t_gn_dep)
                            t_gtn = sch.g("tensor",
                                          *[(lambda e, jj=jj, gsrc=gsrc, gtsrc=gtsrc:
                                             e.matmul(pf[:, 4 + jj // 4, sl(jj % 4, 128)],
                                                      gsrc[:, jj, :],
                                                      gtsrc[:, jj, :], start=True,
                                                      stop=True))
                                            for jj in range(HB)],
                                          deps=t_gtn_dep)
                            t_gne = sch.g("vector",
                                          lambda e: e.tensor_copy(
                                              gbuf[0].rearrange("p a n -> p (a n)"),
                                              pf[:, 2:4, :].rearrange("p a n -> p (a n)")),
                                          deps=[t_gn, t_gtn])
                            t_gtne = sch.g("scalar",
                                           lambda e: e.copy(
                                               gtbuf[0].rearrange("p a n -> p (a n)"),
                                               pf[:, 4:6, :].rearrange("p a n -> p (a n)")),
                                           deps=[t_gtn, t_gn])
                            g_cur = (gbuf[0], gtbuf[0])
                            t_gdep = [t_gne, t_gtne]
                            t_gn_dep = [t_gne]
                            t_gtn_dep = [t_gtne]
                    t_wfin = t_wcur
                    if KN_SOLVE == 0:
                        t_gne = t_w0
                        t_gtne = t_w0
                    pe[("pf", 2)] = t_gne
                    pe[("pf", 3)] = t_gne
                    pe[("pf", 4)] = t_gtne
                    pe[("pf", 5)] = t_gtne
                    pe[("pf", 0)] = t_wfin
                    pe[("pf", 1)] = t_wfin

                    if KN_DN3 < 3:
                        ofm_toks.append(t_wfin)
                        continue
                    aw_mms = []
                    for jj, h in enumerate(heads):
                        par = h % 2
                        aw_mms.append(lambda e, jj=jj, par=par, cw=cw: e.matmul(
                            pf[slice(par * 64, par * 64 + 64), 2, sl(jj // 2, 128)],
                            wbuf[cw][:, jj, 0:64], m_at[:, jj, :],
                            start=True, stop=True, tile_position=(0, par * 64)))
                        aw_mms.append(lambda e, jj=jj, par=par, cw=cw: e.matmul(
                            pf[slice(par * 64, par * 64 + 64), 3, sl(jj // 2, 128)],
                            wbuf[cw][:, jj, 64:128], m_at[:, jj, :],
                            start=True, stop=True, tile_position=(0, par * 64)))
                    t_aw = sch.g("tensor", *aw_mms,
                                 deps=[t_wfin, arg_last, pe.get(("pf", 2)),
                                       pe.get(("pf", 3))])
                    awv_ops = []
                    zt_ops = []
                    for jj, h in enumerate(heads):
                        par = h % 2
                        prow = slice(par * 64, par * 64 + 64)
                        ti = h // 2
                        awv_ops.append(lambda e, jj=jj, prow=prow: e.copy(
                            awvb[prow, jj, :], pf[prow, 2, sl(jj // 2, 128)]))
                        zt_ops.append(lambda e, jj=jj, prow=prow, ti=ti, cs=cs:
                                      e.tensor_mul(ztb[prow, jj, :], qc[prow, ti, cs],
                                                   gexp[prow, jj, :]))
                        zt_ops.append(lambda e, jj=jj, prow=prow: e.tensor_sub(
                            ztb[prow, jj, :], ztb[prow, jj, :],
                            pf[prow, 3, sl(jj // 2, 128)]))
                    t_awv = sch.g("scalar", *awv_ops, deps=[t_aw, pe.get("awvb")])
                    t_zt = sch.g("vector", *zt_ops,
                                 deps=[t_aw, pe.get("ztb")] + tqt)
                    pe[("pf", 2)] = t_awv
                    pe[("pf", 3)] = t_zt

                    pv_mms, pkt_mms = [], []
                    for jj, h in enumerate(heads):
                        par = h % 2
                        pv_mms.append(lambda e, jj=jj, par=par, cw=cw: e.matmul(
                            pf[slice(par * 64, par * 64 + 64), 4, sl(jj // 2, 64)],
                            kcs[:, jj, :], wbuf[cw][:, jj, 0:64],
                            start=True, stop=True, tile_position=(0, par * 64)))
                        pkt_mms.append(lambda e, jj=jj, par=par, cw=cw: e.matmul(
                            pf[slice(par * 64, par * 64 + 64), 5, sl(jj // 2, 64)],
                            wbuf[cw][:, jj, 64:128], kcs[:, jj, :],
                            start=True, stop=True, tile_position=(0, par * 64)))
                    t_pv = sch.g("tensor", *pv_mms,
                                 deps=[t_wfin, t_kcs, pe.get(("pf", 4))] + t_gdep)
                    t_pkt = sch.g("tensor", *pkt_mms,
                                  deps=[t_wfin, t_kcs, pe.get(("pf", 5))] + t_gdep)
                    pv_ops, pkt_ops = [], []
                    for jj, h in enumerate(heads):
                        par = h % 2
                        prow = slice(par * 64, par * 64 + 64)
                        pv_ops.append(lambda e, jj=jj, prow=prow: e.tensor_copy(
                            pvf[prow, jj // 2, :], pf[prow, 4, sl(jj // 2, 64)]))
                        pkt_ops.append(lambda e, jj=jj, prow=prow: e.scalar_tensor_tensor(
                            out=pktb[prow, jj // 2, :], in0=id64f[prow, :],
                            scalar=gcf[prow, jj, 0:1], in1=pf[prow, 5, sl(jj // 2, 64)],
                            op0=ALU.mult, op1=ALU.subtract))
                    t_pve = sch.g("vector", *pv_ops, deps=[t_pv, pe.get("pvf")])
                    t_pkte = sch.g("vector", *pkt_ops, deps=[t_pkt, pe.get("pktb")])
                    if (KN_DBG and int(os.environ.get('KN_DBGSEL', 0)) == 0
                            and st == NST - 1 and bb == 1
                            and ch == int(os.environ.get('KN_DBGCH', NCH - 1))):
                        t_dbgc = sch.g(
                            "vector",
                            (lambda e: e.tensor_copy(
                                ypre[:, 0:2, :].rearrange("p a n -> p (a n)")[:, 0:256],
                                pvf[:].rearrange("p a n -> p (a n)"))),
                            (lambda e: e.tensor_copy(
                                ypre[:, 0:2, :].rearrange("p a n -> p (a n)")[:, 256:512],
                                pktb[:].rearrange("p a n -> p (a n)"))),
                            (lambda e: e.tensor_copy(
                                ypre[:, 2:6, :].rearrange("p a n -> p (a n)")[:, 0:1024],
                                wbuf[0][:].rearrange("p a n -> p (a n)"))),
                            (lambda e: e.tensor_copy(
                                ypre[:, 6:8, :].rearrange("p a n -> p (a n)")[:, 0:512],
                                kcs[:].rearrange("p a n -> p (a n)"))),
                            deps=[t_pve, t_pkte, t_wfin, t_kcs, pe.get("ypre")])
                        t_dbgcd = sch.dma("gpsimd",
                                          lambda e: e.dma_start(
                                              out=dbg_d,
                                              in_=ypre[:, 0:8, :]
                                              .rearrange("p a n -> p (a n)")),
                                          deps=[t_dbgc])
                        pe["ypre"] = t_dbgcd
                        pe["pvf"] = t_dbgc
                        pe["pktb"] = t_dbgc
                        pe["wbuf0"] = t_dbgc
                        pe["wbuf1"] = t_dbgc
                        pe["kcs"] = t_dbgc
                    pe[("pf", 4)] = t_pve
                    pe[("pf", 4, "cr")] = t_pve
                    pe[("pf", 4, "b")] = t_pve
                    pe[("pf", 5)] = t_pkte
                    pe[("pf", 4, "s")] = None
                    pe[("pf", 5, "s")] = None
                    pe["wbuf0"] = t_pkt
                    pe["wbuf1"] = t_pkt
                    pe["kcs"] = t_pkt
                    pe["m_m"] = t_aw
                    pe["m_mt"] = t_aw
                    pe["m_at"] = t_aw

                    if KN_DN3 < 4:
                        ofm_toks.append(t_pkte)
                        continue
                    y_mms, s_mms, ys_meta = [], [], []
                    for par in range(2):
                        for jj, h in enumerate(heads):
                            if h % 2 != par:
                                continue
                            prow = slice(par * 64, par * 64 + 64)
                            pi = h // 2
                            pj = jj // 2
                            y_mms.append(lambda e, prow=prow, pi=pi, jj=jj, pj=pj, par=par:
                                         e.matmul(pf[prow, 0, sl(pj, 128)],
                                                  stb[prow, pi, :], ztb[prow, jj, :],
                                                  start=True, stop=True,
                                                  tile_position=(par * 64, par * 64)))
                            s_mms.append(lambda e, prow=prow, pi=pi, jj=jj, pj=pj, par=par:
                                         e.matmul(pf[prow, 1, sl(pj, 64)],
                                                  pktb[prow, pj, :], stb[prow, pi, :],
                                                  start=True, stop=True,
                                                  tile_position=(par * 64, par * 64)))
                            ys_meta.append((par * 64, 64))
                    t_ymm = sch.g("tensor", *y_mms,
                                  deps=[t_zt, pe.get(("pf", 0)), pe.get("stb_r")],
                                  mk=ys_meta)
                    t_smm = sch.g("tensor", *s_mms,
                                  deps=[t_pkte, pe.get(("pf", 1)), t_ymm], mk=ys_meta)
                    o_ops, st_ops, stb_ops = [], [], []
                    for jj, h in enumerate(heads):
                        par = h % 2
                        prow = slice(par * 64, par * 64 + 64)
                        pi = h // 2
                        pj = jj // 2
                        o_ops.append(lambda e, prow=prow, pi=pi, jj=jj, pj=pj, cs=cs:
                                     e.tensor_add(ofm[prow, pi, cs],
                                                  pf[prow, 0, sl(pj, 128)],
                                                  awvb[prow, jj, :]))
                        st_ops.append(lambda e, prow=prow, pi=pi, pj=pj:
                                      e.tensor_add(stf[prow, pi, :],
                                                   pf[prow, 1, sl(pj, 64)],
                                                   pvf[prow, pj, :]))
                        stb_ops.append(lambda e, prow=prow, pi=pi:
                                       e.tensor_copy(stb[prow, pi, :], stf[prow, pi, :]))
                    t_oe = sch.g("vector", *o_ops, deps=[t_ymm, t_awv, pe.get("ofm")])
                    t_ste = sch.g("vector", *st_ops, deps=[t_smm, t_pve])
                    t_stb = sch.g("vector", *stb_ops, deps=[t_ste, pe.get("stb_r2")])
                    pe[("pf", 0)] = t_oe
                    pe[("pf", 1)] = t_ste
                    pe["stb_r"] = t_stb
                    pe["ztb"] = t_ymm
                    pe["pvf"] = t_ste
                    pe["pktb"] = t_smm
                    pe["awvb"] = t_oe
                    ofm_toks.append(t_oe)
                pe["ctm%d" % ch] = ofm_toks[-1]

            if KN_STAGE < 4:
                continue
            t_osq = sch.g("scalar",
                          *[(lambda e, i=i: e.activation(sq1[:, i, :], ofm[:, i, :],
                                                         AF.Square))
                            for i in range(8)],
                          deps=ofm_toks + [pe.get("sq1")])
            t_omm = sch.g("tensor",
                          *[(lambda e, i=i: e.matmul(pf[0:1, 4, 0:T1], onescb[:],
                                                     sq1[:, i, :],
                                                     start=(i == 0), stop=(i == 7)))
                            for i in range(8)],
                          deps=[t_osq, pe.get(("pf", 4))])
            pe["sq1"] = t_omm
            t_os = sch.g("vector",
                         lambda e: e.tensor_scalar(out=r1a[:], in0=pf[0:1, 4, 0:T1],
                                                   scalar1=1.0 / D, scalar2=1e-6,
                                                   op0=ALU.mult, op1=ALU.add),
                         deps=[t_omm, pe.get("r1a")])
            pe[("pf", 4)] = t_os
            t_ol = sch.g("scalar",
                         lambda e: e.activation(r1b[:], r1a[:], AF.Ln),
                         lambda e: e.activation(r1a[:], r1b[:], AF.Exp, scale=-0.5),
                         deps=[t_os, pe.get("r1b")])
            t_oexp = sch.g("tensor",
                           lambda e: e.matmul(pf[:, 5, 0:T1], onesrf[:], r1a[:],
                                              start=True, stop=True),
                           deps=[t_ol, pe.get(("pf", 5))], mk=[(0, 1)])
            t_on = sch.g("vector",
                         *[(lambda e, i=i: e.tensor_mul(ofm[:, i, :], ofm[:, i, :],
                                                        pf[:, 5, 0:T1]))
                           for i in range(8)],
                         deps=[t_oexp] + ofm_toks)
            pe[("pf", 5)] = t_on
            pe["r1a"] = t_oexp

            yp_toks = []
            for pr in range(4):
                t_mm = None
                for s_ in range(2):
                    oc = 2 * pr + s_
                    mms = [(lambda e, oc=oc, kk=kk, pr=pr, s_=s_, st=st:
                            e.matmul(pf[:, pr, s_ * T1:s_ * T1 + T1],
                                     wsb[wsbuf_of(st, 2)][:, kk, sl(oc, 128)],
                                     ofm[:, kk, :], start=(kk == 0), stop=(kk == 7)))
                           for kk in range(8)]
                    t_mm = sch.g("tensor", *mms,
                                 deps=[t_on, ws_tok[(st, "wo")]]
                                 + ([pe.get(("pf", pr))] if s_ == 0 else []))
                t_ev = sch.g("vector",
                             lambda e, pr=pr, cur=cur:
                             e.tensor_add(ypre[:, 2 * pr:2 * pr + 2, :],
                                          pf[:, pr, 0:2 * T1]
                                          .rearrange("p (a n) -> p a n", a=2),
                                          xf[cur][:, 2 * pr:2 * pr + 2, :]),
                             deps=[t_mm, tx, pe.get("ypre")])
                pe[("pf", pr)] = t_ev
                yp_toks.append(t_ev)
            pe[("xf", cur)] = yp_toks[-1]
            if st + 1 < NST:
                ws_tok[(st + 1, "rk")] = load_ws("rk", wsbuf_of(st + 1, 1), yp_toks)

            t_ypb = sch.g("scalar",
                          *[(lambda e, i=i: e.copy(ypb[:, i, :], ypre[:, i, :]))
                            for i in range(8)],
                          deps=yp_toks + [pe.get("ypb")])
            t_s1 = sch.g("tensor",
                         *[(lambda e, i=i: e.matmul(pf[0:1, 4, 0:T1], onescb[:],
                                                    ypb[:, i, :],
                                                    start=(i == 0), stop=(i == 7)))
                           for i in range(8)],
                         deps=[t_ypb, pe.get(("pf", 4))])
            t_mu = sch.g("vector",
                         lambda e: e.tensor_scalar(out=r1a[:], in0=pf[0:1, 4, 0:T1],
                                                   scalar1=1.0 / D, scalar2=None,
                                                   op0=ALU.mult),
                         deps=[t_s1, pe.get("r1a")])
            pe[("pf", 4)] = t_mu
            t_sq2 = sch.g("scalar",
                          *[(lambda e, i=i: e.activation(sq1[:, i, :], ypb[:, i, :],
                                                         AF.Square))
                            for i in range(8)],
                          deps=[t_ypb, pe.get("sq1")])
            t_s2 = sch.g("tensor",
                         *[(lambda e, i=i: e.matmul(pf[0:1, 4, 256:256 + T1], onescb[:],
                                                    sq1[:, i, :],
                                                    start=(i == 0), stop=(i == 7)))
                           for i in range(8)],
                         deps=[t_sq2, t_mu])
            pe["sq1"] = t_s2
            t_var = sch.g("vector",
                          (lambda e: e.tensor_scalar(out=r1b[:], in0=pf[0:1, 4, 256:256 + T1],
                                                     scalar1=1.0 / D, scalar2=None,
                                                     op0=ALU.mult)),
                          (lambda e: e.tensor_mul(pf[0:1, 4, 256:256 + T1], r1a[:], r1a[:])),
                          (lambda e: e.tensor_sub(r1b[:], r1b[:], pf[0:1, 4, 256:256 + T1])),
                          (lambda e: e.tensor_scalar(out=r1b[:], in0=r1b[:], scalar1=1e-5,
                                                     scalar2=None, op0=ALU.add)),
                          deps=[t_s2, pe.get("r1b")])
            pe[("pf", 4)] = t_var
            t_inv = sch.g("scalar",
                          lambda e: e.activation(r1b[:], r1b[:], AF.Ln),
                          lambda e: e.activation(r1b[:], r1b[:], AF.Exp, scale=-0.5),
                          deps=[t_var])
            t_mexp = sch.g("tensor",
                           lambda e: e.matmul(pf[:, 4, 0:T1], onesrf[:], r1a[:],
                                              start=True, stop=True),
                           deps=[t_inv, t_var], mk=[(0, 1)])
            t_iexp = sch.g("tensor",
                           lambda e: e.matmul(pf[:, 5, 0:T1], onesrf[:], r1b[:],
                                              start=True, stop=True),
                           deps=[t_inv, pe.get(("pf", 5))], mk=[(0, 1)])
            pe["r1a"] = t_mexp
            pe["r1b"] = t_iexp
            y1_ops = []
            for i in range(8):
                y1_ops.append(lambda e, i=i: e.tensor_sub(y1f[:, i, :], ypre[:, i, :],
                                                          pf[:, 4, 0:T1]))
                y1_ops.append(lambda e, i=i: e.tensor_mul(y1f[:, i, :], y1f[:, i, :],
                                                          pf[:, 5, 0:T1]))
            t_y1 = sch.g("vector", *y1_ops,
                         deps=[t_mexp, t_iexp] + yp_toks + [pe.get("y1f")])
            pe[("pf", 4)] = t_y1
            pe[("pf", 5)] = t_y1
            pe["ypre"] = t_y1
            pe["ypb"] = t_s2
            pe["ofm"] = t_mu
            pe["ofm"] = yp_toks[-1]
            t_y1d = sch.dma("gpsimd",
                            lambda e, st=st: e.dma_start(
                                out=y1s.rearrange("(a p) s -> p a s", p=128)[:, :, sl(st, T1)],
                                in_=y1f),
                            deps=[t_y1])
            pe["y1f"] = t_y1d
            pe["ypre"] = t_y1d
            y1_toks.append(t_y1d)
            pe["qcF"] = t_skq if False else None
            pe["qcF"] = t_y1
            pe["kcF"] = t_y1
            pe["vcF"] = t_y1
            pe["rqF"] = tqt[-1]
            pe["rkF"] = tks_[-1]
            if DEBUG and st == 0:
                sch.dma("gpsimd",
                        lambda e: e.dma_start(out=dbg["d_qt"].rearrange(
                            "(a p) n -> p (a n)", p=128),
                            in_=qc[:].rearrange("p a n -> p (a n)")),
                        lambda e: e.dma_start(out=dbg["d_kt"].rearrange(
                            "(a p) n -> p (a n)", p=128),
                            in_=kc_[:].rearrange("p a n -> p (a n)")),
                        lambda e: e.dma_start(out=dbg["d_vc"].rearrange(
                            "(a p) n -> p (a n)", p=128),
                            in_=vc[:].rearrange("p a n -> p (a n)")),
                        lambda e: e.dma_start(out=dbg["d_ctm"], in_=ctm[:, 0, :]),
                        lambda e: e.dma_start(out=dbg["d_O"].rearrange(
                            "(a p) n -> p (a n)", p=128),
                            in_=ofm[:].rearrange("p a n -> p (a n)")),
                        lambda e: e.dma_start(out=dbg["d_y1"].rearrange(
                            "(a p) n -> p (a n)", p=128),
                            in_=y1f[:].rearrange("p a n -> p (a n)")),
                        deps=[t_y1])

        t_q_last = None
        if not KN_P15:
            sch.dma("gpsimd",
                    lambda e: e.dma_start(
                        out=outT.rearrange("(a p) s -> p a s", p=128)[:, :, 0:T1],
                        in_=y1f),
                    deps=y1_toks[-1:] if y1_toks else [])
        t_wz = sch.g("vector", lambda e: e.memset(wsum[:], 0.0), deps=[])
        t_sum_last = t_wz
        red_tok = {}
        s_units = []
        for i in range(8):
            for hlf in range(4):
                s_units.append(("w1", i, hlf))
        for j in range(32):
            s_units.append(("w2", j, 0))
        for u, (kind, i, hlf) in enumerate(s_units if KN_P15 else []):
            buf = stAB[u % 2]
            if kind == "w1":
                src = w1T_d[sl(i, 128), sl(hlf, 1024)]
                tcol, acol = 10, i
            else:
                src = w2T_d[sl(i, 128), :]
                tcol, acol = 9, 8
            t_d = sch.dma(
                "sync",
                lambda e, buf=buf, src=src: e.dma_start(out=buf[:], in_=src),
                deps=[red_tok[u - 2]] if u >= 2 else
                ([t_swo] if u == 0 else [yp_toks[-1]]))
            t_r = sch.g("vector",
                        lambda e, buf=buf, tcol=tcol: e.tensor_reduce(
                            out=wsum[:, tcol:tcol + 1], in_=buf[:],
                            axis=AXX, op=ALU.add, apply_absolute_value=True),
                        deps=[t_d, t_wz])
            red_tok[u] = sch.g("vector",
                               lambda e, tcol=tcol, acol=acol: e.tensor_add(
                                   wsum[:, acol:acol + 1], wsum[:, acol:acol + 1],
                                   wsum[:, tcol:tcol + 1]),
                               deps=[t_r])
            t_sum_last = red_tok[u]
        t_tot = sch.g("tensor",
                      (lambda e: e.matmul(pf[0:1, 4, 0:12], onescf[:], wsum[:],
                                          start=True, stop=True)),
                      deps=[t_sum_last, pe.get(("pf", 4))])
        t_sc = sch.g("vector",
                     (lambda e: e.tensor_reduce(out=pf[0:1, 4, 16:17], in_=pf[0:1, 4, 0:8],
                                                axis=AXX, op=ALU.add)),
                     (lambda e: e.tensor_scalar(out=pf[0:1, 4, 17:18], in0=pf[0:1, 4, 16:17],
                                                scalar1=1.0 / (D * DFF), scalar2=1e-5,
                                                op0=ALU.mult, op1=ALU.max)),
                     (lambda e: e.tensor_scalar(out=pf[0:1, 4, 18:19], in0=pf[0:1, 4, 8:9],
                                                scalar1=1.0 / (D * DFF), scalar2=1e-5,
                                                op0=ALU.mult, op1=ALU.max)),
                     deps=[t_tot])
        t_scv = sch.g("vector",
                      lambda e: e.tensor_copy(sval[0:1, 6:8], pf[0:1, 4, 17:19]),
                      deps=[t_sc])
        t_bc = sch.g("tensor",
                     lambda e: e.matmul(pf[:, 5, 0:2], onesrf[:], sval[0:1, 6:8],
                                        start=True, stop=True),
                     deps=[t_scv, pe.get(("pf", 5))], mk=[(0, 1)])
        t_sv = sch.g("vector",
                     (lambda e: e.tensor_copy(sval[:, 4:6], pf[:, 5, 0:2])),
                     (lambda e: e.reciprocal(sval[:, 0:1], sval[:, 4:5])),
                     (lambda e: e.reciprocal(sval[:, 1:2], sval[:, 5:6])),
                     (lambda e: e.tensor_scalar(out=sval[:, 2:3], in0=sval[:, 4:5],
                                                scalar1=1.0 / 127.0, scalar2=None,
                                                op0=ALU.mult)),
                     (lambda e: e.tensor_scalar(out=sval[:, 3:4], in0=sval[:, 5:6],
                                                scalar1=1.0 / 127.0, scalar2=None,
                                                op0=ALU.mult)),
                     deps=[t_bc])
        pe[("pf", 4)] = t_sv
        pe[("pf", 5)] = t_sv

        def quant_chunk(dst_ap, src_ap, invcol, clip_lo, clip_hi, deps):
            return sch.g(
                "vector",
                (lambda e: e.tensor_scalar(out=qs_g[:, 0:src_ap.shape[-1]], in0=src_ap,
                                           scalar1=invcol, scalar2=0.5,
                                           op0=ALU.mult, op1=ALU.add)),
                (lambda e: e.tensor_copy(qs_i[:, 0:src_ap.shape[-1]],
                                         qs_g[:, 0:src_ap.shape[-1]])),
                (lambda e: e.tensor_copy(qs_f[:, 0:src_ap.shape[-1]],
                                         qs_i[:, 0:src_ap.shape[-1]])),
                (lambda e: e.tensor_tensor(out=qs_g[:, 0:src_ap.shape[-1]],
                                           in0=qs_f[:, 0:src_ap.shape[-1]],
                                           in1=qs_g[:, 0:src_ap.shape[-1]],
                                           op=ALU.is_gt)),
                (lambda e: e.tensor_sub(qs_f[:, 0:src_ap.shape[-1]],
                                        qs_f[:, 0:src_ap.shape[-1]],
                                        qs_g[:, 0:src_ap.shape[-1]])),
                (lambda e: e.tensor_scalar(out=dst_ap, in0=qs_f[:, 0:src_ap.shape[-1]],
                                           scalar1=float(clip_lo), scalar2=float(clip_hi),
                                           op0=ALU.max, op1=ALU.min)),
                deps=deps)

        t_q_last = t_sv
        qdone = {}
        NU1 = 32 if KN_P15 else 0
        for u in range(NU1):
            i, hlf = u // 4, u % 4
            dep0 = [qdone[u - 2]] if u >= 2 else \
                ([red_tok[62 + u]] if red_tok else [])
            t_d = sch.dma("sync",
                          lambda e, i=i, hlf=hlf, u=u: e.dma_start(
                              out=stAB[u % 2][:],
                              in_=w1T_d[sl(i, 128), sl(hlf, 1024)]),
                          deps=dep0)
            for s2_ in range(2):
                t_q_last = quant_chunk(
                    w1qb[:, i, hlf * 1024 + s2_ * 512:hlf * 1024 + s2_ * 512 + 512],
                    stAB[u % 2][:, sl(s2_, 512)],
                    sval[:, 0:1], -1, 1, [t_d, t_q_last])
            qdone[u] = t_q_last
        for j in range(32 if KN_P15 else 0):
            u = NU1 + j
            t_d = sch.dma("sync",
                          lambda e, j=j, u=u: e.dma_start(out=stAB[u % 2][:],
                                                          in_=w2T_d[sl(j, 128), :]),
                          deps=[qdone[u - 2]] if u >= 2 else [])
            for s2_ in range(2):
                t_q_last = quant_chunk(w2qc[j % 2][:, sl(s2_, 512)],
                                       stAB[u % 2][:, sl(s2_, 512)], sval[:, 1:2],
                                       -1, 1,
                                       [t_d, t_q_last]
                                       + ([pe.get(("w2st", j % 2))]
                                          if s2_ == 0 else []))
            qdone[u] = t_q_last
            t_wout = sch.dma("gpsimd",
                             lambda e, j=j: e.dma_start(out=w2qs[sl(j, 128), :],
                                                        in_=w2qc[j % 2][:, :]),
                             deps=[t_q_last])
            pe[("w2st", j % 2)] = t_wout

        y1r = y1s.rearrange("(a p) s -> p a s", p=128)

        def load_y(tt, deps):
            return sch.dma(
                "sync",
                (lambda e, tt=tt: e.dma_start(out=hy_pv,
                                              in_=y1r[:, 0, sl(tt, T2)])),
                (lambda e, tt=tt: e.dma_start(out=hy_st,
                                              in_=y1r[:, 1:3, sl(tt, T2)])),
                (lambda e, tt=tt: e.dma_start(out=hy_qg,
                                              in_=y1r[:, 3, sl(tt, T2)])),
                (lambda e, tt=tt: e.dma_start(out=hy_qf,
                                              in_=y1r[:, 4, sl(tt, T2)])),
                (lambda e, tt=tt: e.dma_start(out=hy_tl,
                                              in_=y1r[:, 5:8, sl(tt, T2)])),
                deps=deps)

        y_ld = None
        if NT2 > 0:
            y_ld = load_y(0, [t_q_last] + y1_toks[0:1]
                          + (tcv[5:] if NST > 0 else []))
        w2_tok = {}

        def w2l(i):
            return w2qs4[i // 2][:, (i % 2) * 512:(i % 2) * 512 + 512]

        if NT2 > 0:
            w2_tok[0] = sch.dma("sync",
                                lambda e: e.dma_start(out=w2l(0),
                                                      in_=w2qs[0:128, 0:512]),
                                deps=[pe.get(("w2st", 0)), t_q_last])
            pe[("w2u", 1)] = pe.get(("w2st", 0))
            pe[("w2u", 2)] = pe.get(("w2st", 1))
            pe[("w2u", 3)] = pe.get(("w2st", 1))
            for k_ in range(4, 8):
                pe[("w2u", k_)] = ofm_toks[-1] if NST > 0 else None
        out_toks = []

        for tt in range(NT2):
            ty = y_ld

            t_ab = sch.g("scalar",
                         *[(lambda e, i=i: e.activation(
                             sq1[:, i, 0:T2], hyfc[i], AF.Abs))
                           for i in range(8)],
                         deps=[ty, pe.get("sq1")])
            mx_ops = [lambda e: e.tensor_max(tmax[:], sq1[:, 0, 0:T2], sq1[:, 1, 0:T2])]
            for i in range(2, 8):
                mx_ops.append(lambda e, i=i: e.tensor_max(tmax[:], tmax[:],
                                                          sq1[:, i, 0:T2]))
            t_mx = sch.g("vector", *mx_ops, deps=[t_ab, pe.get("tmax")])
            pe["sq1"] = t_mx
            t_mt = sch.g("tensor",
                         (lambda e: e.transpose(pf[:, 4, 0:128], tmax[:, 0:128],
                                                identf[:])),
                         (lambda e: e.transpose(pf[:, 4, 128:256], tmax[:, 128:256],
                                                identf[:])),
                         deps=[t_mx, pe.get(("pf", 4))])
            t_pm = sch.g("vector",
                         (lambda e: e.tensor_reduce(out=p2sc[:, 0:1],
                                                    in_=pf[:, 4, 0:128],
                                                    axis=AXX, op=ALU.max)),
                         (lambda e: e.tensor_reduce(out=p2sc[:, 1:2],
                                                    in_=pf[:, 4, 128:256],
                                                    axis=AXX, op=ALU.max)),
                         (lambda e: e.tensor_scalar(out=p2sc[:, 0:2], in0=p2sc[:, 0:2],
                                                    scalar1=1e-5, scalar2=None,
                                                    op0=ALU.max)),
                         (lambda e: e.reciprocal(p2sc[:, 2:4], p2sc[:, 0:2])),
                         (lambda e: e.tensor_scalar(out=p2sc[:, 2:4], in0=p2sc[:, 2:4],
                                                    scalar1=127.0, scalar2=None,
                                                    op0=ALU.mult)),
                         (lambda e: e.tensor_scalar(out=p2sc[:, 0:2], in0=p2sc[:, 0:2],
                                                    scalar1=sval[:, 2:3], scalar2=None,
                                                    op0=ALU.mult)),
                         deps=[t_mt, pe.get("p2sc")])
            pe[("pf", 4)] = t_pm
            t_rt = sch.g("tensor",
                         lambda e: e.transpose(pf[0:4, 4, 256:384], p2sc[:, 0:4],
                                               identf[:]),
                         deps=[t_pm, pe.get(("pf", 4, "b"))])
            t_rv = sch.g("vector",
                         lambda e: e.tensor_copy(scrows[0:4, :], pf[0:4, 4, 256:384]),
                         deps=[t_rt, pe.get("scrows")])
            pe[("pf", 4, "b")] = t_rv
            t_ex = sch.g("tensor",
                         (lambda e: e.matmul(pf[:, 5, 0:128], sel16f[:, sl(4, 128)],
                                             scrows[:], start=True, stop=True)),
                         (lambda e: e.matmul(pf[:, 5, 128:256], sel16f[:, sl(6, 128)],
                                             scrows[:], start=True, stop=True)),
                         (lambda e: e.matmul(pf[:, 5, 256:384], sel16f[:, sl(0, 128)],
                                             scrows[:], start=True, stop=True)),
                         (lambda e: e.matmul(pf[:, 5, 384:512], sel16f[:, sl(2, 128)],
                                             scrows[:], start=True, stop=True)),
                         deps=[t_rv, pe.get(("pf", 5))], mk=[(0, 16)] * 4)
            t_exe = sch.g("vector",
                          (lambda e: e.tensor_copy(scexp[:, 0, :], pf[:, 5, 0:256])),
                          (lambda e: e.tensor_copy(scexp[:, 1, :],
                                                   pf[:, 5, 256:512])),
                          deps=[t_ex, pe.get("scexp")])
            pe[("pf", 5)] = t_exe

            n1_toks = []
            for i in range(8):
                t_n = sch.g(
                    "vector",
                    (lambda e, i=i: e.scalar_tensor_tensor(
                        out=qs_g[:, 0:T2], in0=hyfc[i], scalar=0.5,
                        in1=scexp[:, 0, :], op0=ALU.bypass, op1=ALU.mult)),
                    (lambda e: e.tensor_scalar(out=qs_g[:, 0:T2], in0=qs_g[:, 0:T2],
                                               scalar1=0.5, scalar2=None, op0=ALU.add)),
                    (lambda e: e.tensor_copy(qs_i[:, 0:T2], qs_g[:, 0:T2])),
                    (lambda e: e.tensor_copy(qs_f[:, 0:T2], qs_i[:, 0:T2])),
                    (lambda e: e.tensor_tensor(out=qs_g[:, 0:T2], in0=qs_f[:, 0:T2],
                                               in1=qs_g[:, 0:T2], op=ALU.is_gt)),
                    (lambda e: e.tensor_sub(qs_f[:, 0:T2], qs_f[:, 0:T2],
                                            qs_g[:, 0:T2])),
                    (lambda e, i=i: e.tensor_scalar(out=n1b[:, i, :], in0=qs_f[:, 0:T2],
                                                    scalar1=-128.0, scalar2=127.0,
                                                    op0=ALU.max, op1=ALU.min)),
                    deps=[t_exe, ty, pe.get("n1b")])
                n1_toks.append(t_n)

            h_toks = []
            for oc in range(32):
                bank = oc % 4
                po = ((oc // 4) % 2) * 256
                mms = [(lambda e, oc=oc, kk=kk, bank=bank, po=po:
                        e.matmul(pf[:, bank, po:po + T2], w1qb[:, kk, sl(oc, 128)],
                                 n1b[:, kk, :], start=(kk == 0), stop=(kk == 7)))
                       for kk in range(8)]
                t_mm = sch.g("tensor", *mms,
                             deps=n1_toks + [t_q_last, pe.get(("pf", bank, po))])
                pe["n1b"] = t_mm
                t_dq = sch.g("vector",
                             lambda e, oc=oc, bank=bank, po=po: e.tensor_mul(
                                 hf[:, oc, :], pf[:, bank, po:po + T2],
                                 scexp[:, 1, :]),
                             deps=[t_mm, pe.get("hf")]
                             + ([pe.get("y2f")] if oc == 0 else [])
                             + ([y1_toks[-1]] if oc == 20 and y1_toks else []))
                t_g = sch.g("scalar",
                            lambda e, oc=oc: e.activation(hf[:, oc, :], hf[:, oc, :],
                                                          AF.Gelu),
                            deps=[t_dq])
                pe[("pf", bank, po)] = t_g
                h_toks.append(t_g)

            t_mx2 = None
            for grp in range(4):
                t_ab2 = sch.g("scalar",
                              *[(lambda e, i=i, grp=grp: e.activation(
                                  sq1[:, i, 0:T2], hf[:, grp * 8 + i, :], AF.Abs))
                                for i in range(8)],
                              deps=h_toks[grp * 8:grp * 8 + 8]
                              + ([t_mx2] if t_mx2 else [pe.get("sq1")]))
                ops = []
                if grp == 0:
                    ops.append(lambda e: e.tensor_max(tmax[:], sq1[:, 0, 0:T2],
                                                      sq1[:, 1, 0:T2]))
                    rng = range(2, 8)
                else:
                    rng = range(0, 8)
                for i in rng:
                    ops.append(lambda e, i=i: e.tensor_max(tmax[:], tmax[:],
                                                           sq1[:, i, 0:T2]))
                t_mx2 = sch.g("vector", *ops,
                              deps=[t_ab2] + ([t_exe] if grp == 0 else []))
            pe["sq1"] = t_mx2
            pe["tmax"] = t_mx2
            t_mt2 = sch.g("tensor",
                          (lambda e: e.transpose(pf[:, 4, 0:128], tmax[:, 0:128],
                                                 identf[:])),
                          (lambda e: e.transpose(pf[:, 4, 128:256], tmax[:, 128:256],
                                                 identf[:])),
                          deps=[t_mx2, pe.get(("pf", 4))])
            t_pm2 = sch.g("vector",
                          (lambda e: e.tensor_reduce(out=p2sc[:, 4:5],
                                                     in_=pf[:, 4, 0:128],
                                                     axis=AXX, op=ALU.max)),
                          (lambda e: e.tensor_reduce(out=p2sc[:, 5:6],
                                                     in_=pf[:, 4, 128:256],
                                                     axis=AXX, op=ALU.max)),
                          (lambda e: e.tensor_scalar(out=p2sc[:, 4:6], in0=p2sc[:, 4:6],
                                                     scalar1=1e-5, scalar2=None,
                                                     op0=ALU.max)),
                          (lambda e: e.reciprocal(p2sc[:, 6:8], p2sc[:, 4:6])),
                          (lambda e: e.tensor_scalar(out=p2sc[:, 6:8], in0=p2sc[:, 6:8],
                                                     scalar1=127.0, scalar2=None,
                                                     op0=ALU.mult)),
                          (lambda e: e.tensor_scalar(out=p2sc[:, 4:6], in0=p2sc[:, 4:6],
                                                     scalar1=sval[:, 3:4], scalar2=None,
                                                     op0=ALU.mult)),
                          deps=[t_mt2, t_rv])
            pe[("pf", 4)] = t_pm2
            t_rt2 = sch.g("tensor",
                          lambda e: e.transpose(pf[0:4, 4, 256:384], p2sc[:, 4:8],
                                                identf[:]),
                          deps=[t_pm2, pe.get(("pf", 4, "b"))])
            t_rv2 = sch.g("vector",
                          lambda e: e.tensor_copy(scrows[0:4, :], pf[0:4, 4, 256:384]),
                          deps=[t_rt2, t_exe, t_ex])
            pe[("pf", 4, "b")] = t_rv2
            t_ex2 = sch.g("tensor",
                          (lambda e: e.matmul(pf[:, 5, 0:128], sel16f[:, sl(4, 128)],
                                              scrows[:], start=True, stop=True)),
                          (lambda e: e.matmul(pf[:, 5, 128:256], sel16f[:, sl(6, 128)],
                                              scrows[:], start=True, stop=True)),
                          (lambda e: e.matmul(pf[:, 5, 256:384], sel16f[:, sl(0, 128)],
                                              scrows[:], start=True, stop=True)),
                          (lambda e: e.matmul(pf[:, 5, 384:512], sel16f[:, sl(2, 128)],
                                              scrows[:], start=True, stop=True)),
                          deps=[t_rv2, pe.get(("pf", 5))], mk=[(0, 16)] * 4)
            t_exe2 = sch.g("vector",
                           (lambda e: e.tensor_copy(scexp[:, 2, :], pf[:, 5, 0:256])),
                           (lambda e: e.tensor_copy(scexp[:, 3, :],
                                                    pf[:, 5, 256:512])),
                           deps=[t_ex2, pe.get("scexp2")])
            pe[("pf", 5)] = t_exe2

            n2_toks = []
            for oc in range(32):
                t_n = sch.g(
                    "vector",
                    (lambda e, oc=oc: e.scalar_tensor_tensor(
                        out=qs_g[:, 0:T2], in0=hf[:, oc, :], scalar=0.5,
                        in1=scexp[:, 2, :], op0=ALU.bypass, op1=ALU.mult)),
                    (lambda e: e.tensor_scalar(out=qs_g[:, 0:T2], in0=qs_g[:, 0:T2],
                                               scalar1=0.5, scalar2=None, op0=ALU.add)),
                    (lambda e: e.tensor_copy(qs_i[:, 0:T2], qs_g[:, 0:T2])),
                    (lambda e: e.tensor_copy(qs_f[:, 0:T2], qs_i[:, 0:T2])),
                    (lambda e: e.tensor_tensor(out=qs_g[:, 0:T2], in0=qs_f[:, 0:T2],
                                               in1=qs_g[:, 0:T2], op=ALU.is_gt)),
                    (lambda e: e.tensor_sub(qs_f[:, 0:T2], qs_f[:, 0:T2],
                                            qs_g[:, 0:T2])),
                    (lambda e, oc=oc: e.tensor_scalar(out=n2b[:, oc, :],
                                                      in0=qs_f[:, 0:T2],
                                                      scalar1=-128.0, scalar2=127.0,
                                                      op0=ALU.max, op1=ALU.min)),
                    deps=[t_exe2, h_toks[oc], pe.get("n2b")])
                n2_toks.append(t_n)
            pe["hf"] = n2_toks[-1]

            y2_toks = []
            for half in range(2):
                for kk in range(32):
                    gidx = tt * 64 + half * 32 + kk
                    wtok = w2_tok[gidx]
                    wb = w2l(gidx % 8)
                    nxt = gidx + 1
                    if nxt < NT2 * 64:
                        nh = (nxt % 64) // 32
                        nk = nxt % 32
                        w2_tok[nxt] = sch.dma(
                            "sync" if nxt % 2 == 0 else "gpsimd",
                            lambda e, nxt=nxt, nh=nh, nk=nk: e.dma_start(
                                out=w2l(nxt % 8),
                                in_=w2qs[sl(nk, 128), nh * 512:nh * 512 + 512]),
                            deps=[pe.get(("w2u", nxt % 8))])
                    mms = [(lambda e, kk=kk, oc=oc, wb=wb: e.matmul(
                        pf[:, oc, 0:T2], wb[:, sl(oc, 128)],
                        n2b[:, kk, :], start=(kk == 0), stop=(kk == 31)))
                        for oc in range(4)]
                    t_mm2 = sch.g("tensor", *mms,
                                  deps=[wtok, n2_toks[kk]]
                                  + ([pe.get(("pf", b_, 0)) for b_ in range(4)]
                                     if kk == 0 else []))
                    pe[("w2u", gidx % 8)] = t_mm2
                for oc in range(4):
                    t_e2v = sch.g("vector",
                                  (lambda e, oc=oc, half=half: e.tensor_mul(
                                      y2f[:, half * 4 + oc, :], pf[:, oc, 0:T2],
                                      scexp[:, 3, :])),
                                  (lambda e, oc=oc, half=half: e.tensor_add(
                                      y2f[:, half * 4 + oc, :],
                                      y2f[:, half * 4 + oc, :],
                                      hyfc[half * 4 + oc])),
                                  deps=[t_mm2, t_exe2, pe.get("y2f")])
                    pe[("pf", oc, 0)] = t_e2v
                    y2_toks.append(t_e2v)
            if tt + 1 < NT2:
                y_ld = load_y(tt + 1,
                              [y2_toks[-1],
                               y1_toks[min(tt + 1, len(y1_toks) - 1)]])

            t_y2b = sch.g("scalar",
                          *[(lambda e, i=i: e.copy(ypb[:, i, 0:T2], y2f[:, i, :]))
                            for i in range(8)],
                          deps=y2_toks + [pe.get("ypb2")])
            t_s1b = sch.g("tensor",
                          *[(lambda e, i=i: e.matmul(pf[0:1, 4, 0:T2], onescb[:],
                                                     ypb[:, i, 0:T2],
                                                     start=(i == 0), stop=(i == 7)))
                            for i in range(8)],
                          deps=[t_y2b, pe.get(("pf", 4))])
            t_mu2 = sch.g("vector",
                          lambda e: e.tensor_scalar(out=r1a[0:1, 0:T2],
                                                    in0=pf[0:1, 4, 0:T2],
                                                    scalar1=1.0 / D, scalar2=None,
                                                    op0=ALU.mult),
                          deps=[t_s1b, pe.get("r1a2")])
            pe[("pf", 4)] = t_mu2
            t_sq2b = sch.g("scalar",
                           *[(lambda e, i=i: e.activation(sq1[:, i, 0:T2],
                                                          ypb[:, i, 0:T2], AF.Square))
                             for i in range(8)],
                           deps=[t_y2b, pe.get("sq1")])
            t_s2b = sch.g("tensor",
                          *[(lambda e, i=i: e.matmul(pf[0:1, 4, 256:256 + T2], onescb[:],
                                                     sq1[:, i, 0:T2],
                                                     start=(i == 0), stop=(i == 7)))
                            for i in range(8)],
                          deps=[t_sq2b, t_mu2])
            pe["sq1"] = t_s2b
            t_var2 = sch.g("vector",
                           (lambda e: e.tensor_scalar(out=r1b[0:1, 0:T2],
                                                      in0=pf[0:1, 4, 256:256 + T2],
                                                      scalar1=1.0 / D, scalar2=None,
                                                      op0=ALU.mult)),
                           (lambda e: e.tensor_mul(pf[0:1, 4, 256:256 + T2],
                                                   r1a[0:1, 0:T2], r1a[0:1, 0:T2])),
                           (lambda e: e.tensor_sub(r1b[0:1, 0:T2], r1b[0:1, 0:T2],
                                                   pf[0:1, 4, 256:256 + T2])),
                           (lambda e: e.tensor_scalar(out=r1b[0:1, 0:T2],
                                                      in0=r1b[0:1, 0:T2],
                                                      scalar1=1e-5, scalar2=None,
                                                      op0=ALU.add)),
                           deps=[t_s2b, pe.get("r1b2")])
            pe[("pf", 4)] = t_var2
            t_inv2 = sch.g("scalar",
                           lambda e: e.activation(r1b[0:1, 0:T2], r1b[0:1, 0:T2], AF.Ln),
                           lambda e: e.activation(r1b[0:1, 0:T2], r1b[0:1, 0:T2],
                                                  AF.Exp, scale=-0.5),
                           deps=[t_var2])
            t_mexp2 = sch.g("tensor",
                            lambda e: e.matmul(pf[:, 4, 0:T2], onesrf[:],
                                               r1a[0:1, 0:T2], start=True, stop=True),
                            deps=[t_inv2, t_var2], mk=[(0, 1)])
            t_iexp2 = sch.g("tensor",
                            lambda e: e.matmul(pf[:, 5, 0:T2], onesrf[:],
                                               r1b[0:1, 0:T2], start=True, stop=True),
                            deps=[t_inv2, pe.get(("pf", 5))], mk=[(0, 1)])
            pe["r1a2"] = t_mexp2
            pe["r1b2"] = t_iexp2
            out_ops = []
            for i in range(8):
                out_ops.append(lambda e, i=i: e.tensor_sub(y2f[:, i, :], y2f[:, i, :],
                                                           pf[:, 4, 0:T2]))
                out_ops.append(lambda e, i=i: e.tensor_mul(y2f[:, i, :], y2f[:, i, :],
                                                           pf[:, 5, 0:T2]))
            t_out = sch.g("vector", *out_ops, deps=[t_mexp2, t_iexp2] + y2_toks)
            pe[("pf", 4)] = t_out
            pe[("pf", 5)] = t_out
            pe["ypb2"] = t_s2b
            t_od = sch.dma("gpsimd",
                           lambda e, tt=tt: e.dma_start(
                               out=outT.rearrange("(a p) s -> p a s", p=128)
                               [:, :, sl(tt, T2)],
                               in_=y2f),
                           deps=[t_out])
            pe["y2f"] = t_od
            out_toks.append(t_od)
            pe["n2b"] = t_mm2
            pe["scexp"] = t_exe2
            pe["scexp2"] = t_out
            pe["scrows"] = t_ex2
            pe["p2sc"] = t_rt2

        final_deps = (out_toks[-1:] if out_toks else []) + \
            [(ln, sch.dval[ln]) for ln in DSEM["gpsimd"] if ln in sch.dval]
        t_fin = sch.g("gpsimd", lambda e: e.nop(), deps=final_deps)

        with nc.Block() as block:
            @block.sync
            def _(eng):
                sch.replay("sync", eng, sems)

            @block.tensor
            def _(eng):
                sch.replay("tensor", eng, sems)

            @block.vector
            def _(eng):
                sch.replay("vector", eng, sems)

            @block.scalar
            def _(eng):
                sch.replay("scalar", eng, sems)

            @block.gpsimd
            def _(eng):
                sch.replay("gpsimd", eng, sems)

    _CACHE["nc"] = nc
    return nc


def _consts():
    ii = np.arange(128)
    c = {}
    c["ident"] = np.eye(128, dtype=np.float32)
    c["id64"] = np.vstack([np.eye(64, dtype=np.float32)] * 2)
    c["maskM"] = (ii[None, :] < ii[:, None]).astype(np.float32)
    c["maskMT"] = (ii[:, None] < ii[None, :]).astype(np.float32)
    c["maskAT"] = (ii[:, None] <= ii[None, :]).astype(np.float32)
    bexp = np.zeros((16, 8 * 128), np.float32)
    for i in range(8):
        for p in range(128):
            bexp[(i * 128 + p) // 64, i * 128 + p] = 1.0
    c["bexp"] = bexp
    c["bexpT"] = bexp.T.copy()
    c["onesc"] = np.ones((128, 1), np.float32)
    c["onesr"] = np.ones((1, 128), np.float32)
    sel16 = np.zeros((16, 16 * 128), np.float32)
    for j in range(16):
        r = 8 * (j % 2) + j // 2
        sel16[r, j * 128:(j + 1) * 128] = 1.0
    c["sel16"] = sel16
    return c


def kernel(**inputs):
    nc = build_nc()
    c = _consts()
    f32 = np.float32

    def A(v):
        return np.ascontiguousarray(np.asarray(v), dtype=f32)

    x = A(inputs["L0_attn_x"])
    wq = A(inputs["L0_attn_Wq"]); wk = A(inputs["L0_attn_Wk"])
    wv = A(inputs["L0_attn_Wv"]); wo = A(inputs["L0_attn_Wo"])
    cq = A(inputs["L0_attn_conv_q"]); ck = A(inputs["L0_attn_conv_k"])
    cv = A(inputs["L0_attn_conv_v"])
    wf = A(inputs["L0_attn_Wf"]); bf = A(inputs["L0_attn_bf"])
    wl = A(inputs["L0_attn_Wl"]); bl = A(inputs["L0_attn_bl"])
    wrq = A(inputs["L0_attn_Wrq"]); wrk = A(inputs["L0_attn_Wrk"])
    gs = A(inputs["L0_attn_graph_scale"])
    qn = A(inputs["L0_attn_qn_w"]); kn = A(inputs["L0_attn_kn_w"])
    on = A(inputs["L0_attn_on_w"])
    w1 = A(inputs["L0_ffn_w1"]); w2 = A(inputs["L0_ffn_w2"])

    wfl = np.zeros((D, 64), f32)
    wfl[:, 0:16] = wf
    wfl[:, 32:48] = wl
    bfl = np.zeros((64, 1), f32)
    bfl[0:16, 0] = bf
    bfl[32:48, 0] = bl
    wrq_p = np.zeros((D, D), f32)
    wrk_p = np.zeros((D, D), f32)
    RR_ = 48
    wrq3 = wrq.reshape(D, 16, RR_)
    wrk3 = wrk.reshape(D, 16, RR_)
    wrq_p.reshape(D, 16, 64)[:, :, 0:RR_] = wrq3
    wrk_p.reshape(D, 16, 64)[:, :, 0:RR_] = wrk3
    qnw = np.tile(qn, 2).reshape(128, 1)
    knw = np.tile(kn, 2).reshape(128, 1)
    onw = np.ascontiguousarray(on.reshape(8, 128).T)
    w1T = np.ascontiguousarray(w1.T)
    w2T = np.ascontiguousarray(w2.T)

    def cvl(cw_):
        return np.ascontiguousarray(cw_.reshape(8, 128, KC).transpose(1, 0, 2)
                                    .reshape(128, 8 * KC))
    shared = dict(wq=wq, wk=wk, wv=wv, wo=wo, wrq=wrq_p, wrk=wrk_p, wfl=wfl,
                  bfl=bfl, convq=cvl(cq), convk=cvl(ck), convv=cvl(cv),
                  gsc=gs.reshape(16, 1), qnw=qnw, knw=knw, onw=onw,
                  w1T=w1T, w2T=w2T, **c)
    in_maps = []
    for b in range(8):
        m = dict(shared)
        m["xT"] = np.ascontiguousarray(x[b].T)
        in_maps.append(m)

    res = run_bass_kernel_spmd(nc, in_maps, core_ids=list(range(8)))
    out = np.empty((8, S, D), np.float32)
    for b in range(8):
        out[b] = res.results[b]["outT"].T
    return out
